# version 02
# baseline (speedup 1.0000x reference)
import numpy as np
import ml_dtypes
from contextlib import ExitStack, contextmanager

import concourse.bass as bass
import concourse.mybir as mybir
from concourse.bass_utils import run_bass_kernel_spmd

F32 = mybir.dt.float32
BF16 = mybir.dt.bfloat16
AF = mybir.ActivationFunctionType
ALU = mybir.AluOpType
AX = mybir.AxisListType

NCORES = 8
D = 1024
NCH = 8
T = 2048
NT = T // 128
S = 8192
EPS_LN = 1e-5

BF = ml_dtypes.bfloat16


class Ctx:
    def __init__(self, nc, dram):
        self.nc = nc
        self.dram = dram
        self.root = ExitStack()
        self.stacks = [self.root]
        self.sems = {}
        self.sem_count = {}
        self._uid = 0
        self.const = {}

    def uid(self):
        self._uid += 1
        return self._uid

    def sb(self, name, shape, dt, side=None):
        return self.stacks[-1].enter_context(
            self.nc.sbuf_tensor(f"{name}_{self.uid()}", shape, dt, side=side))

    def ps(self, name, shape, dt=F32):
        return self.stacks[-1].enter_context(
            self.nc.psum_tensor(f"{name}_{self.uid()}", shape, dt))

    SEM_ALIAS = {
        "b1_wp2": "a0b_wp", "b1_xl": "a0b_xl", "b1_om": "a0b_om",
        "b1_oe": "a0b_oe", "b1_os": "a0b_ost",
        "a1_wl": "a0_wl", "a1_mm": "a0_mm", "a1_ev": "a0_ev",
        "f1_lq": "ln_q", "f1_lmt": "ln_mm", "f1_lse": "ln_ev",
        "f1_ls1": "ln_s1", "f1_lsv": "ln_sv",
        "b1_nd": "a0b_kv", "b1_mm": "a0_mm2", "b1_bb": "a0b_S",
        "b1_rw": "a0b_E", "b1_ev": "a0b_P",
    }

    def sem(self, name):
        name = self.SEM_ALIAS.get(name, name)
        if name not in self.sems:
            s = self.root.enter_context(self.nc.semaphore(name))
            self.sems[name] = s
            self.sem_count[s] = 0
        return self.sems[name]

    def inc(self, sem, n=1):
        self.sem_count[sem] += n
        return self.sem_count[sem]

    def cnt(self, sem):
        return self.sem_count[sem]

    @contextmanager
    def scoped(self):
        st = ExitStack()
        self.stacks.append(st)
        try:
            with st:
                yield
        finally:
            self.stacks.pop()


class SerialEng:

    def __init__(self, ctx, eng, sem):
        self._ctx = ctx
        self._eng = eng
        self._sem = sem

    def __getattr__(self, name):
        fn = getattr(self._eng, name)

        def wrap(*a, **kw):
            self._eng.wait_ge(self._sem, self._ctx.cnt(self._sem))
            inst = fn(*a, **kw)
            inst.then_inc(self._sem, 1)
            self._ctx.inc(self._sem, 1)
            return inst
        return wrap

    def wait_ge(self, sem, val):
        self._eng.wait_ge(sem, val)

    def milestone(self, sem, n=1):
        self._eng.wait_ge(self._sem, self._ctx.cnt(self._sem))
        self._eng.nop().then_inc(sem, n)


def emit_const_setup(ctx):
    nc = ctx.nc
    ctx.const["ones_bf"] = ctx.sb("ones_bf", [128, 1], BF16)
    ctx.const["ones_row"] = ctx.sb("ones_row", [1, 128], F32)
    ctx.const["ones_row_bf"] = ctx.sb("ones_row_bf", [1, 128], BF16)
    ctx.const["ones_col"] = ctx.sb("ones_col", [128, 1], F32)
    ctx.const["ident"] = ctx.sb("ident", [128, 128], BF16)
    ctx.const["ident2"] = ctx.sb("ident2", [128, 64], BF16)
    sem = ctx.sem("cst")
    with nc.Block() as blk:
        @blk.vector
        def _(v):
            v.memset(ctx.const["ones_bf"][:, :], 1.0)
            v.memset(ctx.const["ones_row"][:, :], 1.0)
            v.memset(ctx.const["ones_row_bf"][:, :], 1.0)
            v.memset(ctx.const["ones_col"][:, :], 1.0)

        @blk.gpsimd
        def _(g):
            g.dma_start(out=ctx.const["ident"][:, :],
                        in_=ctx.dram["c_ident"][:, :]).then_inc(sem, 16)
            g.dma_start(out=ctx.const["ident2"][:, :],
                        in_=ctx.dram["c_ident2"][:, :]).then_inc(sem, 16)
            g.wait_ge(sem, ctx.inc(sem, 32))


def emit_ln(ctx, xt, xn_bf, ntok, gamma_col=None, beta_col=None, var_row=None):
    nc = ctx.nc
    assert ntok % 512 == 0
    nblk = ntok // 512
    onesbf = ctx.const["ones_bf"]
    onesrow = ctx.const["ones_row"]
    with ctx.scoped():
        xcast = [ctx.sb("ln_xc", [128, NCH, 512], BF16) for _ in range(2)]
        xsq = [ctx.sb("ln_xsq", [128, NCH, 512], BF16) for _ in range(2)]
        stat_s = [ctx.ps("ln_ss", [1, 512], F32) for _ in range(2)]
        stat_q = [ctx.ps("ln_sq", [1, 512], F32) for _ in range(2)]
        rsum = ctx.sb("ln_rsum", [1, ntok], F32)
        rsq = ctx.sb("ln_rsq", [1, ntok], F32)
        rrow = ctx.sb("ln_r", [1, ntok], F32)
        prow = ctx.sb("ln_p", [1, ntok], F32)
        bmat = [ctx.ps("ln_b", [128, 512], F32) for _ in range(2)]
        rB = ctx.sb("ln_rB", [128, ntok], F32)
        pB = ctx.sb("ln_pB", [128, ntok], F32)
        tmp = ctx.sb("ln_tmp", [128, 512], F32)
        sv = ctx.sem("ln_v")
        ssq = ctx.sem("ln_q")
        sm = ctx.sem("ln_mm")
        se = ctx.sem("ln_ev")
        s1 = ctx.sem("ln_s1")
        sb2 = ctx.sem("ln_b2")

        b_sv, b_ssq, b_sm, b_se = (ctx.cnt(sv), ctx.cnt(ssq), ctx.cnt(sm), ctx.cnt(se))
        with nc.Block() as blk:
            @blk.vector
            def _(v):
                for ib in range(nblk):
                    if ib >= 2:
                        v.wait_ge(sm, b_sm + ib - 1)
                    for ch in range(NCH):
                        last = v.tensor_copy(xcast[ib % 2][:, ch, :],
                                             xt[:, ch, ib * 512:(ib + 1) * 512])
                    last.then_inc(sv, 1)
            ctx.inc(sv, nblk)

            @blk.scalar
            def _(s):
                for ib in range(nblk):
                    if ib >= 2:
                        s.wait_ge(sm, b_sm + ib - 1)
                    for ch in range(NCH):
                        last = s.activation(xsq[ib % 2][:, ch, :],
                                            xt[:, ch, ib * 512:(ib + 1) * 512], AF.Square)
                    last.then_inc(ssq, 1)
            ctx.inc(ssq, nblk)

            @blk.tensor
            def _(t):
                for ib in range(nblk):
                    t.wait_ge(sv, b_sv + ib + 1)
                    t.wait_ge(ssq, b_ssq + ib + 1)
                    if ib >= 2:
                        t.wait_ge(se, b_se + ib - 1)
                    for ch in range(NCH):
                        t.matmul(stat_s[ib % 2][:, :], onesbf[:, :],
                                 xcast[ib % 2][:, ch, :],
                                 start=(ch == 0), stop=(ch == NCH - 1))
                    for ch in range(NCH):
                        last = t.matmul(stat_q[ib % 2][:, :], onesbf[:, :],
                                        xsq[ib % 2][:, ch, :],
                                        start=(ch == 0), stop=(ch == NCH - 1))
                    last.then_inc(sm, 1)
            ctx.inc(sm, nblk)

            @blk.vector
            def _(v):
                for ib in range(nblk):
                    v.wait_ge(sm, b_sm + ib + 1)
                    v.tensor_copy(rsum[:, ib * 512:(ib + 1) * 512], stat_s[ib % 2][:, :])
                    v.tensor_copy(rsq[:, ib * 512:(ib + 1) * 512],
                                  stat_q[ib % 2][:, :]).then_inc(se, 1)
            ctx.inc(se, nblk)

        with nc.Block() as blk:
            @blk.scalar
            def _(s):
                s.activation(prow[:, :], rsum[:, :], AF.Square, bias=np.float32(0.0),
                             scale=np.float32(1.0 / D)).then_inc(s1, 1)

            @blk.vector
            def _(v):
                sv = SerialEng(ctx, v, ctx.sem("ln_sv"))
                v.wait_ge(s1, ctx.inc(s1))
                sv.scalar_tensor_tensor(rrow[:, :], rsq[:, :], 1.0 / D, prow[:, :],
                                        ALU.mult, ALU.subtract)
                if var_row is not None:
                    sv.tensor_copy(var_row[:, :], rrow[:, :])
                sv.milestone(s1, 1)

            @blk.scalar
            def _(s):
                s.wait_ge(s1, ctx.inc(s1))
                s.activation(rrow[:, :], rrow[:, :], AF.Sqrt, bias=np.float32(EPS_LN),
                             scale=np.float32(1.0)).then_inc(s1, 1)

            @blk.vector
            def _(v):
                sv = SerialEng(ctx, v, ctx.sem("ln_sv"))
                v.wait_ge(s1, ctx.inc(s1))
                sv.reciprocal(rrow[:, :], rrow[:, :])
                sv.tensor_scalar_mul(prow[:, :], rsum[:, :], 1.0 / D)
                sv.tensor_mul(prow[:, :], prow[:, :], rrow[:, :])
                sv.tensor_scalar_mul(prow[:, :], prow[:, :], -1.0)

        b_b2 = ctx.cnt(sb2)
        with nc.Block() as blk:
            @blk.tensor
            def _(t):
                for j, row in enumerate((rrow, prow)):
                    for ib in range(nblk):
                        k = j * nblk + ib
                        if k >= 2:
                            t.wait_ge(sb2, b_b2 + k - 1)
                        t.matmul(bmat[k % 2][:, :], onesrow[:, :],
                                 row[:, ib * 512:(ib + 1) * 512],
                                 start=True, stop=True).then_inc(sm, 1)
            ctx.inc(sm, 2 * nblk)

            @blk.vector
            def _(v):
                b = ctx.cnt(sm) - 2 * nblk
                for j, dst in enumerate((rB, pB)):
                    for ib in range(nblk):
                        k = j * nblk + ib
                        v.wait_ge(sm, b + k + 1)
                        v.tensor_copy(dst[:, ib * 512:(ib + 1) * 512],
                                      bmat[k % 2][:, :]).then_inc(sb2, 1)
            ctx.inc(sb2, 2 * nblk)

        with nc.Block() as blk:
            @blk.vector
            def _(v):
                for ch in range(NCH):
                    for ib in range(nblk):
                        sl = slice(ib * 512, (ib + 1) * 512)
                        v.tensor_mul(tmp[:, :], xt[:, ch, sl], rB[:, sl])
                        v.tensor_add(xn_bf[:, ch, sl], tmp[:, :], pB[:, sl])
                        if gamma_col is not None:
                            v.tensor_scalar(xn_bf[:, ch, sl], xn_bf[:, ch, sl],
                                            gamma_col[:, ch:ch + 1],
                                            beta_col[:, ch:ch + 1],
                                            ALU.mult, ALU.add)


def emit_ffn0(ctx, xin_d, xout_d, park_d, int_d):
    nc = ctx.nc
    dram = ctx.dram
    GB = 16
    NGB = (T // 32) // GB
    WAVE = 16
    NWV = 128 // WAVE
    ident = ctx.const["ident"]

    with ctx.scoped():
        xn = ctx.sb("f0_xn", [128, NCH, T], BF16)
        permT = ctx.sb("f0_permT", [128, T], BF16)
        permN = ctx.sb("f0_permN", [128, NT, 128], BF16)
        sx = ctx.sem("f0_sx")

        with ctx.scoped():
            xT = ctx.sb("f0_xT", [128, NCH, T], F32)
            with nc.Block() as blk:
                @blk.sync
                def _(sy):
                    sy.dma_start(out=xT[:, :, :],
                                 in_=xin_d.rearrange("(c p) t -> p c t", p=128)) \
                        .then_inc(sx, 16)
                    sy.wait_ge(sx, ctx.inc(sx, 16))
            emit_ln(ctx, xT, xn, T)
            with nc.Block() as blk:
                @blk.sync
                def _(sy):
                    sy.dma_start(out=park_d.rearrange("(c p) t -> p c t", p=128),
                                 in_=xT[:, :, :]).then_inc(sx, 16)
                    sy.wait_ge(sx, ctx.inc(sx, 16))

        sl = ctx.sem("f0_l")
        sm = ctx.sem("f0_m")
        sv = ctx.sem("f0_v")
        with ctx.scoped():
            ctrl = ctx.sb("f0_ctrl", [128, NCH, 128], BF16)
            tieB = ctx.sb("f0_tie", [128, 512], F32)
            lgT = ctx.sb("f0_lgT", [128, T], F32)
            gmax = ctx.sb("f0_gmax", [128, T // 32], F32)
            plg = [ctx.ps("f0_plg", [128, 512], F32) for _ in range(2)]
            ptr = [ctx.ps("f0_ptr", [128, 128], BF16) for _ in range(2)]
            b_l, b_m, b_v = ctx.cnt(sl), ctx.cnt(sm), ctx.cnt(sv)
            with nc.Block() as blk:
                @blk.gpsimd
                def _(g):
                    g.dma_start(out=ctrl[:, :, :],
                                in_=dram["w_ctrl"].transpose([1, 0, 2])) \
                        .then_inc(sl, 16)
                    g.dma_start(out=tieB[:, :], in_=dram["c_tie"][:, :]).then_inc(sl, 16)
                ctx.inc(sl, 32)

                @blk.tensor
                def _(t):
                    t.wait_ge(sl, b_l + 32)
                    for ib in range(4):
                        if ib >= 2:
                            t.wait_ge(sv, b_v + ib - 1)
                        for ch in range(NCH):
                            last = t.matmul(plg[ib % 2][:, :], ctrl[:, ch, :],
                                            xn[:, ch, ib * 512:(ib + 1) * 512],
                                            start=(ch == 0), stop=(ch == NCH - 1))
                        last.then_inc(sm, 1)
                ctx.inc(sm, 4)

                @blk.vector
                def _(v):
                    for ib in range(4):
                        v.wait_ge(sm, b_m + ib + 1)
                        v.tensor_add(lgT[:, ib * 512:(ib + 1) * 512],
                                     plg[ib % 2][:, :], tieB[:, :]).then_inc(sv, 1)
                ctx.inc(sv, 4)

            with nc.Block() as blk:
                @blk.vector
                def _(v):
                    v.tensor_reduce(gmax[:, :].rearrange("p (g o) -> p g o", o=1),
                                    lgT[:, :].rearrange("p (g t) -> p g t", t=32),
                                    AX.X, ALU.max)
                    v.tensor_tensor(permT[:, :].rearrange("p (g t) -> p g t", t=32),
                                    lgT[:, :].rearrange("p (g t) -> p g t", t=32),
                                    gmax[:, :].rearrange("p (g o) -> p g o", o=1)
                                    .broadcast_to([128, T // 32, 32]),
                                    ALU.is_ge)

            b_m, b_v = ctx.cnt(sm), ctx.cnt(sv)
            with nc.Block() as blk:
                @blk.tensor
                def _(t):
                    for it in range(NT):
                        if it >= 2:
                            t.wait_ge(sv, b_v + it - 1)
                        t.transpose(ptr[it % 2][:, :], permT[:, it * 128:(it + 1) * 128],
                                    ident[:, :]).then_inc(sm, 1)
                ctx.inc(sm, NT)

                @blk.vector
                def _(v):
                    for it in range(NT):
                        v.wait_ge(sm, b_m + it + 1)
                        v.tensor_copy(permN[:, it, :], ptr[it % 2][:, :]).then_inc(sv, 1)
                ctx.inc(sv, NT)

        dT = ctx.sb("f0_dT", [128, NCH, GB, 128], BF16)
        xnat = [ctx.sb("f0_xnat", [128, NCH, 128], BF16) for _ in range(4)]
        f1w = [ctx.sb("f0_f1w", [128, WAVE, NCH, 32], BF16) for _ in range(2)]
        f2w = [ctx.sb("f0_f2w", [128, 4, 1024], BF16) for _ in range(2)]
        f1bias = ctx.sb("f0_f1bias", [32, 128], F32)
        inner = ctx.sb("f0_inner", [128, 4, GB], BF16)
        stage = [ctx.sb("f0_stage", [GB, 4, 1024], BF16) for _ in range(2)]
        intG = ctx.sb("f0_intG", [128, GB, 1024], BF16)
        xpc = [ctx.sb("f0_xpc", [128, NCH, 512], F32) for _ in range(2)]

        swl = ctx.sem("f0_wl")
        sxm = ctx.sem("f0_xm")
        sxe = ctx.sem("f0_xe")
        sdm = ctx.sem("f0_dm")
        sde = ctx.sem("f0_de")
        sf1 = ctx.sem("f0_f1")
        sfe = ctx.sem("f0_fe")
        sf2 = ctx.sem("f0_f2")
        sev = ctx.sem("f0_ev")
        srg = ctx.sem("f0_rg")
        sld = ctx.sem("f0_ld")
        ssc = ctx.sem("f0_sc")
        sre = ctx.sem("f0_re")
        sxp = ctx.sem("f0_xp")
        sst = ctx.sem("f0_st")

        with nc.Block() as blk:
            @blk.gpsimd
            def _(g):
                g.dma_start(out=f1bias[:, :], in_=dram["w_f1b"][:, :]).then_inc(swl, 16)
                g.wait_ge(swl, ctx.inc(swl, 16))

        for gb in range(NGB):
            g0 = gb * GB
            tile0 = g0 // 4

            with ctx.scoped():
                pxt = [ctx.ps("f0_pxt", [128, 128], BF16) for _ in range(2)]
                pd = [ctx.ps("f0_pd", [128, NCH, 128], F32) for _ in range(2)]
                b_xm, b_xe = ctx.cnt(sxm), ctx.cnt(sxe)
                b_dm, b_de = ctx.cnt(sdm), ctx.cnt(sde)
                with nc.Block() as blk:
                    @blk.tensor
                    def _(t):
                        for lt in range(4):
                            it = tile0 + lt
                            for ch in range(NCH):
                                k = lt * NCH + ch
                                if k >= 2:
                                    t.wait_ge(sxe, b_xe + k - 1)
                                t.transpose(pxt[k % 2][:, :],
                                            xn[:, ch, it * 128:(it + 1) * 128],
                                            ident[:, :]).then_inc(sxm, 1)
                    ctx.inc(sxm, 4 * NCH)

                    @blk.vector
                    def _(v):
                        for lt in range(4):
                            for ch in range(NCH):
                                k = lt * NCH + ch
                                v.wait_ge(sxm, b_xm + k + 1)
                                v.tensor_copy(xnat[lt][:, ch, :],
                                              pxt[k % 2][:, :]).then_inc(sxe, 1)
                    ctx.inc(sxe, 4 * NCH)

                with nc.Block() as blk:
                    @blk.tensor
                    def _(t):
                        for lg in range(GB):
                            lt = lg // 4
                            rowb = (lg % 4) * 32
                            it = tile0 + lt
                            if lg >= 2:
                                t.wait_ge(sde, b_de + lg - 1)
                            for ch in range(NCH):
                                last = t.matmul(
                                    pd[lg % 2][:, ch, :],
                                    xnat[lt][rowb:rowb + 32, ch, :],
                                    permN[rowb:rowb + 32, it, :],
                                    start=(ch % 4 == 0), stop=(ch % 4 == 3),
                                    tile_position=(rowb, 0))
                            last.then_inc(sdm, 1)
                    ctx.inc(sdm, GB)

                    @blk.vector
                    def _(v):
                        for lg in range(GB):
                            v.wait_ge(sdm, b_dm + lg + 1)
                            v.tensor_copy(dT[:, :, lg, :],
                                          pd[lg % 2][:, :, :]).then_inc(sde, 1)
                    ctx.inc(sde, GB)

            with ctx.scoped():
                pf = [ctx.ps("f0_pf", [32, GB], F32) for _ in range(4)]
                pi = [ctx.ps("f0_pi", [GB, 1024], F32) for _ in range(2)]
                b_wl, b_f1 = ctx.cnt(swl), ctx.cnt(sf1)
                b_fe, b_f2 = ctx.cnt(sfe), ctx.cnt(sf2)
                b_ev, b_rg = ctx.cnt(sev), ctx.cnt(srg)
                b_ld = ctx.cnt(sld)

                def strip(es):
                    return (es % WAVE) // 4

                with nc.Block() as blk:
                    @blk.gpsimd
                    def _(g):
                        for w in range(NWV):
                            if w >= 2:
                                g.wait_ge(sf2, b_f2 + WAVE * (w - 1))
                            e0 = w * WAVE
                            g.dma_start(out=f1w[w % 2][:, :, :, :],
                                        in_=dram["w_f1"][:, e0:e0 + WAVE, :, :]) \
                                .then_inc(swl, 16)
                            for j in range(4):
                                g.dma_start(
                                    out=f2w[w % 2][32 * j:32 * j + 32, :, :],
                                    in_=dram["w_f2"][:, e0 + 4 * j:e0 + 4 * j + 4, :]) \
                                    .then_inc(swl, 16)
                    ctx.inc(swl, 5 * 16 * NWV)

                    @blk.tensor
                    def _(t):
                        def f2_mms(es):
                            w = es // WAVE
                            st = strip(es)
                            el = es % 4
                            t.wait_ge(sfe, b_fe + es + 1)
                            if es >= 2:
                                t.wait_ge(sev, b_ev + es - 1)
                            for nb in range(2):
                                last = t.matmul(
                                    pi[es % 2][:, nb * 512:(nb + 1) * 512],
                                    inner[32 * st:32 * st + 32, es % 4, :],
                                    f2w[w % 2][32 * st:32 * st + 32, el,
                                               nb * 512:(nb + 1) * 512],
                                    start=True, stop=True,
                                    tile_position=(32 * st, 0))
                            last.then_inc(sf2, 1)

                        for es in range(128):
                            w = es // WAVE
                            t.wait_ge(swl, b_wl + 5 * 16 * (w + 1))
                            if es >= 4:
                                t.wait_ge(sfe, b_fe + es - 3)
                            for ch in range(NCH):
                                last = t.matmul(pf[es % 4][:, :],
                                                f1w[w % 2][:, es % WAVE, ch, :],
                                                dT[:, ch, :, es],
                                                start=(ch == 0), stop=(ch == NCH - 1))
                            last.then_inc(sf1, 1)
                            if es >= 2:
                                f2_mms(es - 2)
                        f2_mms(126)
                        f2_mms(127)
                    ctx.inc(sf1, 128)
                    ctx.inc(sf2, 128)

                    @blk.vector
                    def _(v):
                        def evac(es):
                            q = es // 4
                            if q >= 2:
                                v.wait_ge(srg, b_rg + 16 * (q - 1))
                            v.wait_ge(sf2, b_f2 + es + 1)
                            v.tensor_copy(stage[q % 2][:, es % 4, :],
                                          pi[es % 2][:, :]).then_inc(sev, 1)

                        for es in range(128):
                            st = strip(es)
                            v.wait_ge(sf1, b_f1 + es + 1)
                            if es >= 16:
                                v.wait_ge(sf2, b_f2 + es - 15)
                            v.tensor_scalar(inner[32 * st:32 * st + 32, es % 4, :],
                                            pf[es % 4][:, :],
                                            f1bias[:, es:es + 1], 0.0,
                                            ALU.add, ALU.max).then_inc(sfe, 1)
                            if es >= 2:
                                evac(es - 2)
                        evac(126)
                        evac(127)
                    ctx.inc(sfe, 128)
                    ctx.inc(sev, 128)

                    @blk.sync
                    def _(sy):
                        for q in range(32):
                            sy.wait_ge(sev, b_ev + 4 * (q + 1))
                            sy.dma_start(
                                out=int_d[4 * q:4 * q + 4, :, :].transpose([1, 0, 2]),
                                in_=stage[q % 2][:, :, :],
                            ).then_inc(srg, 16)
                        sy.wait_ge(srg, b_rg + 32 * 16)
                        sy.dma_start(out=intG[:, :, :], in_=int_d[:, :, :]) \
                            .then_inc(sld, 16)
                    ctx.inc(srg, 32 * 16)
                    ctx.inc(sld, 16)

            with ctx.scoped():
                psc = [ctx.ps("f0_psc", [128, 128], F32) for _ in range(2)]
                b_sc, b_re = ctx.cnt(ssc), ctx.cnt(sre)
                b_ld2, b_xp, b_st = ctx.cnt(sld), ctx.cnt(sxp), ctx.cnt(sst)
                t0 = g0 * 32
                with nc.Block() as blk:
                    @blk.gpsimd
                    def _(g):
                        if gb >= 2:
                            g.wait_ge(sst, b_st - 16)
                        g.dma_start(out=xpc[gb % 2][:, :, :],
                                    in_=park_d.rearrange("(c p) t -> p c t", p=128)
                                    [:, :, t0:t0 + 512]).then_inc(sxp, 16)
                    ctx.inc(sxp, 16)

                    @blk.tensor
                    def _(t):
                        t.wait_ge(sld, b_ld2)
                        for lt in range(4):
                            for ch in range(NCH):
                                idx = lt * NCH + ch
                                if idx >= 2:
                                    t.wait_ge(sre, b_re + idx - 1)
                                for lg4 in range(4):
                                    lg = lt * 4 + lg4
                                    last = t.matmul(
                                        psc[idx % 2][:, lg4 * 32:(lg4 + 1) * 32],
                                        intG[:, lg, ch * 128:(ch + 1) * 128],
                                        permT[:, (g0 + lg) * 32:(g0 + lg + 1) * 32],
                                        start=(lg4 == 0), stop=(lg4 == 3))
                                last.then_inc(ssc, 1)
                    ctx.inc(ssc, 4 * NCH)

                    @blk.vector
                    def _(v):
                        v.wait_ge(sxp, b_xp + 16)
                        for lt in range(4):
                            for ch in range(NCH):
                                idx = lt * NCH + ch
                                v.wait_ge(ssc, b_sc + idx + 1)
                                v.scalar_tensor_tensor(
                                    xpc[gb % 2][:, ch, lt * 128:(lt + 1) * 128],
                                    psc[idx % 2][:, :], 0.5,
                                    xpc[gb % 2][:, ch, lt * 128:(lt + 1) * 128],
                                    ALU.mult, ALU.add).then_inc(sre, 1)
                    ctx.inc(sre, 4 * NCH)

                    @blk.sync
                    def _(sy):
                        sy.wait_ge(sre, b_re + 4 * NCH)
                        sy.dma_start(
                            out=xout_d.rearrange("(c p) t -> p c t", p=128)
                            [:, :, t0:t0 + 512],
                            in_=xpc[gb % 2][:, :, :]).then_inc(sst, 16)
                        if gb == NGB - 1:
                            sy.wait_ge(sst, b_st + 16)
                    ctx.inc(sst, 16)


def prep_ffn0_host(controller, f1, bias, f2):
    w_ctrl = np.ascontiguousarray(np.asarray(controller, np.float32).reshape(8, 128, 128))
    w_f1 = np.ascontiguousarray(
        np.transpose(np.asarray(f1, np.float32).reshape(8, 128, 128, 32), (1, 2, 0, 3)))
    w_f1b = np.ascontiguousarray(np.asarray(bias, np.float32).reshape(128, 32).T)
    w_f2 = np.ascontiguousarray(
        np.transpose(np.asarray(f2, np.float32).reshape(128, 32, 1024), (1, 0, 2)))
    tie = np.linspace(0.0, 1e-6, 32, dtype=np.float32)
    c_tie = np.ascontiguousarray(np.broadcast_to(np.tile(tie, 16)[None, :], (128, 512)))
    return dict(w_ctrl=w_ctrl, w_f1=w_f1, w_f1b=w_f1b, w_f2=w_f2, c_tie=c_tie)


def const_host():
    i64 = np.eye(64, dtype=BF)
    return dict(c_ident=np.eye(128, dtype=BF),
                c_ident2=np.ascontiguousarray(np.concatenate([i64, i64], axis=0)))


FFN0_DRAM_SPECS = [
    ("w_ctrl", [8, 128, 128], F32),
    ("w_f1", [128, 128, 8, 32], F32),
    ("w_f1b", [32, 128], F32),
    ("w_f2", [32, 128, 1024], F32),
    ("c_tie", [128, 512], F32),
]


def build_ffn0_launch():
    nc = bass.Bass("TRN2", target_bir_lowering=False, detect_race_conditions=False)
    dram = {}
    dram["x_in"] = nc.dram_tensor("x_in", [D, T], F32, kind="ExternalInput").ap()
    dram["x_out"] = nc.dram_tensor("x_out", [D, T], F32, kind="ExternalOutput").ap()
    dram["c_ident"] = nc.dram_tensor("c_ident", [128, 128], BF16, kind="ExternalInput").ap()
    dram["c_ident2"] = nc.dram_tensor("c_ident2", [128, 64], BF16, kind="ExternalInput").ap()
    for nm, shape, dt in FFN0_DRAM_SPECS:
        dram[nm] = nc.dram_tensor(nm, shape, dt, kind="ExternalInput").ap()
    park = nc.dram_tensor("park", [D, T], F32).ap()
    intd = nc.dram_tensor("intd", [128, 16, 1024], BF16).ap()
    ctx = Ctx(nc, dram)
    with ctx.root:
        emit_const_setup(ctx)
        emit_ffn0(ctx, dram["x_in"], dram["x_out"], park, intd)
    return nc




def emit_att0a(ctx, xin_d, qT_d, kT_d, vhm_d):
    nc = ctx.nc
    dram = ctx.dram
    with ctx.scoped():
        xT = ctx.sb("a0_xT", [128, NCH, T], F32)
        xn = ctx.sb("a0_xn", [128, NCH, T], BF16)
        sx = ctx.sem("a0_sx")
        with nc.Block() as blk:
            @blk.sync
            def _(sy):
                sy.dma_start(out=xT[:, :, :],
                             in_=xin_d.rearrange("(c p) t -> p c t", p=128)) \
                    .then_inc(sx, 16)
                sy.wait_ge(sx, ctx.inc(sx, 16))
        emit_ln(ctx, xT, xn, T)

    with ctx.scoped():
        wqk = [ctx.sb("a0_wqk", [128, NCH, 128], BF16) for _ in range(2)]
        ctab = ctx.sb("a0_cos", [128, T], F32)
        stab = ctx.sb("a0_sin", [128, T], F32)
        qa = ctx.sb("a0_qa", [128, T], F32)
        qs = ctx.sb("a0_qs", [128, T], F32)
        qrot = ctx.sb("a0_qrot", [128, T], BF16)
        swl = ctx.sem("a0_wl")
        smm = ctx.sem("a0_mm")
        sev = ctx.sem("a0_ev")
        srp = ctx.sem("a0_rp")
        sst = ctx.sem("a0_st")
        b_wl, b_mm, b_ev = ctx.cnt(swl), ctx.cnt(smm), ctx.cnt(sev)
        b_rp, b_st = ctx.cnt(srp), ctx.cnt(sst)
        with ctx.scoped():
            pq = [ctx.ps("a0_pq", [128, 512], F32) for _ in range(4)]
            with nc.Block() as blk:
                @blk.gpsimd
                def _(g):
                    g.dma_start(out=ctab[:, :], in_=dram["c_cos"][:, :]).then_inc(swl, 16)
                    g.dma_start(out=stab[:, :], in_=dram["c_sin"][:, :]).then_inc(swl, 16)
                    for m in range(16):
                        if m >= 2:
                            g.wait_ge(smm, b_mm + 4 * (m - 1))
                        g.dma_start(out=wqk[m % 2][:, :, :],
                                    in_=dram["w_qk"][m, :, :, :].transpose([1, 0, 2])).then_inc(swl, 16)
                ctx.inc(swl, 18 * 16)

                @blk.tensor
                def _(t):
                    for m in range(16):
                        t.wait_ge(swl, b_wl + 32 + 16 * (m + 1))
                        if m >= 1:
                            t.wait_ge(srp, b_rp + m)
                        for tb in range(4):
                            for ch in range(NCH):
                                last = t.matmul(pq[tb % 4][:, :], wqk[m % 2][:, ch, :],
                                                xn[:, ch, tb * 512:(tb + 1) * 512],
                                                start=(ch == 0), stop=(ch == NCH - 1))
                            last.then_inc(smm, 1)
                ctx.inc(smm, 64)

                @blk.vector
                def _(v):
                    for m in range(16):
                        for tb in range(4):
                            v.wait_ge(smm, b_mm + 4 * m + tb + 1)
                            v.tensor_copy(qa[:, tb * 512:(tb + 1) * 512],
                                          pq[tb % 4][:, :])
                        if m >= 1:
                            v.wait_ge(sst, b_st + 16 * m)
                        for j in range(4):
                            src = (j // 2) * 64 + (1 - j % 2) * 32
                            dst = (j // 2) * 64 + (j % 2) * 32
                            v.tensor_copy(qs[dst:dst + 32, :], qa[src:src + 32, :])
                        v.tensor_mul(qs[:, :], qs[:, :], stab[:, :])
                        v.tensor_mul(qa[:, :], qa[:, :], ctab[:, :])
                        v.tensor_add(qrot[:, :], qa[:, :], qs[:, :]).then_inc(srp, 1)
                ctx.inc(srp, 16)

                @blk.sync
                def _(sy):
                    for m in range(16):
                        sy.wait_ge(srp, b_rp + m + 1)
                        dst = qT_d if m < 8 else kT_d
                        mm = m % 8
                        sy.dma_start(out=dst[mm * 128:(mm + 1) * 128, :],
                                     in_=qrot[:, :]).then_inc(sst, 16)
                        sy.wait_ge(sst, b_st + 16 * (m + 1))
                ctx.inc(sst, 16 * 16)

        with ctx.scoped():
            wv = ctx.sb("a0_wv", [128, NCH, 1024], BF16)
            vtile = [ctx.sb("a0_vt", [128, 16, 65], BF16) for _ in range(2)]
            pv = [ctx.ps("a0_pv", [128, 512], F32) for _ in range(4)]
            b_wl2, b_mm2, b_ev2 = ctx.cnt(swl), ctx.cnt(smm), ctx.cnt(sev)
            b_st2 = ctx.cnt(sst)
            with nc.Block() as blk:
                @blk.gpsimd
                def _(g):
                    g.dma_start(out=wv[:, :, :], in_=dram["w_v"].transpose([1, 0, 2])) \
                        .then_inc(swl, 16)
                ctx.inc(swl, 16)

                @blk.scalar
                def _(s):
                    pass

                @blk.tensor
                def _(t):
                    t.wait_ge(swl, b_wl2 + 16)
                    for it in range(NT):
                        if it >= 2:
                            t.wait_ge(sev, b_ev2 + it - 1)
                        for nb in range(2):
                            for ch in range(NCH):
                                last = t.matmul(pv[2 * (it % 2) + nb][:, :],
                                                xn[:, ch, it * 128:(it + 1) * 128],
                                                wv[:, ch, nb * 512:(nb + 1) * 512],
                                                start=(ch == 0), stop=(ch == NCH - 1))
                            last.then_inc(smm, 1)
                ctx.inc(smm, NT * 2)

                @blk.vector
                def _(v):
                    v.memset(vtile[0][:, :, 64:65], 1.0)
                    v.memset(vtile[1][:, :, 64:65], 1.0)
                    for it in range(NT):
                        if it >= 2:
                            v.wait_ge(sst, b_st2 + 16 * (it - 1))
                        for nb in range(2):
                            v.wait_ge(smm, b_mm2 + 2 * it + nb + 1)
                            last = v.tensor_copy(
                                vtile[it % 2][:, nb * 8:(nb + 1) * 8, 0:64],
                                pv[2 * (it % 2) + nb][:, :]
                                .rearrange("p (h e) -> p h e", e=64))
                        last.then_inc(sev, 1)
                ctx.inc(sev, NT)

                @blk.sync
                def _(sy):
                    for it in range(NT):
                        sy.wait_ge(sev, b_ev2 + it + 1)
                        sy.dma_start(
                            out=vhm_d[:, :, it, :].transpose([1, 0, 2]),
                            in_=vtile[it % 2][:, :, :],
                        ).then_inc(sst, 16)
                    sy.wait_ge(sst, b_st2 + 16 * NT)
                ctx.inc(sst, 16 * NT)


def ident_of(ctx):
    return ctx.const["ident"][:, :]


def prep_att0a_host(w_qkv, s0):
    w = np.asarray(w_qkv, np.float32)
    perm = np.zeros(1024, np.int64)
    for h in range(16):
        for j in range(64):
            perm[h * 64 + j] = h * 64 + (2 * j if j < 32 else 2 * (j - 32) + 1)
    wq = w[:, 0:1024][:, perm]
    wk = w[:, 1024:2048][:, perm]
    wv = w[:, 2048:3072]
    def chunks(a):
        return np.ascontiguousarray(
            a.reshape(8, 128, 8, 128).transpose(2, 0, 1, 3))
    w_qk = np.concatenate([chunks(wq), chunks(wk)], axis=0)
    w_v = np.ascontiguousarray(wv.reshape(8, 128, 1024))
    i = np.arange(32, dtype=np.float64)
    theta = (10000.0 ** (-(2 * i) / 64.0)) * 3.0
    pos = np.arange(s0, s0 + T, dtype=np.float64)
    ang = pos[:, None] * theta[None, :]
    cos = np.cos(ang).astype(np.float32).T
    sin = np.sin(ang).astype(np.float32).T
    blk64_c = np.concatenate([cos, cos], axis=0)
    blk64_s = np.concatenate([-sin, sin], axis=0)
    c_cos = np.ascontiguousarray(np.tile(blk64_c, (2, 1)))
    c_sin = np.ascontiguousarray(np.tile(blk64_s, (2, 1)))
    return dict(w_qk=w_qk, w_v=w_v, c_cos=c_cos, c_sin=c_sin)


ATT0A_DRAM_SPECS = [
    ("w_qk", [16, 8, 128, 128], F32),
    ("w_v", [8, 128, 1024], F32),
    ("c_cos", [128, T], F32),
    ("c_sin", [128, T], F32),
]


def build_att0a_launch():
    nc = bass.Bass("TRN2", target_bir_lowering=False, detect_race_conditions=False)
    dram = {}
    dram["x_in"] = nc.dram_tensor("x_in", [D, T], F32, kind="ExternalInput").ap()
    dram["c_ident"] = nc.dram_tensor("c_ident", [128, 128], BF16, kind="ExternalInput").ap()
    dram["c_ident2"] = nc.dram_tensor("c_ident2", [128, 64], BF16, kind="ExternalInput").ap()
    for nm, shape, dt in ATT0A_DRAM_SPECS:
        dram[nm] = nc.dram_tensor(nm, shape, dt, kind="ExternalInput").ap()
    qT = nc.dram_tensor("qT", [D, T], BF16, kind="ExternalOutput").ap()
    kT = nc.dram_tensor("kT", [D, T], BF16, kind="ExternalOutput").ap()
    vhm = nc.dram_tensor("vhm", [16, 128, NT, 65], BF16, kind="ExternalOutput").ap()
    ctx = Ctx(nc, dram)
    with ctx.root:
        emit_const_setup(ctx)
        emit_att0a(ctx, dram["x_in"], qT, kT, vhm)
    return nc


def emit_att0b(ctx, xin_d, qT_d, kT_d, vhm_d, xout_d):
    nc = ctx.nc
    dram = ctx.dram
    NKC = S // 1024
    onesrow = ctx.const["ones_row"]
    with ctx.scoped():
        attn_T = ctx.sb("a0b_at", [128, NCH, T], BF16)
        inner_scope = ctx.scoped()
        inner_scope.__enter__()
        qTc = ctx.sb("a0b_q", [128, NCH, T], BF16)
        out_sb = ctx.sb("a0b_os", [128, 16, 512], F32)
        kTc = [ctx.sb("a0b_k", [128, NCH, 1024], BF16) for _ in range(2)]
        vga = [ctx.sb("a0b_v", [128, 16, 8, 65], BF16) for _ in range(2)]
        Pt = [ctx.sb("a0b_pt", [128, 1024], BF16) for _ in range(2)]
        rcp = ctx.sb("a0b_rcp", [1, 512], F32)
        sx = ctx.sem("a0b_sx")
        skv = ctx.sem("a0b_kv")
        sS = ctx.sem("a0b_S")
        sE = ctx.sem("a0b_E")
        sP = ctx.sem("a0b_P")
        sO = ctx.sem("a0b_O")
        sN = ctx.sem("a0b_N")
        sB = ctx.sem("a0b_B")
        sM = ctx.sem("a0b_M")

        with nc.Block() as blk:
            @blk.sync
            def _(sy):
                sy.dma_start(out=qTc[:, :, :],
                             in_=qT_d.rearrange("(hp p) q -> p hp q", p=128)) \
                    .then_inc(sx, 16)
                sy.wait_ge(sx, ctx.inc(sx, 16))

        for qb in range(4):
            q0 = qb * 512
            with ctx.scoped():
                pS = [ctx.ps("a0b_pS", [128, 1024], F32) for _ in range(2)]
                pO = [ctx.ps("a0b_pO", [128, 512], F32) for _ in range(2)]
                b_kv, b_S, b_E = ctx.cnt(skv), ctx.cnt(sS), ctx.cnt(sE)
                b_P, b_O = ctx.cnt(sP), ctx.cnt(sO)
                with nc.Block() as blk:
                    @blk.gpsimd
                    def _(g):
                        for kc in range(NKC):
                            if kc >= 2:
                                g.wait_ge(sP, b_P + (kc - 1) * 64)
                            r, lp = kc // 2, (kc % 2) * 1024
                            g.dma_start(out=kTc[kc % 2][:, :, :],
                                        in_=kT_d[r, :, lp:lp + 1024]
                                        .rearrange("(hp p) k -> p hp k", p=128)) \
                                .then_inc(skv, 16)
                            g.dma_start(out=vga[kc % 2][:, :, :, :],
                                        in_=vhm_d[r, :, :, (kc % 2) * 8:
                                                  (kc % 2) * 8 + 8, :]
                                        .transpose([1, 0, 2, 3])) \
                                .then_inc(skv, 16)
                    ctx.inc(skv, 32 * NKC)

                    @blk.tensor
                    def _(t):
                        for kc in range(NKC):
                            t.wait_ge(skv, b_kv + 32 * (kc + 1))
                            for h in range(16):
                                hb = (h % 2) * 64
                                hp = h // 2
                                idx = kc * 16 + h
                                for u in range(4):
                                    j = 4 * idx + u
                                    if j >= 2:
                                        t.wait_ge(sE, b_E + j - 1)
                                    for kt in range(2):
                                        ktile = 2 * u + kt
                                        last = t.matmul(
                                            pS[j % 2][:, kt * 512:(kt + 1) * 512],
                                            kTc[kc % 2][hb:hb + 64, hp,
                                                        ktile * 128:(ktile + 1) * 128],
                                            qTc[hb:hb + 64, hp, q0:q0 + 512],
                                            start=True, stop=True)
                                    last.then_inc(sS, 1)
                                    if u >= 1:
                                        jp = j - 1
                                        up = u - 1
                                        t.wait_ge(sE, b_E + jp + 1)
                                        if idx >= 2 and up == 0:
                                            t.wait_ge(sO, b_O + idx - 1)
                                        for kt in range(2):
                                            ktile = 2 * up + kt
                                            last = t.matmul(
                                                pO[h % 2][0:65, :],
                                                vga[kc % 2][:, h, ktile, :],
                                                Pt[jp % 2][:, kt * 512:(kt + 1) * 512],
                                                start=(up == 0 and kt == 0),
                                                stop=False)
                                        last.then_inc(sP, 1)
                                j = 4 * idx + 3
                                t.wait_ge(sE, b_E + j + 1)
                                for kt in range(2):
                                    ktile = 6 + kt
                                    last = t.matmul(
                                        pO[h % 2][0:65, :],
                                        vga[kc % 2][:, h, ktile, :],
                                        Pt[j % 2][:, kt * 512:(kt + 1) * 512],
                                        start=False, stop=(kt == 1))
                                last.then_inc(sP, 1)
                    ctx.inc(sS, NKC * 64)
                    ctx.inc(sP, NKC * 64)

                    @blk.scalar
                    def _(s):
                        for kc in range(NKC):
                            for h in range(16):
                                idx = kc * 16 + h
                                for u in range(4):
                                    j = 4 * idx + u
                                    s.wait_ge(sS, b_S + j + 1)
                                    if j >= 2:
                                        s.wait_ge(sP, b_P + j - 1)
                                    s.activation(Pt[j % 2][:, :], pS[j % 2][:, :],
                                                 AF.Exp, scale=np.float32(0.125)) \
                                        .then_inc(sE, 1)
                    ctx.inc(sE, NKC * 64)

                    @blk.vector
                    def _(v):
                        for kc in range(NKC):
                            for h in range(16):
                                idx = kc * 16 + h
                                v.wait_ge(sP, b_P + 4 * idx + 4)
                                if kc == 0:
                                    v.tensor_copy(out_sb[0:65, h, :],
                                                  pO[h % 2][0:65, :]).then_inc(sO, 1)
                                else:
                                    v.tensor_add(out_sb[0:65, h, :],
                                                 out_sb[0:65, h, :],
                                                 pO[h % 2][0:65, :]).then_inc(sO, 1)
                    ctx.inc(sO, NKC * 16)

            with ctx.scoped():
                pB = [ctx.ps("a0b_pB", [64, 512], F32) for _ in range(2)]
                b_N, b_B, b_M = ctx.cnt(sN), ctx.cnt(sB), ctx.cnt(sM)
                with nc.Block() as blk:
                    @blk.vector
                    def _(v):
                        for h in range(16):
                            v.reciprocal(rcp[:, :], out_sb[64:65, h, :]).then_inc(sN, 1)
                            v.wait_ge(sB, b_B + h + 1)
                            v.tensor_mul(
                                attn_T[(h % 2) * 64:(h % 2) * 64 + 64, h // 2,
                                       q0:q0 + 512],
                                out_sb[0:64, h, :], pB[h % 2][:, :]).then_inc(sM, 1)
                    ctx.inc(sN, 16)
                    ctx.inc(sM, 16)

                    @blk.tensor
                    def _(t):
                        for h in range(16):
                            t.wait_ge(sN, b_N + h + 1)
                            if h >= 2:
                                t.wait_ge(sM, b_M + h - 1)
                            t.matmul(pB[h % 2][:, :], onesrow[0:1, 0:64],
                                     rcp[:, :], start=True, stop=True).then_inc(sB, 1)
                    ctx.inc(sB, 16)

        inner_scope.__exit__(None, None, None)

        with ctx.scoped():
            wp = ctx.sb("a0b_wp", [128, NCH, 1024], BF16)
            bcol = ctx.sb("a0b_bc", [128, NCH], F32)
            xb1 = [ctx.sb("a0b_x1", [128, NCH, 512], F32) for _ in range(2)]
            po = [ctx.ps("a0b_po", [128, 512], F32) for _ in range(2)]
            swp = ctx.sem("a0b_wp")
            sxl = ctx.sem("a0b_xl")
            som = ctx.sem("a0b_om")
            soe = ctx.sem("a0b_oe")
            sos = ctx.sem("a0b_ost")
            b_wp, b_xl, b_om = ctx.cnt(swp), ctx.cnt(sxl), ctx.cnt(som)
            b_oe, b_os = ctx.cnt(soe), ctx.cnt(sos)
            with nc.Block() as blk:
                @blk.gpsimd
                def _(g):
                    g.dma_start(out=wp[:, :, :],
                                in_=dram["w_proj"].transpose([1, 0, 2])).then_inc(swp, 16)
                    g.dma_start(out=bcol[:, :],
                                in_=dram["b_proj"][:, :]).then_inc(swp, 16)
                    for tb in range(4):
                        if tb >= 2:
                            g.wait_ge(sos, b_os + 16 * (tb - 1))
                        g.dma_start(out=xb1[tb % 2][:, :, :],
                                    in_=xin_d.rearrange("(c p) t -> p c t", p=128)
                                    [:, :, tb * 512:(tb + 1) * 512]).then_inc(sxl, 16)
                ctx.inc(swp, 32)
                ctx.inc(sxl, 64)

                @blk.tensor
                def _(t):
                    t.wait_ge(swp, b_wp + 32)
                    for tb in range(4):
                        for m in range(NCH):
                            idx = tb * NCH + m
                            if idx >= 2:
                                t.wait_ge(soe, b_oe + idx - 1)
                            for ch in range(NCH):
                                last = t.matmul(po[idx % 2][:, :],
                                                wp[:, ch, m * 128:(m + 1) * 128],
                                                attn_T[:, ch, tb * 512:(tb + 1) * 512],
                                                start=(ch == 0), stop=(ch == NCH - 1))
                            last.then_inc(som, 1)
                ctx.inc(som, 32)

                @blk.vector
                def _(v):
                    for tb in range(4):
                        v.wait_ge(sxl, b_xl + 16 * (tb + 1))
                        for m in range(NCH):
                            idx = tb * NCH + m
                            v.wait_ge(som, b_om + idx + 1)
                            v.scalar_tensor_tensor(
                                xb1[tb % 2][:, m, :], po[idx % 2][:, :],
                                bcol[:, m:m + 1], xb1[tb % 2][:, m, :],
                                ALU.add, ALU.add).then_inc(soe, 1)
                ctx.inc(soe, 32)

                @blk.sync
                def _(sy):
                    for tb in range(4):
                        sy.wait_ge(soe, b_oe + NCH * (tb + 1))
                        sy.dma_start(
                            out=xout_d.rearrange("(c p) t -> p c t", p=128)
                            [:, :, tb * 512:(tb + 1) * 512],
                            in_=xb1[tb % 2][:, :, :]).then_inc(sos, 16)
                    sy.wait_ge(sos, b_os + 64)
                ctx.inc(sos, 64)


def prep_att0b_host(w_proj, b_proj):
    wp = np.ascontiguousarray(np.asarray(w_proj, np.float32).reshape(8, 128, 1024))
    bc = np.ascontiguousarray(np.asarray(b_proj, np.float32).reshape(8, 128).T)
    return dict(w_proj=wp, b_proj=bc)


ATT0B_DRAM_SPECS = [
    ("w_proj", [8, 128, 1024], F32),
    ("b_proj", [128, 8], F32),
]


def build_att0b_launch():
    nc = bass.Bass("TRN2", target_bir_lowering=False, detect_race_conditions=False)
    dram = {}
    dram["x_in"] = nc.dram_tensor("x_in", [D, T], F32, kind="ExternalInput").ap()
    dram["qT"] = nc.dram_tensor("qT", [D, T], BF16, kind="ExternalInput").ap()
    dram["kT"] = nc.dram_tensor("kT", [4, D, T], BF16, kind="ExternalInput").ap()
    dram["vhm"] = nc.dram_tensor("vhm", [4, 16, 128, NT, 65], BF16,
                                 kind="ExternalInput").ap()
    dram["c_ident"] = nc.dram_tensor("c_ident", [128, 128], BF16, kind="ExternalInput").ap()
    dram["c_ident2"] = nc.dram_tensor("c_ident2", [128, 64], BF16, kind="ExternalInput").ap()
    for nm, shape, dt in ATT0B_DRAM_SPECS:
        dram[nm] = nc.dram_tensor(nm, shape, dt, kind="ExternalInput").ap()
    xout = nc.dram_tensor("x_out", [D, T], F32, kind="ExternalOutput").ap()
    ctx = Ctx(nc, dram)
    with ctx.root:
        emit_const_setup(ctx)
        emit_att0b(ctx, dram["x_in"], dram["qT"], dram["kT"], dram["vhm"], xout)
    return nc


def emit_ffn1_spec(ctx, etype, i, hbuf1, hbuf2, TB, s0b, l0g, l0b, s1ab, s1bb,
                   l1g, l1b, s2b, phase):
    nc = ctx.nc
    dram = ctx.dram
    onesbf = ctx.const["ones_bf"]
    onesrow = ctx.const["ones_row"]
    onescol = ctx.const["ones_col"]
    NB = TB // 512

    def dense(dst, src_buf, nk, w_dram_idx, w_name, nm_out, bias_cols, post,
              ln_g=None, ln_b=None):
        with ctx.scoped():
            ww = [ctx.sb("f1_sw", [128, nk, 128], BF16) for _ in range(2)]
            pp = [ctx.ps("f1_sp", [128, 512], F32) for _ in range(2)]
            swl = ctx.sem("f1_wl")
            smm = ctx.sem("f1_mm")
            sev = ctx.sem("f1_ev")
            b_wl, b_mm, b_ev = ctx.cnt(swl), ctx.cnt(smm), ctx.cnt(sev)
            with nc.Block() as blk:
                @blk.gpsimd
                def _(g):
                    for m in range(nm_out):
                        if m >= 2:
                            g.wait_ge(smm, b_mm + NB * (m - 1))
                        g.dma_start(out=ww[m % 2][:, :, :],
                                    in_=dram[w_name][w_dram_idx, m, :, :, :]
                                    .transpose([1, 0, 2])).then_inc(swl, 16)
                ctx.inc(swl, nm_out * 16)

                @blk.tensor
                def _(t):
                    for m in range(nm_out):
                        t.wait_ge(swl, b_wl + 16 * (m + 1))
                        for nb in range(NB):
                            k = NB * m + nb
                            if k >= 2:
                                t.wait_ge(sev, b_ev + k - 1)
                            for ch in range(nk):
                                last = t.matmul(
                                    pp[k % 2][:, :], ww[m % 2][:, ch, :],
                                    src_buf[:, ch, nb * 512:(nb + 1) * 512],
                                    start=(ch == 0), stop=(ch == nk - 1))
                            last.then_inc(smm, 1)
                ctx.inc(smm, nm_out * NB)

                @blk.scalar
                def _(s):
                    for m in range(nm_out):
                        for nb in range(NB):
                            k = NB * m + nb
                            s.wait_ge(smm, b_mm + k + 1)
                            s.activation(
                                dst[:, m, nb * 512:(nb + 1) * 512],
                                pp[k % 2][:, :],
                                AF.Gelu if post == "gelu" else AF.Identity,
                                bias=bias_cols[:, i, m:m + 1],
                                scale=onescol[:, :]).then_inc(sev, 1)
                ctx.inc(sev, nm_out * NB)

    def lnorm(buf, nm, g_cols, b_cols, gelu_after):
        F = 128 * nm
        with ctx.scoped():
            sq = [ctx.sb("f1_lsq", [128, 512], BF16) for _ in range(2)]
            stat_s = [ctx.ps("f1_lss", [1, 512], F32) for _ in range(NB)]
            stat_q = [ctx.ps("f1_lsq2", [1, 512], F32) for _ in range(NB)]
            rsum = ctx.sb("f1_lrs", [1, TB], F32)
            rsq = ctx.sb("f1_lrq", [1, TB], F32)
            rrow = ctx.sb("f1_lr", [1, TB], F32)
            prow = ctx.sb("f1_lp", [1, TB], F32)
            rB = ctx.sb("f1_lrB", [128, TB], F32)
            pB = ctx.sb("f1_lpB", [128, TB], F32)
            tmp = ctx.sb("f1_ltmp", [128, 512], F32)
            bmat = [ctx.ps("f1_lbm", [128, 512], F32) for _ in range(2)]
            ssq = ctx.sem("f1_lq")
            smt = ctx.sem("f1_lmt")
            sse = ctx.sem("f1_lse")
            b_q, b_mt, b_se = ctx.cnt(ssq), ctx.cnt(smt), ctx.cnt(sse)
            with nc.Block() as blk:
                @blk.scalar
                def _(s):
                    for m in range(nm):
                        for nb in range(NB):
                            k = NB * m + nb
                            if k >= 2:
                                s.wait_ge(smt, b_mt + k - 1)
                            s.activation(sq[k % 2][:, :],
                                         buf[:, m, nb * 512:(nb + 1) * 512],
                                         AF.Square).then_inc(ssq, 1)
                ctx.inc(ssq, nm * NB)

                @blk.tensor
                def _(t):
                    for m in range(nm):
                        for nb in range(NB):
                            k = NB * m + nb
                            t.wait_ge(ssq, b_q + k + 1)
                            t.matmul(stat_s[nb][:, :], onesbf[:, :],
                                     buf[:, m, nb * 512:(nb + 1) * 512],
                                     start=(m == 0), stop=(m == nm - 1))
                            last = t.matmul(stat_q[nb][:, :], onesbf[:, :],
                                            sq[k % 2][:, :],
                                            start=(m == 0), stop=(m == nm - 1))
                            last.then_inc(smt, 1)
                ctx.inc(smt, nm * NB)

                @blk.vector
                def _(v):
                    v.wait_ge(smt, b_mt + nm * NB)
                    for nb in range(NB):
                        v.tensor_copy(rsum[:, nb * 512:(nb + 1) * 512], stat_s[nb][:, :])
                        v.tensor_copy(rsq[:, nb * 512:(nb + 1) * 512], stat_q[nb][:, :])
                    v.nop().then_inc(sse, 1)
            ctx.inc(sse, 1)

            s1 = ctx.sem("f1_ls1")
            b_s1 = ctx.cnt(s1)
            with nc.Block() as blk:
                @blk.scalar
                def _(s):
                    s.activation(prow[:, :], rsum[:, :], AF.Square,
                                 bias=np.float32(0.0),
                                 scale=np.float32(1.0 / F)).then_inc(s1, 1)

                @blk.vector
                def _(v):
                    v.wait_ge(s1, ctx.inc(s1))
                    v.scalar_tensor_tensor(rrow[:, :], rsq[:, :], 1.0 / F, prow[:, :],
                                           ALU.mult, ALU.subtract).then_inc(s1, 1)

                @blk.scalar
                def _(s):
                    s.wait_ge(s1, ctx.inc(s1))
                    s.activation(rrow[:, :], rrow[:, :], AF.Sqrt,
                                 bias=np.float32(EPS_LN),
                                 scale=np.float32(1.0)).then_inc(s1, 1)

                @blk.vector
                def _(v):
                    sv = SerialEng(ctx, v, ctx.sem("f1_lsv"))
                    v.wait_ge(s1, ctx.inc(s1))
                    sv.reciprocal(rrow[:, :], rrow[:, :])
                    sv.tensor_scalar_mul(prow[:, :], rsum[:, :], 1.0 / F)
                    sv.tensor_mul(prow[:, :], prow[:, :], rrow[:, :])
                    sv.tensor_scalar_mul(prow[:, :], prow[:, :], -1.0)

            b_mt2, b_se2 = ctx.cnt(smt), ctx.cnt(sse)
            with nc.Block() as blk:
                @blk.tensor
                def _(t):
                    for j, row in enumerate((rrow, prow)):
                        for nb in range(NB):
                            k = j * NB + nb
                            if k >= 2:
                                t.wait_ge(sse, b_se2 + k - 1)
                            t.matmul(bmat[k % 2][:, :], onesrow[:, :],
                                     row[:, nb * 512:(nb + 1) * 512],
                                     start=True, stop=True).then_inc(smt, 1)
                ctx.inc(smt, 2 * NB)

                @blk.vector
                def _(v):
                    for j, dstb in enumerate((rB, pB)):
                        for nb in range(NB):
                            k = j * NB + nb
                            v.wait_ge(smt, b_mt2 + k + 1)
                            v.tensor_copy(dstb[:, nb * 512:(nb + 1) * 512],
                                          bmat[k % 2][:, :]).then_inc(sse, 1)
                ctx.inc(sse, 2 * NB)

            if getattr(ctx, "_lnorm_dbg", False):
                sdbg = ctx.sem("f1_dbg")
                with nc.Block() as blk:
                    @blk.sync
                    def _(sy):
                        sy.dma_start(out=ctx.dbg["rows"][0:1, :], in_=rsum[:, :]).then_inc(sdbg, 16)
                        sy.dma_start(out=ctx.dbg["rows"][1:2, :], in_=rsq[:, :]).then_inc(sdbg, 16)
                        sy.dma_start(out=ctx.dbg["rows"][2:3, :], in_=rrow[:, :]).then_inc(sdbg, 16)
                        sy.dma_start(out=ctx.dbg["rows"][3:4, :], in_=prow[:, :]).then_inc(sdbg, 16)
                        sy.wait_ge(sdbg, ctx.inc(sdbg, 64))
            sap = ctx.sem("f1_lap")
            b_ap = ctx.cnt(sap)
            with nc.Block() as blk:
                @blk.vector
                def _(v):
                    for m in range(nm):
                        for nb in range(NB):
                            sl = slice(nb * 512, (nb + 1) * 512)
                            v.tensor_mul(tmp[:, :], buf[:, m, sl], rB[:, sl])
                            last = v.tensor_add(tmp[:, :], tmp[:, :], pB[:, sl])
                            if gelu_after:
                                last.then_inc(sap, 1)
                            else:
                                v.tensor_scalar(buf[:, m, sl], tmp[:, :],
                                                g_cols[:, i, m:m + 1],
                                                b_cols[:, i, m:m + 1],
                                                ALU.mult, ALU.add)
                            if gelu_after:
                                v.wait_ge(sap, b_ap + 2 * (NB * m + nb) + 2)
                ctx.inc(sap, nm * NB * (1 if gelu_after else 0))

                if gelu_after:
                    @blk.scalar
                    def _(s):
                        for m in range(nm):
                            for nb in range(NB):
                                k = NB * m + nb
                                sl = slice(nb * 512, (nb + 1) * 512)
                                s.wait_ge(sap, b_ap + 2 * k + 1)
                                s.activation(buf[:, m, sl], tmp[:, :], AF.Gelu,
                                             bias=b_cols[:, i, m:m + 1],
                                             scale=g_cols[:, i, m:m + 1]) \
                                    .then_inc(sap, 1)
                    ctx.inc(sap, nm * NB)

    if phase == "dense":
        if etype == "s0":
            dense(hbuf2, hbuf1, 32, i, "f1_s0_W", 32, s0b, "copy")
        elif etype == "s2":
            dense(hbuf2, hbuf1, 32, i, "f1_s2_W", 32, s2b, "gelu")
        else:
            with ctx.scoped():
                t1buf = ctx.sb("f1_t1", [128, 16, TB], BF16)
                dense(t1buf, hbuf1, 32, i, "f1_s1a_W", 16, s1ab, "gelu")
                dense(hbuf1, t1buf, 16, i, "f1_s1b_W", 32, s1bb, "copy")
    else:
        if etype == "s0":
            lnorm(hbuf2, 32, l0g, l0b, gelu_after=True)
        elif etype == "s1":
            lnorm(hbuf2, 32, l1g, l1b, gelu_after=False)


def emit_ffn1_down(ctx, e, tb, t0, TB, hdown, w_sb, y_nat, first):
    nc = ctx.nc
    dram = ctx.dram
    onesrow = ctx.const["ones_row"]
    NB = TB // 512
    NTT = TB // 128
    with ctx.scoped():
        dww = ctx.sb("f1_dww", [128, 32, 512], BF16)
        dnb = ctx.sb("f1_dnb", [1, 512], BF16)
        pd = [ctx.ps("f1_pd", [128, 512], F32) for _ in range(2)]
        swl = ctx.sem("f1_dwl")
        smm = ctx.sem("f1_dmm")
        sev = ctx.sem("f1_dev")
        onesrow_bf = ctx.const["ones_row_bf"]
        for nb2 in range(2):
            b_wl, b_mm, b_ev = ctx.cnt(swl), ctx.cnt(smm), ctx.cnt(sev)
            with nc.Block() as blk:
                @blk.gpsimd
                def _(g):
                    g.dma_start(out=dww[:, :, :],
                                in_=dram["f1_down_W"][e, :, :,
                                                      nb2 * 512:(nb2 + 1) * 512]
                                .transpose([1, 0, 2])).then_inc(swl, 16)
                    g.dma_start(out=dnb[:, :],
                                in_=dram["f1_down_b"][:, e,
                                                      nb2 * 512:(nb2 + 1) * 512]) \
                        .then_inc(swl, 16)
                ctx.inc(swl, 32)

                @blk.tensor
                def _(t):
                    t.wait_ge(swl, b_wl + 32)
                    for tile in range(NTT):
                        if tile >= 2:
                            t.wait_ge(sev, b_ev + tile - 1)
                        for kch in range(32):
                            t.matmul(pd[tile % 2][:, :],
                                     hdown[:, kch, tile * 128:(tile + 1) * 128],
                                     dww[:, kch, :],
                                     start=(kch == 0), stop=False)
                        last = t.matmul(pd[tile % 2][:, :], onesrow_bf[:, :],
                                        dnb[:, :],
                                        start=False, stop=True)
                        last.then_inc(smm, 1)
                ctx.inc(smm, NTT)

                @blk.vector
                def _(v):
                    for tile in range(NTT):
                        v.wait_ge(smm, b_mm + tile + 1)
                        gtile = (t0 // 128) + tile
                        wcol = w_sb[:, gtile, e:e + 1]
                        dst = y_nat[:, tile, nb2 * 512:(nb2 + 1) * 512]
                        if first:
                            last = v.tensor_scalar_mul(dst, pd[tile % 2][:, :], wcol)
                        else:
                            last = v.scalar_tensor_tensor(
                                dst, pd[tile % 2][:, :], wcol, dst,
                                ALU.mult, ALU.add)
                        last.then_inc(sev, 1)
                ctx.inc(sev, NTT)


def emit_ffn1_final(ctx, tb, t0, TB, y_nat, xin_d, zpark_d, xout_d):
    nc = ctx.nc
    ident = ctx.const["ident"][:, :]
    NTT = TB // 128
    with ctx.scoped():
        xb = ctx.sb("f1_xb", [128, NCH, 128], F32)
        zb = ctx.sb("f1_zb", [128, NCH, 128], BF16)
        ob = ctx.sb("f1_ob", [128, NCH, 128], F32)
        pt = [ctx.ps("f1_pt", [128, 128], BF16) for _ in range(2)]
        sxl = ctx.sem("f1_fx")
        stt_ = ctx.sem("f1_ft")
        sev = ctx.sem("f1_fe")
        sst = ctx.sem("f1_fs")
        for tile in range(NTT):
            tt = t0 + tile * 128
            b_xl, b_tt, b_ev, b_st = (ctx.cnt(sxl), ctx.cnt(stt_),
                                      ctx.cnt(sev), ctx.cnt(sst))
            with nc.Block() as blk:
                @blk.gpsimd
                def _(g):
                    g.dma_start(out=xb[:, :, :],
                                in_=xin_d.rearrange("(c p) t -> p c t", p=128)
                                [:, :, tt:tt + 128]).then_inc(sxl, 16)
                    g.dma_start(out=zb[:, :, :],
                                in_=zpark_d[:, :, tt:tt + 128]).then_inc(sxl, 16)
                ctx.inc(sxl, 32)

                @blk.tensor
                def _(t):
                    for ch in range(NCH):
                        if ch >= 2:
                            t.wait_ge(sev, b_ev + ch - 1)
                        t.transpose(pt[ch % 2][:, :],
                                    y_nat[:, tile, ch * 128:(ch + 1) * 128],
                                    ident).then_inc(stt_, 1)
                ctx.inc(stt_, NCH)

                @blk.vector
                def _(v):
                    sv = SerialEng(ctx, v, ctx.sem("f1_fsv"))
                    v.wait_ge(sxl, b_xl + 32)
                    for ch in range(NCH):
                        v.wait_ge(stt_, b_tt + ch + 1)
                        sv.scalar_tensor_tensor(xb[:, ch, :], pt[ch % 2][:, :], 0.5,
                                                xb[:, ch, :], ALU.mult, ALU.add)
                        sv.scalar_tensor_tensor(xb[:, ch, :], zb[:, ch, :], 0.5,
                                                xb[:, ch, :], ALU.mult, ALU.add)
                        sv.milestone(sev, 1)
                ctx.inc(sev, NCH)

                @blk.sync
                def _(sy):
                    sy.wait_ge(sev, b_ev + NCH)
                    sy.dma_start(out=xout_d.rearrange("(c p) t -> p c t", p=128)
                                 [:, :, tt:tt + 128],
                                 in_=xb[:, :, :]).then_inc(sst, 16)
                    sy.wait_ge(sst, ctx.inc(sst, 16))


def emit_ffn1(ctx, xin_d, xout_d, zpark_d):
    nc = ctx.nc
    dram = ctx.dram
    onesrow = ctx.const["ones_row"]
    onescol = ctx.const["ones_col"]
    TB = 1024
    NTB = T // TB

    with ctx.scoped():
        w_sb = ctx.sb("f1_w", [128, NT, 8], F32)
        gcol = ctx.sb("f1_g", [128, NCH], F32)
        bcol = ctx.sb("f1_b", [128, NCH], F32)
        upb = ctx.sb("f1_upb", [128, 8, 32], F32)
        s0b = ctx.sb("f1_s0b", [128, 2, 32], F32)
        l0g = ctx.sb("f1_l0g", [128, 2, 32], F32)
        l0b = ctx.sb("f1_l0b", [128, 2, 32], F32)
        s1ab = ctx.sb("f1_s1ab", [128, 2, 16], F32)
        s1bb = ctx.sb("f1_s1bb", [128, 2, 32], F32)
        l1g = ctx.sb("f1_l1g", [128, 2, 32], F32)
        l1b = ctx.sb("f1_l1b", [128, 2, 32], F32)
        s2b = ctx.sb("f1_s2b", [128, 2, 32], F32)
        sld = ctx.sem("f1_ld")

        with nc.Block() as blk:
            @blk.gpsimd
            def _(g):
                n = 0
                for nm, dst in [("f1_ln_g", gcol[:, :]), ("f1_ln_b", bcol[:, :]),
                                ("f1_up_b", upb[:, :, :]),
                                ("f1_s0_b", s0b[:, :, :]), ("f1_ln0_g", l0g[:, :, :]),
                                ("f1_ln0_b", l0b[:, :, :]), ("f1_s1a_b", s1ab[:, :, :]),
                                ("f1_s1b_b", s1bb[:, :, :]), ("f1_ln1_g", l1g[:, :, :]),
                                ("f1_ln1_b", l1b[:, :, :]), ("f1_s2_b", s2b[:, :, :])]:
                    g.dma_start(out=dst, in_=dram[nm]).then_inc(sld, 16)
                    n += 16
                g.wait_ge(sld, ctx.inc(sld, n))

        vrow = ctx.sb("f1_vrow", [1, T], F32)
        with ctx.scoped():
            z = ctx.sb("f1_z", [128, NCH, T], BF16)
            sx = ctx.sem("f1_sx")
            with ctx.scoped():
                xT = ctx.sb("f1_xT", [128, NCH, T], F32)
                with nc.Block() as blk:
                    @blk.sync
                    def _(sy):
                        sy.dma_start(out=xT[:, :, :],
                                     in_=xin_d.rearrange("(c p) t -> p c t", p=128)) \
                            .then_inc(sx, 16)
                        sy.wait_ge(sx, ctx.inc(sx, 16))
                emit_ln(ctx, xT, z, T, var_row=vrow)
            with nc.Block() as blk:
                @blk.sync
                def _(sy):
                    sy.dma_start(out=zpark_d[:, :, :], in_=z[:, :, :]).then_inc(sx, 16)
                    sy.wait_ge(sx, ctx.inc(sx, 16))
            nx = ctx.stacks[-2].enter_context(
                nc.sbuf_tensor(f"f1_nx_{ctx.uid()}", [128, NCH, T], BF16,
                               side="right"))
            ctx.nx_dbg = nx

            srow = ctx.sb("f1_srow", [1, T], F32)
            sB = ctx.sb("f1_sB", [128, T], F32)
            tmp = ctx.sb("f1_tmp", [128, 512], F32)
            s1_ = ctx.sem("f1_s1")
            sm_ = ctx.sem("f1_sm")
            sv_ = ctx.sem("f1_sv")
            with ctx.scoped():
                bmat = [ctx.ps("f1_bm", [128, 512], F32) for _ in range(2)]
                b_s1, b_sv = ctx.cnt(s1_), ctx.cnt(sv_)
                with nc.Block() as blk:
                    @blk.scalar
                    def _(s):
                        s.activation(srow[:, :], vrow[:, :], AF.Sqrt,
                                     bias=np.float32(EPS_LN),
                                     scale=np.float32(1.0)).then_inc(s1_, 1)
                        s.wait_ge(sv_, b_sv + 1)
                        s.activation(srow[:, :], srow[:, :], AF.Sqrt,
                                     bias=np.float32(EPS_LN),
                                     scale=np.float32(1.0)).then_inc(s1_, 1)
                    ctx.inc(s1_, 2)

                    @blk.vector
                    def _(v):
                        sv = SerialEng(ctx, v, ctx.sem("f1_svv"))
                        v.wait_ge(s1_, b_s1 + 1)
                        sv.reciprocal(srow[:, :], srow[:, :])
                        sv.tensor_mul(srow[:, :], srow[:, :], srow[:, :])
                        sv.tensor_mul(srow[:, :], srow[:, :], vrow[:, :])
                        sv.milestone(sv_, 1)
                        v.wait_ge(s1_, b_s1 + 2)
                        sv.reciprocal(srow[:, :], srow[:, :])
                        sv.milestone(sv_, 1)
                    ctx.inc(sv_, 2)

                b_sm2, b_sv2 = ctx.cnt(sm_), ctx.cnt(sv_)
                with nc.Block() as blk:
                    @blk.tensor
                    def _(t):
                        for ib in range(4):
                            if ib >= 2:
                                t.wait_ge(sv_, b_sv2 + ib - 1)
                            t.matmul(bmat[ib % 2][:, :], onesrow[:, :],
                                     srow[:, ib * 512:(ib + 1) * 512],
                                     start=True, stop=True).then_inc(sm_, 1)
                    ctx.inc(sm_, 4)

                    @blk.vector
                    def _(v):
                        for ib in range(4):
                            v.wait_ge(sm_, b_sm2 + ib + 1)
                            v.tensor_copy(sB[:, ib * 512:(ib + 1) * 512],
                                          bmat[ib % 2][:, :]).then_inc(sv_, 1)
                    ctx.inc(sv_, 4)

                with nc.Block() as blk:
                    @blk.vector
                    def _(v):
                        for ch in range(NCH):
                            for ib in range(4):
                                sl = slice(ib * 512, (ib + 1) * 512)
                                v.tensor_mul(tmp[:, :], z[:, ch, sl], sB[:, sl])
                                v.tensor_scalar(nx[:, ch, sl], tmp[:, :],
                                                gcol[:, ch:ch + 1],
                                                bcol[:, ch:ch + 1],
                                                ALU.mult, ALU.add)

        with ctx.scoped():
            rw = ctx.sb("f1_rw", [128, NCH, 8], BF16)
            lg = ctx.sb("f1_lg", [128, 8], F32)
            e_ = ctx.sb("f1_e", [128, 8], F32)
            e2_ = ctx.sb("f1_e2", [128, 8], F32)
            msk = ctx.sb("f1_msk", [128, 8], F32)
            ncol = ctx.sb("f1_ncol", [128, 1], F32)
            mcol = ctx.sb("f1_mcol", [128, 1], F32)
            prt = [ctx.ps("f1_prt", [128, 8], F32) for _ in range(2)]
            srt = ctx.sem("f1_rt")
            sva = ctx.sem("f1_va")
            ssb = ctx.sem("f1_sb")
            sre_ = ctx.sem("f1_re")
            b_rt, b_va, b_sb, b_re = (ctx.cnt(srt), ctx.cnt(sva),
                                      ctx.cnt(ssb), ctx.cnt(sre_))
            with nc.Block() as blk:
                @blk.gpsimd
                def _(g):
                    g.dma_start(out=rw[:, :, :],
                                in_=dram["f1_router_w"].transpose([1, 0, 2])) \
                        .then_inc(sld, 16)
                ctx.inc(sld, 16)

                @blk.tensor
                def _(t):
                    t.wait_ge(sld, ctx.cnt(sld))
                    for it in range(NT):
                        if it >= 2:
                            t.wait_ge(sre_, b_re + it - 1)
                        for ch in range(NCH):
                            last = t.matmul(prt[it % 2][:, :],
                                            nx[:, ch, it * 128:(it + 1) * 128],
                                            rw[:, ch, :],
                                            start=(ch == 0), stop=(ch == NCH - 1))
                        last.then_inc(srt, 1)
                ctx.inc(srt, NT)

                @blk.vector
                def _(v):
                    sv = SerialEng(ctx, v, ctx.sem("f1_vv"))
                    for it in range(NT):
                        v.wait_ge(srt, b_rt + it + 1)
                        sv.tensor_copy(lg[:, :], prt[it % 2][:, :])
                        sv.tensor_reduce(ncol[:, :], lg[:, :], AX.X,
                                         ALU.max, negate=True)
                        sv.milestone(sva, 1)
                        v.wait_ge(ssb, b_sb + it + 1)
                        sv.tensor_reduce(mcol[:, :], e_[:, :], AX.X, ALU.max)
                        sv.tensor_single_scalar(e2_[:, :], e_[:, :], mcol[:, :],
                                                ALU.is_lt)
                        sv.tensor_mul(e2_[:, :], e2_[:, :], e_[:, :])
                        sv.tensor_reduce(mcol[:, :], e2_[:, :], AX.X, ALU.max)
                        sv.tensor_single_scalar(msk[:, :], e_[:, :], mcol[:, :],
                                                ALU.is_ge)
                        sv.tensor_mul(msk[:, :], msk[:, :], e_[:, :])
                        sv.tensor_reduce(mcol[:, :], msk[:, :], AX.X, ALU.add)
                        sv.reciprocal(mcol[:, :], mcol[:, :])
                        sv.tensor_scalar_mul(w_sb[:, it, :], msk[:, :],
                                             mcol[:, :])
                        sv.milestone(sre_, 1)
                ctx.inc(sva, NT)
                ctx.inc(sre_, NT)

                @blk.scalar
                def _(s):
                    for it in range(NT):
                        s.wait_ge(sva, b_va + it + 1)
                        s.activation(e_[:, :], lg[:, :], AF.Exp,
                                     bias=ncol[:, :], scale=onescol[:, :]) \
                            .then_inc(ssb, 1)
                ctx.inc(ssb, NT)

        if getattr(ctx, "dbg", None) is not None:
            sdbg = ctx.sem("f1_dbg")
            with nc.Block() as blk:
                @blk.sync
                def _(sy):
                    sy.dma_start(out=ctx.dbg["nx"], in_=nx[:, :, :]).then_inc(sdbg, 16)
                    sy.dma_start(out=ctx.dbg["w"], in_=w_sb[:, :, :]).then_inc(sdbg, 16)
                    sy.wait_ge(sdbg, ctx.inc(sdbg, 32))

        y_nat = ctx.sb("f1_y", [128, 8, 1024], BF16)
        for tb in range(NTB):
            t0 = tb * TB
            for e in range(8):
                etype = ("s0" if e in (0, 4) else "s1" if e in (1, 5)
                         else "s2" if e in (2, 6) else "id")
                i = 0 if e < 4 else 1
                spec_args = (s0b, l0g, l0b, s1ab, s1bb, l1g, l1b, s2b)
                dbg_h1 = (getattr(ctx, "dbg", None) is not None and tb == 0
                          and e == 0)
                if etype in ("id", "s1"):
                    with ctx.scoped():
                        hbuf1 = ctx.sb("f1_h1", [128, 32, TB], BF16)
                        emit_ffn1_up(ctx, e, t0, TB, nx, upb, hbuf1)
                        if etype == "s1":
                            emit_ffn1_spec(ctx, etype, i, hbuf1, None, TB,
                                           *spec_args, phase="dense")
                            emit_ffn1_spec(ctx, etype, i, None, hbuf1, TB,
                                           *spec_args, phase="post")
                        emit_ffn1_down(ctx, e, tb, t0, TB, hbuf1, w_sb,
                                       y_nat, first=(e == 0))
                else:
                    with ctx.scoped():
                        hbuf2 = ctx.sb("f1_h2", [128, 32, TB], BF16)
                        with ctx.scoped():
                            hbuf1 = ctx.sb("f1_h1", [128, 32, TB], BF16)
                            emit_ffn1_up(ctx, e, t0, TB, nx, upb, hbuf1)
                            if dbg_h1:
                                sdbg = ctx.sem("f1_dbg")
                                with nc.Block() as blk:
                                    @blk.sync
                                    def _(sy):
                                        sy.dma_start(out=ctx.dbg["h1e0"],
                                                     in_=hbuf1[:, :, :]) \
                                            .then_inc(sdbg, 16)
                                        sy.wait_ge(sdbg, ctx.inc(sdbg, 16))
                            emit_ffn1_spec(ctx, etype, i, hbuf1, hbuf2, TB,
                                           *spec_args, phase="dense")
                            if dbg_h1:
                                sdbg = ctx.sem("f1_dbg")
                                with nc.Block() as blk:
                                    @blk.sync
                                    def _(sy):
                                        sy.dma_start(out=ctx.dbg["pre0"],
                                                     in_=hbuf2[:, :, :]) \
                                            .then_inc(sdbg, 16)
                                        sy.wait_ge(sdbg, ctx.inc(sdbg, 16))
                        ctx._lnorm_dbg = dbg_h1
                        emit_ffn1_spec(ctx, etype, i, None, hbuf2, TB,
                                       *spec_args, phase="post")
                        ctx._lnorm_dbg = False
                        if dbg_h1:
                            sdbg = ctx.sem("f1_dbg")
                            with nc.Block() as blk:
                                @blk.sync
                                def _(sy):
                                    sy.dma_start(out=ctx.dbg["s0e0"],
                                                 in_=hbuf2[:, :, :]) \
                                        .then_inc(sdbg, 16)
                                    sy.wait_ge(sdbg, ctx.inc(sdbg, 16))
                        emit_ffn1_down(ctx, e, tb, t0, TB, hbuf2, w_sb,
                                       y_nat, first=(e == 0))
                        if dbg_h1:
                            sdbg = ctx.sem("f1_dbg")
                            with nc.Block() as blk:
                                @blk.sync
                                def _(sy):
                                    sy.dma_start(out=ctx.dbg["ye0"],
                                                 in_=y_nat[:, :, :]) \
                                        .then_inc(sdbg, 16)
                                    sy.wait_ge(sdbg, ctx.inc(sdbg, 16))
            if getattr(ctx, "dbg", None) is not None and tb == 0:
                sdbg = ctx.sem("f1_dbg")
                with nc.Block() as blk:
                    @blk.sync
                    def _(sy):
                        sy.dma_start(out=ctx.dbg["y0"], in_=y_nat[:, :, :]) \
                            .then_inc(sdbg, 16)
                        sy.wait_ge(sdbg, ctx.inc(sdbg, 16))
            emit_ffn1_final(ctx, tb, t0, TB, y_nat, xin_d, zpark_d, xout_d)


def emit_ffn1_up(ctx, e, t0, TB, nx, upb, hbuf1):
    nc = ctx.nc
    dram = ctx.dram
    onescol = ctx.const["ones_col"]
    with ctx.scoped():
        upw = [ctx.sb("f1_upw", [128, NCH, 128], BF16) for _ in range(2)]
        pu = [ctx.ps("f1_pu", [128, 512], F32) for _ in range(2)]
        swl = ctx.sem("f1_wl")
        smm = ctx.sem("f1_mm")
        sev = ctx.sem("f1_ev")
        b_wl, b_mm, b_ev = ctx.cnt(swl), ctx.cnt(smm), ctx.cnt(sev)
        with nc.Block() as blk:
            @blk.gpsimd
            def _(g):
                for m in range(32):
                    if m >= 2:
                        g.wait_ge(smm, b_mm + 2 * (m - 1))
                    g.dma_start(out=upw[m % 2][:, :, :],
                                in_=dram["f1_up_W"][e, m, :, :, :]
                                .transpose([1, 0, 2])).then_inc(swl, 16)
            ctx.inc(swl, 32 * 16)

            @blk.tensor
            def _(t):
                for m in range(32):
                    t.wait_ge(swl, b_wl + 16 * (m + 1))
                    for nb in range(2):
                        k = 2 * m + nb
                        if k >= 2:
                            t.wait_ge(sev, b_ev + k - 1)
                        for ch in range(NCH):
                            last = t.matmul(
                                pu[k % 2][:, :], upw[m % 2][:, ch, :],
                                nx[:, ch, t0 + nb * 512:t0 + (nb + 1) * 512],
                                start=(ch == 0), stop=(ch == NCH - 1))
                        last.then_inc(smm, 1)
            ctx.inc(smm, 64)

            @blk.scalar
            def _(s):
                for m in range(32):
                    for nb in range(2):
                        k = 2 * m + nb
                        s.wait_ge(smm, b_mm + k + 1)
                        s.activation(hbuf1[:, m, nb * 512:(nb + 1) * 512],
                                     pu[k % 2][:, :], AF.Gelu,
                                     bias=upb[:, e, m:m + 1],
                                     scale=onescol[:, :]).then_inc(sev, 1)
            ctx.inc(sev, 64)


def prep_ffn1_host(ins):
    def cols(a, inner):
        a = np.asarray(a, np.float32)
        k = a.shape[0]
        return np.ascontiguousarray(a.reshape(k, a.shape[1] // 128, 128)
                                    .transpose(2, 0, 1))
    d = {}
    d["f1_ln_g"] = np.ascontiguousarray(
        np.asarray(ins["L1_ffn_ln_g"], np.float32).reshape(8, 128).T)
    d["f1_ln_b"] = np.ascontiguousarray(
        np.asarray(ins["L1_ffn_ln_b"], np.float32).reshape(8, 128).T)
    d["f1_router_w"] = np.ascontiguousarray(
        np.asarray(ins["L1_ffn_router_w"], np.float32).reshape(8, 128, 8))
    d["f1_up_W"] = np.ascontiguousarray(
        np.asarray(ins["L1_ffn_up_W"], np.float32)
        .reshape(8, 8, 128, 32, 128).transpose(0, 3, 1, 2, 4))
    d["f1_up_b"] = cols(ins["L1_ffn_up_b"], 128)
    d["f1_down_W"] = np.ascontiguousarray(
        np.asarray(ins["L1_ffn_down_W"], np.float32).reshape(8, 32, 128, 1024))
    d["f1_down_b"] = np.ascontiguousarray(
        np.asarray(ins["L1_ffn_down_b"], np.float32).reshape(1, 8, 1024))
    d["f1_s0_W"] = np.ascontiguousarray(
        np.asarray(ins["L1_ffn_spec0_W"], np.float32)
        .reshape(2, 32, 128, 32, 128).transpose(0, 3, 1, 2, 4))
    d["f1_s0_b"] = cols(ins["L1_ffn_spec0_b"], 128)
    d["f1_ln0_g"] = cols(ins["L1_ffn_ln0_g"], 128)
    d["f1_ln0_b"] = cols(ins["L1_ffn_ln0_b"], 128)
    d["f1_s1a_W"] = np.ascontiguousarray(
        np.asarray(ins["L1_ffn_spec1a_W"], np.float32)
        .reshape(2, 32, 128, 16, 128).transpose(0, 3, 1, 2, 4))
    d["f1_s1a_b"] = cols(ins["L1_ffn_spec1a_b"], 128)
    d["f1_s1b_W"] = np.ascontiguousarray(
        np.asarray(ins["L1_ffn_spec1b_W"], np.float32)
        .reshape(2, 16, 128, 32, 128).transpose(0, 3, 1, 2, 4))
    d["f1_s1b_b"] = cols(ins["L1_ffn_spec1b_b"], 128)
    d["f1_ln1_g"] = cols(ins["L1_ffn_ln1_g"], 128)
    d["f1_ln1_b"] = cols(ins["L1_ffn_ln1_b"], 128)
    d["f1_s2_W"] = np.ascontiguousarray(
        np.asarray(ins["L1_ffn_spec2_W"], np.float32)
        .reshape(2, 32, 128, 32, 128).transpose(0, 3, 1, 2, 4))
    d["f1_s2_b"] = cols(ins["L1_ffn_spec2_b"], 128)
    return d


FFN1_DRAM_SPECS = [
    ("f1_ln_g", [128, 8], F32), ("f1_ln_b", [128, 8], F32),
    ("f1_router_w", [8, 128, 8], F32),
    ("f1_up_W", [8, 32, 8, 128, 128], F32), ("f1_up_b", [128, 8, 32], F32),
    ("f1_down_W", [8, 32, 128, 1024], F32), ("f1_down_b", [1, 8, 1024], F32),
    ("f1_s0_W", [2, 32, 32, 128, 128], F32), ("f1_s0_b", [128, 2, 32], F32),
    ("f1_ln0_g", [128, 2, 32], F32), ("f1_ln0_b", [128, 2, 32], F32),
    ("f1_s1a_W", [2, 16, 32, 128, 128], F32), ("f1_s1a_b", [128, 2, 16], F32),
    ("f1_s1b_W", [2, 32, 16, 128, 128], F32), ("f1_s1b_b", [128, 2, 32], F32),
    ("f1_ln1_g", [128, 2, 32], F32), ("f1_ln1_b", [128, 2, 32], F32),
    ("f1_s2_W", [2, 32, 32, 128, 128], F32), ("f1_s2_b", [128, 2, 32], F32),
]


def build_ffn1_launch(debug=False):
    nc = bass.Bass("TRN2", target_bir_lowering=False, detect_race_conditions=False)
    dram = {}
    dram["x_in"] = nc.dram_tensor("x_in", [D, T], F32, kind="ExternalInput").ap()
    dram["c_ident"] = nc.dram_tensor("c_ident", [128, 128], BF16, kind="ExternalInput").ap()
    dram["c_ident2"] = nc.dram_tensor("c_ident2", [128, 64], BF16, kind="ExternalInput").ap()
    for nm, shape, dt in FFN1_DRAM_SPECS:
        dram[nm] = nc.dram_tensor(nm, shape, dt, kind="ExternalInput").ap()
    xout = nc.dram_tensor("x_out", [D, T], F32, kind="ExternalOutput").ap()
    zpark = nc.dram_tensor("zpark", [128, NCH, T], BF16).ap()
    ctx = Ctx(nc, dram)
    if debug:
        ctx.dbg = {
            "nx": nc.dram_tensor("d_nx", [128, NCH, T], BF16, kind="ExternalOutput").ap(),
            "w": nc.dram_tensor("d_w", [128, NT, 8], F32, kind="ExternalOutput").ap(),
            "y0": nc.dram_tensor("d_y0", [128, 8, 1024], BF16, kind="ExternalOutput").ap(),
            "h1e0": nc.dram_tensor("d_h1e0", [128, 32, 1024], BF16, kind="ExternalOutput").ap(),
            "s0e0": nc.dram_tensor("d_s0e0", [128, 32, 1024], BF16, kind="ExternalOutput").ap(),
            "pre0": nc.dram_tensor("d_pre0", [128, 32, 1024], BF16, kind="ExternalOutput").ap(),
            "rows": nc.dram_tensor("d_rows", [4, 1024], F32, kind="ExternalOutput").ap(),
            "ye0": nc.dram_tensor("d_ye0", [128, 8, 1024], BF16, kind="ExternalOutput").ap(),
        }
    with ctx.root:
        emit_const_setup(ctx)
        emit_ffn1(ctx, dram["x_in"], xout, zpark)
    return nc


def emit_att1a(ctx, xin_d, sqT_d, numden_d, state_d, relp_d):
    nc = ctx.nc
    dram = ctx.dram
    ident = ctx.const["ident"][:, :]
    with ctx.scoped():
        with ctx.scoped():
            z = ctx.sb("a1_z", [128, NCH, T], BF16, side="right")
            with ctx.scoped():
                xT = ctx.sb("a1_xT", [128, NCH, T], F32)
                sx = ctx.sem("a1_sx")
                with nc.Block() as blk:
                    @blk.sync
                    def _(sy):
                        sy.dma_start(out=xT[:, :, :],
                                     in_=xin_d.rearrange("(c p) t -> p c t", p=128)) \
                            .then_inc(sx, 16)
                        sy.wait_ge(sx, ctx.inc(sx, 16))
                emit_ln(ctx, xT, z, T)
            sqT = ctx.stacks[-2].enter_context(
                nc.sbuf_tensor(f"a1_sq_{ctx.uid()}", [128, NCH, T], BF16))
            skT = ctx.stacks[-2].enter_context(
                nc.sbuf_tensor(f"a1_sk_{ctx.uid()}", [128, NCH, T], BF16))
            vau = ctx.stacks[-2].enter_context(
                nc.sbuf_tensor(f"a1_v_{ctx.uid()}", [128, NT, 16, 65], BF16))

            def proj_tr(wname, dst):
                with ctx.scoped():
                    ww = [ctx.sb("a1_w", [128, NCH, 128], BF16) for _ in range(2)]
                    pq = [ctx.ps("a1_pq", [128, 512], F32) for _ in range(2)]
                    tmp1 = ctx.sb("a1_t1", [128, 512], F32)
                    tmp2 = ctx.sb("a1_t2", [128, 512], F32)
                    swl = ctx.sem("a1_wl")
                    smm = ctx.sem("a1_mm")
                    sev = ctx.sem("a1_ev")
                    sac = ctx.sem("a1_ac")
                    b_wl, b_mm, b_ev, b_ac = (ctx.cnt(swl), ctx.cnt(smm),
                                              ctx.cnt(sev), ctx.cnt(sac))
                    with nc.Block() as blk:
                        @blk.gpsimd
                        def _(g):
                            for m in range(NCH):
                                if m >= 2:
                                    g.wait_ge(smm, b_mm + 4 * (m - 1))
                                g.dma_start(out=ww[m % 2][:, :, :],
                                            in_=dram[wname][m, :, :, :]) \
                                    .then_inc(swl, 16)
                        ctx.inc(swl, NCH * 16)

                        @blk.tensor
                        def _(t):
                            for m in range(NCH):
                                t.wait_ge(swl, b_wl + 16 * (m + 1))
                                for tb in range(4):
                                    k = 4 * m + tb
                                    if k >= 2:
                                        t.wait_ge(sev, b_ev + 2 * (k - 1))
                                    for ch in range(NCH):
                                        last = t.matmul(
                                            pq[k % 2][:, :], ww[m % 2][:, ch, :],
                                            z[:, ch, tb * 512:(tb + 1) * 512],
                                            start=(ch == 0), stop=(ch == NCH - 1))
                                    last.then_inc(smm, 1)
                        ctx.inc(smm, NCH * 4)

                        @blk.vector
                        def _(v):
                            for m in range(NCH):
                                for tb in range(4):
                                    k = 4 * m + tb
                                    v.wait_ge(smm, b_mm + k + 1)
                                    v.tensor_scalar_min(tmp1[:, :], pq[k % 2][:, :],
                                                        0.0)
                                    last = v.tensor_scalar_max(
                                        tmp2[:, :], pq[k % 2][:, :], 0.0)
                                    last.then_inc(sev, 1)
                                    v.wait_ge(sac, b_ac + k + 1)
                                    v.tensor_add(
                                        dst[:, m, tb * 512:(tb + 1) * 512],
                                        tmp1[:, :], tmp2[:, :]).then_inc(sev, 1)
                        ctx.inc(sev, NCH * 4 * 2)

                        @blk.scalar
                        def _(s):
                            for m in range(NCH):
                                for tb in range(4):
                                    k = 4 * m + tb
                                    s.wait_ge(sev, b_ev + 2 * k + 1)
                                    s.activation(tmp1[:, :], tmp1[:, :], AF.Exp) \
                                        .then_inc(sac, 1)
                        ctx.inc(sac, NCH * 4)

            proj_tr("w_qT", sqT)
            proj_tr("w_kT", skT)

            with ctx.scoped():
                wv = ctx.sb("a1_wv", [128, NCH, 1024], BF16)
                pv = [ctx.ps("a1_pv", [128, 512], F32) for _ in range(4)]
                swl = ctx.sem("a1_wl")
                smm = ctx.sem("a1_mm")
                sev = ctx.sem("a1_ev")
                b_wl, b_mm, b_ev = ctx.cnt(swl), ctx.cnt(smm), ctx.cnt(sev)
                with nc.Block() as blk:
                    @blk.gpsimd
                    def _(g):
                        g.dma_start(out=wv[:, :, :],
                                    in_=dram["w_vT"].transpose([1, 0, 2])) \
                            .then_inc(swl, 16)
                    ctx.inc(swl, 16)

                    @blk.tensor
                    def _(t):
                        t.wait_ge(swl, b_wl + 16)
                        for it in range(NT):
                            if it >= 2:
                                t.wait_ge(sev, b_ev + it - 1)
                            for nb in range(2):
                                for ch in range(NCH):
                                    last = t.matmul(
                                        pv[2 * (it % 2) + nb][:, :],
                                        z[:, ch, it * 128:(it + 1) * 128],
                                        wv[:, ch, nb * 512:(nb + 1) * 512],
                                        start=(ch == 0), stop=(ch == NCH - 1))
                                last.then_inc(smm, 1)
                    ctx.inc(smm, NT * 2)

                    @blk.vector
                    def _(v):
                        v.memset(vau[:, :, :, 64:65], 1.0)
                        for it in range(NT):
                            for nb in range(2):
                                v.wait_ge(smm, b_mm + 2 * it + nb + 1)
                                last = v.tensor_copy(
                                    vau[:, it, nb * 8:(nb + 1) * 8, 0:64],
                                    pv[2 * (it % 2) + nb][:, :]
                                    .rearrange("p (h e) -> p h e", e=64))
                            last.then_inc(sev, 1)
                    ctx.inc(sev, NT)

        sst = ctx.sem("a1_st")
        with nc.Block() as blk:
            @blk.sync
            def _(sy):
                sy.dma_start(out=sqT_d.rearrange("(hp p) t -> p hp t", p=128),
                             in_=sqT[:, :, :]).then_inc(sst, 16)
                sy.wait_ge(sst, ctx.inc(sst, 16))

        with ctx.scoped():
            mnorm = ctx.sb("a1_mn", [128, 4, 16], BF16)
            rcol = ctx.sb("a1_rc", [4, 1], F32)
            prel = ctx.ps("a1_prel", [4, 512], F32)
            swl = ctx.sem("a1_wl")
            smm = ctx.sem("a1_mm")
            sev = ctx.sem("a1_ev")
            b_wl, b_mm, b_ev = ctx.cnt(swl), ctx.cnt(smm), ctx.cnt(sev)
            b_st0 = ctx.cnt(sst)
            with nc.Block() as blk:
                @blk.gpsimd
                def _(g):
                    g.dma_start(out=mnorm[:, :, :], in_=dram["w_mnorm"]) \
                        .then_inc(swl, 16)
                ctx.inc(swl, 16)

                @blk.tensor
                def _(t):
                    t.wait_ge(swl, b_wl + 16)
                    for h in range(16):
                        if h >= 1:
                            t.wait_ge(sev, b_ev + h)
                        hb = (h % 2) * 64
                        for tb in range(4):
                            last = t.matmul(prel[:, :],
                                            mnorm[hb:hb + 64, :, h],
                                            sqT[hb:hb + 64, h // 2,
                                                tb * 512:(tb + 1) * 512],
                                            start=(tb == 0), stop=(tb == 3))
                        last.then_inc(smm, 1)
                ctx.inc(smm, 16)

                @blk.vector
                def _(v):
                    sv = SerialEng(ctx, v, ctx.sem("a1_sv"))
                    for h in range(16):
                        v.wait_ge(smm, b_mm + h + 1)
                        if h >= 1:
                            v.wait_ge(sst, b_st0 + 16 * h)
                        sv.tensor_reduce(rcol[:, :], prel[:, :], AX.X, ALU.add)
                        sv.milestone(sev, 1)
                ctx.inc(sev, 16)

                @blk.sync
                def _(sy):
                    for h in range(16):
                        sy.wait_ge(sev, b_ev + h + 1)
                        sy.dma_start(out=relp_d[h:h + 1, :],
                                     in_=rcol[:, :]).then_inc(sst, 16)
                    sy.wait_ge(sst, b_st0 + 16 * 16)
            ctx.inc(sst, 16 * 16)

        with ctx.scoped():
            trimask = ctx.sb("a1_tri", [128, 128], BF16)
            sknat = [ctx.sb("a1_skn", [128, 64], BF16) for _ in range(2)]
            At = [ctx.sb("a1_At", [128, 128], BF16) for _ in range(2)]
            state_bf = ctx.sb("a1_stb", [128, 65], BF16)
            state_f = ctx.sb("a1_stf", [64, 65], F32)
            ndsb = ctx.sb("a1_nd", [65, T], F32)
            swl = ctx.sem("a1_wl")
            with nc.Block() as blk:
                @blk.gpsimd
                def _(g):
                    g.dma_start(out=trimask[:, :], in_=dram["c_tri"][:, :]) \
                        .then_inc(swl, 16)
                    g.wait_ge(swl, ctx.inc(swl, 16))

            for h in range(16):
                hb = (h % 2) * 64
                hp = h // 2
                with ctx.scoped():
                    pT = [ctx.ps("a1_pT", [128, 64], BF16) for _ in range(2)]
                    pA = [ctx.ps("a1_pA", [128, 128], F32) for _ in range(2)]
                    pN = [ctx.ps("a1_pN", [65, 128], F32) for _ in range(2)]
                    pS = ctx.ps("a1_pS", [64, 65], F32)
                    stm = ctx.sem("a1_c_tm")
                    ste = ctx.sem("a1_c_te")
                    sam = ctx.sem("a1_c_am")
                    sae = ctx.sem("a1_c_ae")
                    snm = ctx.sem("a1_c_nm")
                    sne = ctx.sem("a1_c_ne")
                    ssu = ctx.sem("a1_c_su")
                    ssc = ctx.sem("a1_c_sc")
                    b = {s: ctx.cnt(s) for s in (stm, ste, sam, sae, snm, sne,
                                                 ssu, ssc)}
                    with nc.Block() as blk:
                        @blk.tensor
                        def _(t):
                            for c in range(NT):
                                if c >= 2:
                                    t.wait_ge(ste, b[ste] + c - 1)
                                t.transpose(pT[c % 2][:, :],
                                            skT[hb:hb + 64, hp,
                                                c * 128:(c + 1) * 128],
                                            ctx.const["ident2"][hb:hb + 64, :]) \
                                    .then_inc(stm, 1)
                                if c >= 2:
                                    t.wait_ge(sae, b[sae] + c - 1)
                                t.matmul(pA[c % 2][:, :],
                                         skT[hb:hb + 64, hp, c * 128:(c + 1) * 128],
                                         sqT[hb:hb + 64, hp, c * 128:(c + 1) * 128],
                                         start=True, stop=True).then_inc(sam, 1)
                                t.wait_ge(sae, b[sae] + c + 1)
                                if c >= 2:
                                    t.wait_ge(sne, b[sne] + c - 1)
                                if c == 0:
                                    t.matmul(pN[0][:, :], vau[:, 0, h, :],
                                             At[0][:, :], start=True,
                                             stop=True).then_inc(snm, 1)
                                else:
                                    t.wait_ge(ssc, b[ssc] + c)
                                    t.matmul(pN[c % 2][:, :], vau[:, c, h, :],
                                             At[c % 2][:, :], start=True, stop=False)
                                    t.matmul(pN[c % 2][0:65, :],
                                             state_bf[hb:hb + 64, :],
                                             sqT[hb:hb + 64, hp,
                                                 c * 128:(c + 1) * 128],
                                             start=False, stop=True).then_inc(snm, 1)
                                t.wait_ge(ste, b[ste] + c + 1)
                                t.matmul(pS[:, :], sknat[c % 2][:, :],
                                         vau[:, c, h, :],
                                         start=(c == 0), stop=(c == NT - 1)) \
                                    .then_inc(ssu, 1)
                        ctx.inc(stm, NT)
                        ctx.inc(sam, NT)
                        ctx.inc(snm, NT)
                        ctx.inc(ssu, NT)

                        @blk.vector
                        def _(v):
                            for c in range(NT):
                                v.wait_ge(stm, b[stm] + c + 1)
                                v.tensor_copy(sknat[c % 2][:, :],
                                              pT[c % 2][:, :]).then_inc(ste, 1)
                                v.wait_ge(sam, b[sam] + c + 1)
                                v.tensor_mul(At[c % 2][:, :], pA[c % 2][:, :],
                                             trimask[:, :]).then_inc(sae, 1)
                                v.wait_ge(ssu, b[ssu] + c + 1)
                                v.tensor_copy(state_bf[hb:hb + 64, :], pS[:, :]) \
                                    .then_inc(ssc, 1)
                                v.wait_ge(snm, b[snm] + c + 1)
                                v.tensor_copy(ndsb[:, c * 128:(c + 1) * 128],
                                              pN[c % 2][:, :]).then_inc(sne, 1)
                        ctx.inc(ste, NT)
                        ctx.inc(sae, NT)
                        ctx.inc(ssc, NT)
                        ctx.inc(sne, NT)

                    with nc.Block() as blk:
                        @blk.vector
                        def _(v):
                            v.tensor_copy(state_f[:, :], pS[:, :])

                        @blk.sync
                        def _(sy):
                            pass
                    with nc.Block() as blk:
                        @blk.sync
                        def _(sy):
                            sy.dma_start(out=numden_d[h, :, :], in_=ndsb[:, :]) \
                                .then_inc(sst, 16)
                            sy.dma_start(out=state_d[h, :, :], in_=state_f[:, :]) \
                                .then_inc(sst, 16)
                            sy.wait_ge(sst, ctx.inc(sst, 32))


def prep_att1a_host(ins, core):
    d = {}
    for nm, w in [("w_qT", ins["L1_attn_w_q"]), ("w_kT", ins["L1_attn_w_k"]),
                  ("w_vT", ins["L1_attn_w_v"])]:
        wt = np.ascontiguousarray(np.asarray(w, np.float32).T)
        if nm == "w_vT":
            d[nm] = np.ascontiguousarray(wt.reshape(8, 128, 1024))
        else:
            d[nm] = np.ascontiguousarray(
                wt.reshape(8, 128, 8, 128).transpose(2, 0, 1, 3))
    mn = np.asarray(ins["L1_attn_memory_norms"], np.float32).transpose(2, 0, 1)
    d["w_mnorm"] = np.ascontiguousarray(np.concatenate([mn, mn], axis=0))
    tri = np.tril(np.ones((128, 128), np.float32)).T
    d["c_tri"] = np.ascontiguousarray(tri.astype(BF))
    return d


ATT1A_DRAM_SPECS = [
    ("w_qT", [8, 8, 128, 128], F32),
    ("w_kT", [8, 8, 128, 128], F32),
    ("w_vT", [8, 128, 1024], F32),
    ("w_mnorm", [128, 4, 16], F32),
    ("c_tri", [128, 128], BF16),
]


def build_att1a_launch():
    nc = bass.Bass("TRN2", target_bir_lowering=False, detect_race_conditions=False)
    dram = {}
    dram["x_in"] = nc.dram_tensor("x_in", [D, T], F32, kind="ExternalInput").ap()
    dram["c_ident"] = nc.dram_tensor("c_ident", [128, 128], BF16, kind="ExternalInput").ap()
    dram["c_ident2"] = nc.dram_tensor("c_ident2", [128, 64], BF16, kind="ExternalInput").ap()
    for nm, shape, dt in ATT1A_DRAM_SPECS:
        dram[nm] = nc.dram_tensor(nm, shape, dt, kind="ExternalInput").ap()
    sqT = nc.dram_tensor("sqT", [D, T], BF16, kind="ExternalOutput").ap()
    numden = nc.dram_tensor("numden", [16, 65, T], F32, kind="ExternalOutput").ap()
    state = nc.dram_tensor("state", [16, 64, 65], F32, kind="ExternalOutput").ap()
    relp = nc.dram_tensor("relp", [16, 4], F32, kind="ExternalOutput").ap()
    ctx = Ctx(nc, dram)
    with ctx.root:
        emit_const_setup(ctx)
        emit_att1a(ctx, dram["x_in"], sqT, numden, state, relp)
    return nc


def emit_att1b(ctx, xin_d, sqT_d, numden_d, states_d, relp_d, xout_d):
    nc = ctx.nc
    dram = ctx.dram
    onesrow = ctx.const["ones_row"]
    EPSC = 1e-6
    with ctx.scoped():
        sqT = ctx.sb("b1_sq", [128, NCH, T], BF16)
        attn_T = ctx.sb("b1_at", [128, NCH, T], BF16)
        memaug = ctx.sb("b1_mem", [128, 4, 16, 65], BF16)
        Sgp = ctx.sb("b1_sgp", [128, 16, 65], BF16)
        gcol = ctx.sb("b1_g", [128, 16], F32)
        wcol = ctx.sb("b1_wc", [64, 16, 4], F32)
        sx = ctx.sem("b1_sx")
        sv_ = ctx.sem("b1_sv")
        sm_ = ctx.sem("b1_sm")

        with nc.Block() as blk:
            @blk.sync
            def _(sy):
                sy.dma_start(out=sqT[:, :, :],
                             in_=sqT_d.rearrange("(hp p) t -> p hp t", p=128)) \
                    .then_inc(sx, 16)
                sy.wait_ge(sx, ctx.inc(sx, 16))

            @blk.gpsimd
            def _(g):
                g.dma_start(out=memaug[:, :, :, :], in_=dram["w_mem"]).then_inc(sx, 16)
                g.wait_ge(sx, ctx.inc(sx, 16))

        with ctx.scoped():
            sall = ctx.sb("b1_sall", [64, 4, 16, 65], F32)
            pref = ctx.sb("b1_pref", [64, 4], F32)
            sgf = ctx.sb("b1_sgf", [64, 16, 65], F32)
            relT = ctx.sb("b1_rel", [16, 4, 4], F32)
            rsum = ctx.sb("b1_rs", [16, 4], F32)
            rcol = ctx.sb("b1_rc", [16, 1], F32)
            cS = ctx.sb("b1_cS", [16, 1], F32)
            wgt = ctx.sb("b1_wg", [16, 4], F32)
            wrow = ctx.sb("b1_wr", [1, 64], F32)
            grow = ctx.sb("b1_gr", [1, 16], F32)
            gsig = ctx.sb("b1_gs", [1, 16], F32)
            pb = ctx.ps("b1_pb", [64, 64], F32)
            with nc.Block() as blk:
                @blk.gpsimd
                def _(g):
                    g.dma_start(out=sall[:, :, :, :],
                                in_=states_d.transpose([2, 0, 1, 3])) \
                        .then_inc(sx, 16)
                    g.dma_start(out=pref[:, :], in_=dram["c_pref"]).then_inc(sx, 16)
                    g.dma_start(out=relT[:, :, :],
                                in_=relp_d.transpose([1, 0, 2])) \
                        .then_inc(sx, 16)
                    g.dma_start(out=grow[:, :], in_=dram["w_gate"]).then_inc(sx, 16)
                    g.wait_ge(sx, ctx.inc(sx, 64))

            b_v, b_m = ctx.cnt(sv_), ctx.cnt(sm_)
            with nc.Block() as blk:
                @blk.vector
                def _(v):
                    sv = SerialEng(ctx, v, ctx.sem("b1_ser"))
                    v.memset(cS[:, :], 1.0 / S)
                    sv.tensor_scalar_mul(sgf[:, :, :], sall[:, 0, :, :],
                                         pref[:, 0:1])
                    for r in range(1, 4):
                        sv.scalar_tensor_tensor(sgf[:, :, :], sall[:, r, :, :],
                                                pref[:, r:r + 1], sgf[:, :, :],
                                                ALU.mult, ALU.add)
                    sv.tensor_copy(Sgp[0:64, :, :], sgf[:, :, :])
                    sv.tensor_copy(Sgp[64:128, :, :], sgf[:, :, :])
                    sv.tensor_add(rsum[:, :], relT[:, 0, :], relT[:, 1, :])
                    sv.tensor_add(rsum[:, :], rsum[:, :], relT[:, 2, :])
                    sv.tensor_add(rsum[:, :], rsum[:, :], relT[:, 3, :])
                    sv.tensor_reduce(rcol[:, :], rsum[:, :], AX.X, ALU.max,
                                     negate=True)
                    sv.tensor_scalar_mul(rcol[:, :], rcol[:, :], 1.0 / S)
                    sv.milestone(sv_, 1)
                    v.wait_ge(sm_, b_m + 1)
                    sv.tensor_reduce(rcol[:, :], wgt[:, :], AX.X, ALU.add)
                    sv.reciprocal(rcol[:, :], rcol[:, :])
                    sv.tensor_scalar_mul(wgt[:, :], wgt[:, :], rcol[:, :])
                    sv.milestone(sv_, 1)
                ctx.inc(sv_, 2)

                @blk.scalar
                def _(s):
                    s.wait_ge(sv_, b_v + 1)
                    s.activation(wgt[:, :], rsum[:, :], AF.Exp,
                                 bias=rcol[:, 0:1], scale=cS[:, 0:1]) \
                        .then_inc(sm_, 1)
                    s.activation(gsig[:, :], grow[:, :], AF.Sigmoid) \
                        .then_inc(sm_, 1)
                ctx.inc(sm_, 2)

            with nc.Block() as blk:
                @blk.sync
                def _(sy):
                    sy.dma_start(out=wrow[:, :], in_=wgt[:, :]).then_inc(sx, 16)
                    sy.wait_ge(sx, ctx.inc(sx, 16))

            b_m2 = ctx.cnt(sm_)
            with nc.Block() as blk:
                @blk.tensor
                def _(t):
                    t.matmul(pb[:, :], onesrow[0:1, 0:64], wrow[:, :],
                             start=True, stop=True).then_inc(sm_, 1)
                ctx.inc(sm_, 1)

                @blk.vector
                def _(v):
                    v.wait_ge(sm_, b_m2 + 1)
                    v.tensor_copy(wcol[:, :, :],
                                  pb[:, :].rearrange("p (h a) -> p h a", a=4))

            b_m3 = ctx.cnt(sm_)
            with nc.Block() as blk:
                @blk.tensor
                def _(t):
                    t.matmul(pb[:, 0:16], onesrow[0:1, 0:64], gsig[:, :],
                             start=True, stop=True).then_inc(sm_, 1)
                ctx.inc(sm_, 1)

                @blk.vector
                def _(v):
                    v.wait_ge(sm_, b_m3 + 1)
                    v.tensor_copy(gcol[0:64, :], pb[:, 0:16])
                    v.tensor_copy(gcol[64:128, :], pb[:, 0:16])

        with ctx.scoped():
            ndsb = ctx.sb("b1_nd", [65, T], F32)
            ltmp = ctx.sb("b1_lt", [64, 512], F32)
            lcl = ctx.sb("b1_lc", [64, 512], F32)
            tmp64 = ctx.sb("b1_t64", [64, 512], F32)
            macc = ctx.sb("b1_ma", [64, 512], F32)
            rB2 = ctx.sb("b1_rB", [64, 512], F32)
            drow = ctx.sb("b1_dr", [1, 512], F32)
            pC = ctx.ps("b1_pC", [65, 512], F32)
            pM = [ctx.ps("b1_pM", [65, 512], F32) for _ in range(4)]
            pB2 = [ctx.ps("b1_pB2", [64, 512], F32) for _ in range(2)]
            snd = ctx.sem("b1_nd")
            smm = ctx.sem("b1_mm")
            sbb = ctx.sem("b1_bb")
            sev = ctx.sem("b1_ev")
            srw = ctx.sem("b1_rw")
            for h in range(16):
                hb = (h % 2) * 64
                hp = h // 2
                b_nd, b_mm, b_bb = ctx.cnt(snd), ctx.cnt(smm), ctx.cnt(sbb)
                b_ev, b_rw = ctx.cnt(sev), ctx.cnt(srw)
                with nc.Block() as blk:
                    @blk.sync
                    def _(sy):
                        sy.dma_start(out=ndsb[:, :], in_=numden_d[h, :, :]) \
                            .then_inc(snd, 16)
                    ctx.inc(snd, 16)

                    @blk.tensor
                    def _(t):
                        for tb in range(4):
                            sl = slice(tb * 512, (tb + 1) * 512)
                            if tb >= 1:
                                t.wait_ge(sev, b_ev + tb)
                            t.matmul(pC[:, :], Sgp[hb:hb + 64, h, :],
                                     sqT[hb:hb + 64, hp, sl],
                                     start=True, stop=True).then_inc(smm, 1)
                            for mi in range(4):
                                t.matmul(pM[mi][:, :],
                                         memaug[hb:hb + 64, mi, h, :],
                                         sqT[hb:hb + 64, hp, sl],
                                         start=True, stop=True).then_inc(smm, 1)
                            for j in range(5):
                                t.wait_ge(srw, b_rw + 5 * tb + j + 1)
                                t.matmul(pB2[j % 2][:, :],
                                         onesrow[0:1, 0:64], drow[:, :],
                                         start=True, stop=True).then_inc(sbb, 1)
                    ctx.inc(smm, 20)
                    ctx.inc(sbb, 20)

                    @blk.vector
                    def _(v):
                        sv = SerialEng(ctx, v, ctx.sem("b1_ser"))
                        v.wait_ge(snd, b_nd + 16)
                        for tb in range(4):
                            sl = slice(tb * 512, (tb + 1) * 512)
                            v.wait_ge(smm, b_mm + 5 * tb + 1)
                            sv.tensor_add(ltmp[:, :], ndsb[0:64, sl],
                                          pC[0:64, :])
                            sv.tensor_add(drow[:, :], ndsb[64:65, sl],
                                          pC[64:65, :])
                            sv.tensor_scalar_max(drow[:, :], drow[:, :], EPSC)
                            sv.reciprocal(drow[:, :], drow[:, :])
                            sv.milestone(srw, 1)
                            v.wait_ge(sbb, b_bb + 5 * tb + 1)
                            sv.tensor_mul(lcl[:, :], ltmp[:, :], pB2[0][:, :])
                            for mi in range(4):
                                v.wait_ge(smm, b_mm + 5 * tb + mi + 2)
                                sv.tensor_scalar_max(drow[:, :],
                                                     pM[mi][64:65, :], EPSC)
                                sv.reciprocal(drow[:, :], drow[:, :])
                                sv.tensor_scalar_mul(drow[:, :], drow[:, :],
                                                     wcol[0:1, h, mi:mi + 1])
                                sv.milestone(srw, 1)
                                v.wait_ge(sbb, b_bb + 5 * tb + mi + 2)
                                sv.tensor_copy(rB2[:, :], pB2[(mi + 1) % 2][:, :])
                                sv.tensor_mul(tmp64[:, :], pM[mi][0:64, :],
                                              rB2[:, :])
                                if mi == 0:
                                    sv.tensor_copy(macc[:, :], tmp64[:, :])
                                else:
                                    sv.tensor_add(macc[:, :], macc[:, :],
                                                  tmp64[:, :])
                            sv.tensor_sub(tmp64[:, :], macc[:, :], lcl[:, :])
                            sv.scalar_tensor_tensor(
                                attn_T[hb:hb + 64, hp, sl], tmp64[:, :],
                                gcol[0:64, h:h + 1], lcl[:, :],
                                ALU.mult, ALU.add)
                            sv.milestone(sev, 1)
                    ctx.inc(srw, 20)
                    ctx.inc(sev, 4)

        with ctx.scoped():
            wp = ctx.sb("b1_wp", [128, NCH, 1024], BF16)
            xb1 = [ctx.sb("b1_x1", [128, NCH, 512], F32) for _ in range(2)]
            po = [ctx.ps("b1_po", [128, 512], F32) for _ in range(2)]
            swp = ctx.sem("b1_wp2")
            sxl = ctx.sem("b1_xl")
            som = ctx.sem("b1_om")
            soe = ctx.sem("b1_oe")
            sos = ctx.sem("b1_os")
            b_wp, b_xl, b_om = ctx.cnt(swp), ctx.cnt(sxl), ctx.cnt(som)
            b_oe, b_os = ctx.cnt(soe), ctx.cnt(sos)
            with nc.Block() as blk:
                @blk.gpsimd
                def _(g):
                    g.dma_start(out=wp[:, :, :],
                                in_=dram["w_oT"].transpose([1, 0, 2])) \
                        .then_inc(swp, 16)
                    for tb in range(4):
                        if tb >= 2:
                            g.wait_ge(sos, b_os + 16 * (tb - 1))
                        g.dma_start(out=xb1[tb % 2][:, :, :],
                                    in_=xin_d.rearrange("(c p) t -> p c t", p=128)
                                    [:, :, tb * 512:(tb + 1) * 512]).then_inc(sxl, 16)
                ctx.inc(swp, 16)
                ctx.inc(sxl, 64)

                @blk.tensor
                def _(t):
                    t.wait_ge(swp, b_wp + 16)
                    for tb in range(4):
                        for m in range(NCH):
                            idx = tb * NCH + m
                            if idx >= 2:
                                t.wait_ge(soe, b_oe + idx - 1)
                            for ch in range(NCH):
                                last = t.matmul(po[idx % 2][:, :],
                                                wp[:, ch, m * 128:(m + 1) * 128],
                                                attn_T[:, ch, tb * 512:(tb + 1) * 512],
                                                start=(ch == 0), stop=(ch == NCH - 1))
                            last.then_inc(som, 1)
                ctx.inc(som, 32)

                @blk.vector
                def _(v):
                    for tb in range(4):
                        v.wait_ge(sxl, b_xl + 16 * (tb + 1))
                        for m in range(NCH):
                            idx = tb * NCH + m
                            v.wait_ge(som, b_om + idx + 1)
                            v.tensor_add(xb1[tb % 2][:, m, :], xb1[tb % 2][:, m, :],
                                         po[idx % 2][:, :]).then_inc(soe, 1)
                ctx.inc(soe, 32)

                @blk.sync
                def _(sy):
                    for tb in range(4):
                        sy.wait_ge(soe, b_oe + NCH * (tb + 1))
                        sy.dma_start(
                            out=xout_d.rearrange("(c p) t -> p c t", p=128)
                            [:, :, tb * 512:(tb + 1) * 512],
                            in_=xb1[tb % 2][:, :, :]).then_inc(sos, 16)
                    sy.wait_ge(sos, b_os + 64)
                ctx.inc(sos, 64)


def prep_att1b_host(ins, core):
    d = {}
    mem = np.asarray(ins["L1_attn_memories"], np.float32)
    mn = np.asarray(ins["L1_attn_memory_norms"], np.float32)
    aug = np.concatenate([mem, mn[..., None]], axis=-1)
    aug = np.transpose(aug, (2, 0, 1, 3))
    d["w_mem"] = np.ascontiguousarray(np.concatenate([aug, aug], axis=0))
    wo = np.ascontiguousarray(np.asarray(ins["L1_attn_w_o"], np.float32).T)
    d["w_oT"] = np.ascontiguousarray(wo.reshape(8, 128, 1024))
    d["w_gate"] = np.ascontiguousarray(
        np.asarray(ins["L1_attn_gate"], np.float32).reshape(1, 16))
    pref = np.zeros((64, 4), np.float32)
    pref[:, :core % 4] = 1.0
    d["c_pref"] = pref
    return d


ATT1B_DRAM_SPECS = [
    ("w_mem", [128, 4, 16, 65], F32),
    ("w_oT", [8, 128, 1024], F32),
    ("w_gate", [1, 16], F32),
    ("c_pref", [64, 4], F32),
]


def build_att1b_launch():
    nc = bass.Bass("TRN2", target_bir_lowering=False, detect_race_conditions=False)
    dram = {}
    dram["x_in"] = nc.dram_tensor("x_in", [D, T], F32, kind="ExternalInput").ap()
    dram["c_ident"] = nc.dram_tensor("c_ident", [128, 128], BF16, kind="ExternalInput").ap()
    dram["c_ident2"] = nc.dram_tensor("c_ident2", [128, 64], BF16, kind="ExternalInput").ap()
    dram["sqT"] = nc.dram_tensor("sqT", [D, T], BF16, kind="ExternalInput").ap()
    dram["numden"] = nc.dram_tensor("numden", [16, 65, T], F32, kind="ExternalInput").ap()
    dram["states"] = nc.dram_tensor("states", [4, 16, 64, 65], F32, kind="ExternalInput").ap()
    dram["relp"] = nc.dram_tensor("relp", [4, 16, 4], F32, kind="ExternalInput").ap()
    for nm, shape, dt in ATT1B_DRAM_SPECS:
        dram[nm] = nc.dram_tensor(nm, shape, dt, kind="ExternalInput").ap()
    xout = nc.dram_tensor("x_out", [D, T], F32, kind="ExternalOutput").ap()
    ctx = Ctx(nc, dram)
    with ctx.root:
        emit_const_setup(ctx)
        emit_att1b(ctx, dram["x_in"], dram["sqT"], dram["numden"], dram["states"],
                   dram["relp"], xout)
    return nc


_NC_CACHE = {}


def _get_nc(name, builder):
    if name not in _NC_CACHE:
        _NC_CACHE[name] = builder()
    return _NC_CACHE[name]


def kernel_unfused(**inputs):
    ins = {k: np.asarray(v) for k, v in inputs.items()}
    x = np.asarray(ins["L0_attn_x"], np.float32)
    consts = const_host()

    def xs_of(c, xarr):
        b, sl = c // 4, c % 4
        return np.ascontiguousarray(xarr[b, sl * 2048:(sl + 1) * 2048, :].T)

    xcur = [xs_of(c, x) for c in range(8)]

    def run(nc, maps):
        return run_bass_kernel_spmd(nc, maps, core_ids=list(range(8)))

    w_f0 = prep_ffn0_host(ins["L0_ffn_controller"], ins["L0_ffn_f1"],
                          ins["L0_ffn_bias"], ins["L0_ffn_f2"])
    nc = _get_nc("ffn0", build_ffn0_launch)
    res = run(nc, [{**{"x_in": xcur[c]}, **w_f0, **consts} for c in range(8)])
    xcur = [res.results[c]["x_out"] for c in range(8)]

    nc = _get_nc("att0a", build_att0a_launch)
    a0w = [prep_att0a_host(ins["L0_attn_w_qkv"], (c % 4) * 2048) for c in range(8)]
    res_a = run(nc, [{**{"x_in": xcur[c]}, **a0w[c], **consts} for c in range(8)])

    kT_full = {}
    vhm_full = {}
    for b in range(2):
        kT_full[b] = np.ascontiguousarray(np.stack(
            [res_a.results[4 * b + sl]["kT"] for sl in range(4)], axis=0))
        vhm_full[b] = np.ascontiguousarray(np.stack(
            [res_a.results[4 * b + sl]["vhm"] for sl in range(4)], axis=0))
    w_a0b = prep_att0b_host(ins["L0_attn_w_proj"], ins["L0_attn_b_proj"])
    nc = _get_nc("att0b", build_att0b_launch)
    res = run(nc, [{**{"x_in": xcur[c], "qT": res_a.results[c]["qT"],
                       "kT": kT_full[c // 4], "vhm": vhm_full[c // 4]},
                    **w_a0b, **consts} for c in range(8)])
    xcur = [res.results[c]["x_out"] for c in range(8)]

    nc = _get_nc("ffn0", build_ffn0_launch)
    res = run(nc, [{**{"x_in": xcur[c]}, **w_f0, **consts} for c in range(8)])
    xcur = [res.results[c]["x_out"] for c in range(8)]

    w_f1 = prep_ffn1_host(ins)
    nc = _get_nc("ffn1", build_ffn1_launch)
    res = run(nc, [{**{"x_in": xcur[c]}, **w_f1, **consts} for c in range(8)])
    xcur = [res.results[c]["x_out"] for c in range(8)]

    nc = _get_nc("att1a", build_att1a_launch)
    w_a1a = prep_att1a_host(ins, 0)
    res_a = run(nc, [{**{"x_in": xcur[c]}, **w_a1a, **consts} for c in range(8)])

    nc = _get_nc("att1b", build_att1b_launch)
    maps = []
    for c in range(8):
        b = c // 4
        grp = [4 * b + r for r in range(4)]
        states = np.ascontiguousarray(
            np.stack([res_a.results[g_]["state"] for g_ in grp], 0))
        relp = np.ascontiguousarray(
            np.stack([res_a.results[g_]["relp"] for g_ in grp], 0))
        m = {"x_in": xcur[c], "sqT": res_a.results[c]["sqT"],
             "numden": res_a.results[c]["numden"],
             "states": states, "relp": relp}
        m.update(prep_att1b_host(ins, c))
        m.update(consts)
        maps.append(m)
    res = run(nc, maps)
    xcur = [res.results[c]["x_out"] for c in range(8)]

    nc = _get_nc("ffn1", build_ffn1_launch)
    res = run(nc, [{**{"x_in": xcur[c]}, **w_f1, **consts} for c in range(8)])
    xcur = [res.results[c]["x_out"] for c in range(8)]

    out = np.empty((2, 8192, 1024), np.float32)
    for c in range(8):
        b, sl = c // 4, c % 4
        out[b, sl * 2048:(sl + 1) * 2048, :] = xcur[c].T
    return out


def build_fused():
    nc = bass.Bass("TRN2", target_bir_lowering=False, detect_race_conditions=False)
    dram = {}
    dram["x_in"] = nc.dram_tensor("x_in", [D, T], F32, kind="ExternalInput").ap()
    dram["c_ident"] = nc.dram_tensor("c_ident", [128, 128], BF16, kind="ExternalInput").ap()
    dram["c_ident2"] = nc.dram_tensor("c_ident2", [128, 64], BF16, kind="ExternalInput").ap()
    for nm, shape, dt in (FFN0_DRAM_SPECS + ATT0A_DRAM_SPECS + ATT0B_DRAM_SPECS +
                          FFN1_DRAM_SPECS + ATT1A_DRAM_SPECS + ATT1B_DRAM_SPECS):
        dram[nm] = nc.dram_tensor(nm, shape, dt, kind="ExternalInput").ap()
    x_out = nc.dram_tensor("x_out", [D, T], F32, kind="ExternalOutput").ap()

    xi = [nc.dram_tensor(f"x{i}", [D, T], F32).ap() for i in range(1, 6)]
    park = nc.dram_tensor("park", [D, T], F32).ap()
    intd = nc.dram_tensor("intd", [128, 16, 1024], BF16).ap()
    zpark = nc.dram_tensor("zpark", [128, NCH, T], BF16).ap()
    qT = nc.dram_tensor("qT_i", [D, T], BF16).ap()
    kT = nc.dram_tensor("kT_i", [D, T], BF16).ap()
    vhm = nc.dram_tensor("vhm_i", [16, 128, NT, 65], BF16).ap()
    kTg = nc.dram_tensor("kTg_i", [4 * D, T], BF16).ap()
    vhmg = nc.dram_tensor("vhmg_i", [4 * 16, 128, NT, 65], BF16).ap()
    sqT = nc.dram_tensor("sqT_i", [D, T], BF16).ap()
    numden = nc.dram_tensor("numden_i", [16, 65, T], F32).ap()
    state = nc.dram_tensor("state_i", [16, 64, 65], F32).ap()
    relp = nc.dram_tensor("relp_i", [16, 4], F32).ap()
    stateg = nc.dram_tensor("stateg_i", [4 * 16, 64, 65], F32).ap()
    relpg = nc.dram_tensor("relpg_i", [4 * 16, 4], F32).ap()

    RG = [[0, 1, 2, 3], [4, 5, 6, 7]]
    ctx = Ctx(nc, dram)
    with ctx.root:
        emit_const_setup(ctx)
        emit_ffn0(ctx, dram["x_in"], xi[0], park, intd)
        emit_att0a(ctx, xi[0], qT, kT, vhm)
        scc = ctx.sem("cc")
        with nc.Block() as blk:
            @blk.gpsimd
            def _(g):
                g.collective_compute("AllGather", ALU.bypass, replica_groups=RG,
                                     ins=[kT.opt()], outs=[kTg.opt()]).then_inc(scc)
                g.collective_compute("AllGather", ALU.bypass, replica_groups=RG,
                                     ins=[vhm.opt()], outs=[vhmg.opt()]).then_inc(scc)
                g.wait_ge(scc, ctx.inc(scc, 2))
        emit_att0b(ctx, xi[0],
                   qT, kTg.rearrange("(r d) t -> r d t", r=4),
                   vhmg.rearrange("(r h) p n e -> r h p n e", r=4), xi[1])
        emit_ffn0(ctx, xi[1], xi[2], park, intd)
        emit_ffn1(ctx, xi[2], xi[3], zpark)
        emit_att1a(ctx, xi[3], sqT, numden, state, relp)
        with nc.Block() as blk:
            @blk.gpsimd
            def _(g):
                g.collective_compute("AllGather", ALU.bypass, replica_groups=RG,
                                     ins=[state.opt()], outs=[stateg.opt()]) \
                    .then_inc(scc)
                g.collective_compute("AllGather", ALU.bypass, replica_groups=RG,
                                     ins=[relp.opt()], outs=[relpg.opt()]).then_inc(scc)
                g.wait_ge(scc, ctx.inc(scc, 2))
        emit_att1b(ctx, xi[3], sqT, numden,
                   stateg.rearrange("(r h) d e -> r h d e", r=4),
                   relpg.rearrange("(r h) a -> r h a", r=4), xi[4])
        emit_ffn1(ctx, xi[4], x_out, zpark)
    return nc


def kernel(**inputs):
    ins = {k: np.asarray(v) for k, v in inputs.items()}
    x = np.asarray(ins["L0_attn_x"], np.float32)
    consts = const_host()
    w_f0 = prep_ffn0_host(ins["L0_ffn_controller"], ins["L0_ffn_f1"],
                          ins["L0_ffn_bias"], ins["L0_ffn_f2"])
    w_a0b = prep_att0b_host(ins["L0_attn_w_proj"], ins["L0_attn_b_proj"])
    w_f1 = prep_ffn1_host(ins)
    w_a1a = prep_att1a_host(ins, 0)
    nc = _get_nc("fused", build_fused)
    in_maps = []
    for c in range(8):
        b, sl = c // 4, c % 4
        m = {"x_in": np.ascontiguousarray(x[b, sl * 2048:(sl + 1) * 2048, :].T)}
        m.update(consts)
        m.update(w_f0)
        m.update(prep_att0a_host(ins["L0_attn_w_qkv"], sl * 2048))
        m.update(w_a0b)
        m.update(w_f1)
        m.update(w_a1a)
        m.update(prep_att1b_host(ins, c))
        in_maps.append(m)
    res = run_bass_kernel_spmd(nc, in_maps, core_ids=list(range(8)))
    out = np.empty((2, 8192, 1024), np.float32)
    for c in range(8):
        b, sl = c // 4, c % 4
        out[b, sl * 2048:(sl + 1) * 2048, :] = res.results[c]["x_out"].T
    return out


# version 03
# speedup vs baseline: 1.0300x; 1.0300x over previous
import numpy as np
import ml_dtypes
from contextlib import ExitStack, contextmanager

import concourse.bass as bass
import concourse.mybir as mybir
from concourse.bass_utils import run_bass_kernel_spmd

F32 = mybir.dt.float32
BF16 = mybir.dt.bfloat16
AF = mybir.ActivationFunctionType
ALU = mybir.AluOpType
AX = mybir.AxisListType

NCORES = 8
D = 1024
NCH = 8
T = 2048
NT = T // 128
S = 8192
EPS_LN = 1e-5

BF = ml_dtypes.bfloat16


class Ctx:
    def __init__(self, nc, dram):
        self.nc = nc
        self.dram = dram
        self.root = ExitStack()
        self.stacks = [self.root]
        self.sems = {}
        self.sem_count = {}
        self._uid = 0
        self.const = {}

    def uid(self):
        self._uid += 1
        return self._uid

    def sb(self, name, shape, dt, side=None):
        return self.stacks[-1].enter_context(
            self.nc.sbuf_tensor(f"{name}_{self.uid()}", shape, dt, side=side))

    def ps(self, name, shape, dt=F32):
        return self.stacks[-1].enter_context(
            self.nc.psum_tensor(f"{name}_{self.uid()}", shape, dt))

    SEM_ALIAS = {
        "b1_wp2": "a0b_wp", "b1_xl": "a0b_xl", "b1_om": "a0b_om",
        "b1_oe": "a0b_oe", "b1_os": "a0b_ost",
        "a1_wl": "a0_wl", "a1_mm": "a0_mm", "a1_ev": "a0_ev",
        "f1_lq": "ln_q", "f1_lmt": "ln_mm", "f1_lse": "ln_ev",
        "f1_ls1": "ln_s1", "f1_lsv": "ln_sv",
        "b1_nd": "a0b_kv", "b1_mm": "a0_mm2", "b1_bb": "a0b_S",
        "b1_rw": "a0b_E", "b1_ev": "a0b_P",
    }

    def sem(self, name):
        name = self.SEM_ALIAS.get(name, name)
        if name not in self.sems:
            s = self.root.enter_context(self.nc.semaphore(name))
            self.sems[name] = s
            self.sem_count[s] = 0
        return self.sems[name]

    def inc(self, sem, n=1):
        self.sem_count[sem] += n
        return self.sem_count[sem]

    def cnt(self, sem):
        return self.sem_count[sem]

    @contextmanager
    def scoped(self):
        st = ExitStack()
        self.stacks.append(st)
        try:
            with st:
                yield
        finally:
            self.stacks.pop()


class SerialEng:

    def __init__(self, ctx, eng, sem):
        self._ctx = ctx
        self._eng = eng
        self._sem = sem

    def __getattr__(self, name):
        fn = getattr(self._eng, name)

        def wrap(*a, **kw):
            self._eng.wait_ge(self._sem, self._ctx.cnt(self._sem))
            inst = fn(*a, **kw)
            inst.then_inc(self._sem, 1)
            self._ctx.inc(self._sem, 1)
            return inst
        return wrap

    def wait_ge(self, sem, val):
        self._eng.wait_ge(sem, val)

    def milestone(self, sem, n=1):
        self._eng.wait_ge(self._sem, self._ctx.cnt(self._sem))
        self._eng.nop().then_inc(sem, n)


def emit_const_setup(ctx):
    nc = ctx.nc
    ctx.const["ones_bf"] = ctx.sb("ones_bf", [128, 1], BF16)
    ctx.const["ones_row"] = ctx.sb("ones_row", [1, 128], F32)
    ctx.const["ones_row_bf"] = ctx.sb("ones_row_bf", [1, 128], BF16)
    ctx.const["ones_col"] = ctx.sb("ones_col", [128, 1], F32)
    ctx.const["ident"] = ctx.sb("ident", [128, 128], BF16)
    ctx.const["ident2"] = ctx.sb("ident2", [128, 64], BF16)
    sem = ctx.sem("cst")
    with nc.Block() as blk:
        @blk.vector
        def _(v):
            v.memset(ctx.const["ones_bf"][:, :], 1.0)
            v.memset(ctx.const["ones_row"][:, :], 1.0)
            v.memset(ctx.const["ones_row_bf"][:, :], 1.0)
            v.memset(ctx.const["ones_col"][:, :], 1.0)

        @blk.gpsimd
        def _(g):
            g.dma_start(out=ctx.const["ident"][:, :],
                        in_=ctx.dram["c_ident"][:, :]).then_inc(sem, 16)
            g.dma_start(out=ctx.const["ident2"][:, :],
                        in_=ctx.dram["c_ident2"][:, :]).then_inc(sem, 16)
            g.wait_ge(sem, ctx.inc(sem, 32))


def emit_ln(ctx, xt, xn_bf, ntok, gamma_col=None, beta_col=None, var_row=None):
    nc = ctx.nc
    assert ntok % 512 == 0
    nblk = ntok // 512
    onesbf = ctx.const["ones_bf"]
    onesrow = ctx.const["ones_row"]
    with ctx.scoped():
        xcast = [ctx.sb("ln_xc", [128, NCH, 512], BF16) for _ in range(2)]
        xsq = [ctx.sb("ln_xsq", [128, NCH, 512], BF16) for _ in range(2)]
        stat_s = [ctx.ps("ln_ss", [1, 512], F32) for _ in range(2)]
        stat_q = [ctx.ps("ln_sq", [1, 512], F32) for _ in range(2)]
        rsum = ctx.sb("ln_rsum", [1, ntok], F32)
        rsq = ctx.sb("ln_rsq", [1, ntok], F32)
        rrow = ctx.sb("ln_r", [1, ntok], F32)
        prow = ctx.sb("ln_p", [1, ntok], F32)
        bmat = [ctx.ps("ln_b", [128, 512], F32) for _ in range(2)]
        rB = ctx.sb("ln_rB", [128, ntok], F32)
        pB = ctx.sb("ln_pB", [128, ntok], F32)
        tmp = ctx.sb("ln_tmp", [128, 512], F32)
        sv = ctx.sem("ln_v")
        ssq = ctx.sem("ln_q")
        sm = ctx.sem("ln_mm")
        se = ctx.sem("ln_ev")
        s1 = ctx.sem("ln_s1")
        sb2 = ctx.sem("ln_b2")

        b_sv, b_ssq, b_sm, b_se = (ctx.cnt(sv), ctx.cnt(ssq), ctx.cnt(sm), ctx.cnt(se))
        with nc.Block() as blk:
            @blk.vector
            def _(v):
                for ib in range(nblk):
                    if ib >= 2:
                        v.wait_ge(sm, b_sm + ib - 1)
                    for ch in range(NCH):
                        last = v.tensor_copy(xcast[ib % 2][:, ch, :],
                                             xt[:, ch, ib * 512:(ib + 1) * 512])
                    last.then_inc(sv, 1)
            ctx.inc(sv, nblk)

            @blk.scalar
            def _(s):
                for ib in range(nblk):
                    if ib >= 2:
                        s.wait_ge(sm, b_sm + ib - 1)
                    for ch in range(NCH):
                        last = s.activation(xsq[ib % 2][:, ch, :],
                                            xt[:, ch, ib * 512:(ib + 1) * 512], AF.Square)
                    last.then_inc(ssq, 1)
            ctx.inc(ssq, nblk)

            @blk.tensor
            def _(t):
                for ib in range(nblk):
                    t.wait_ge(sv, b_sv + ib + 1)
                    t.wait_ge(ssq, b_ssq + ib + 1)
                    if ib >= 2:
                        t.wait_ge(se, b_se + ib - 1)
                    for ch in range(NCH):
                        t.matmul(stat_s[ib % 2][:, :], onesbf[:, :],
                                 xcast[ib % 2][:, ch, :],
                                 start=(ch == 0), stop=(ch == NCH - 1))
                    for ch in range(NCH):
                        last = t.matmul(stat_q[ib % 2][:, :], onesbf[:, :],
                                        xsq[ib % 2][:, ch, :],
                                        start=(ch == 0), stop=(ch == NCH - 1))
                    last.then_inc(sm, 1)
            ctx.inc(sm, nblk)

            @blk.vector
            def _(v):
                for ib in range(nblk):
                    v.wait_ge(sm, b_sm + ib + 1)
                    v.tensor_copy(rsum[:, ib * 512:(ib + 1) * 512], stat_s[ib % 2][:, :])
                    v.tensor_copy(rsq[:, ib * 512:(ib + 1) * 512],
                                  stat_q[ib % 2][:, :]).then_inc(se, 1)
            ctx.inc(se, nblk)

        with nc.Block() as blk:
            @blk.scalar
            def _(s):
                s.activation(prow[:, :], rsum[:, :], AF.Square, bias=np.float32(0.0),
                             scale=np.float32(1.0 / D)).then_inc(s1, 1)

            @blk.vector
            def _(v):
                sv = SerialEng(ctx, v, ctx.sem("ln_sv"))
                v.wait_ge(s1, ctx.inc(s1))
                sv.scalar_tensor_tensor(rrow[:, :], rsq[:, :], 1.0 / D, prow[:, :],
                                        ALU.mult, ALU.subtract)
                if var_row is not None:
                    sv.tensor_copy(var_row[:, :], rrow[:, :])
                sv.milestone(s1, 1)

            @blk.scalar
            def _(s):
                s.wait_ge(s1, ctx.inc(s1))
                s.activation(rrow[:, :], rrow[:, :], AF.Sqrt, bias=np.float32(EPS_LN),
                             scale=np.float32(1.0)).then_inc(s1, 1)

            @blk.vector
            def _(v):
                sv = SerialEng(ctx, v, ctx.sem("ln_sv"))
                v.wait_ge(s1, ctx.inc(s1))
                sv.reciprocal(rrow[:, :], rrow[:, :])
                sv.tensor_scalar_mul(prow[:, :], rsum[:, :], 1.0 / D)
                sv.tensor_mul(prow[:, :], prow[:, :], rrow[:, :])
                sv.tensor_scalar_mul(prow[:, :], prow[:, :], -1.0)

        b_b2 = ctx.cnt(sb2)
        with nc.Block() as blk:
            @blk.tensor
            def _(t):
                for j, row in enumerate((rrow, prow)):
                    for ib in range(nblk):
                        k = j * nblk + ib
                        if k >= 2:
                            t.wait_ge(sb2, b_b2 + k - 1)
                        t.matmul(bmat[k % 2][:, :], onesrow[:, :],
                                 row[:, ib * 512:(ib + 1) * 512],
                                 start=True, stop=True).then_inc(sm, 1)
            ctx.inc(sm, 2 * nblk)

            @blk.vector
            def _(v):
                b = ctx.cnt(sm) - 2 * nblk
                for j, dst in enumerate((rB, pB)):
                    for ib in range(nblk):
                        k = j * nblk + ib
                        v.wait_ge(sm, b + k + 1)
                        v.tensor_copy(dst[:, ib * 512:(ib + 1) * 512],
                                      bmat[k % 2][:, :]).then_inc(sb2, 1)
            ctx.inc(sb2, 2 * nblk)

        with nc.Block() as blk:
            @blk.vector
            def _(v):
                for ch in range(NCH):
                    for ib in range(nblk):
                        sl = slice(ib * 512, (ib + 1) * 512)
                        v.tensor_mul(tmp[:, :], xt[:, ch, sl], rB[:, sl])
                        v.tensor_add(xn_bf[:, ch, sl], tmp[:, :], pB[:, sl])
                        if gamma_col is not None:
                            v.tensor_scalar(xn_bf[:, ch, sl], xn_bf[:, ch, sl],
                                            gamma_col[:, ch:ch + 1],
                                            beta_col[:, ch:ch + 1],
                                            ALU.mult, ALU.add)


def emit_ffn0(ctx, xin_d, xout_d, park_d, int_d):
    nc = ctx.nc
    dram = ctx.dram
    GB = 16
    NGB = (T // 32) // GB
    WAVE = 16
    NWV = 128 // WAVE
    ident = ctx.const["ident"]

    with ctx.scoped():
        xn = ctx.sb("f0_xn", [128, NCH, T], BF16)
        permT = ctx.sb("f0_permT", [128, T], BF16)
        permN = ctx.sb("f0_permN", [128, NT, 128], BF16)
        sx = ctx.sem("f0_sx")

        with ctx.scoped():
            xT = ctx.sb("f0_xT", [128, NCH, T], F32)
            with nc.Block() as blk:
                @blk.sync
                def _(sy):
                    sy.dma_start(out=xT[:, :, :],
                                 in_=xin_d.rearrange("(c p) t -> p c t", p=128)) \
                        .then_inc(sx, 16)
                    sy.wait_ge(sx, ctx.inc(sx, 16))
            emit_ln(ctx, xT, xn, T)
            with nc.Block() as blk:
                @blk.sync
                def _(sy):
                    sy.dma_start(out=park_d.rearrange("(c p) t -> p c t", p=128),
                                 in_=xT[:, :, :]).then_inc(sx, 16)
                    sy.wait_ge(sx, ctx.inc(sx, 16))

        sl = ctx.sem("f0_l")
        sm = ctx.sem("f0_m")
        sv = ctx.sem("f0_v")
        with ctx.scoped():
            ctrl = ctx.sb("f0_ctrl", [128, NCH, 128], BF16)
            tieB = ctx.sb("f0_tie", [128, 512], F32)
            lgT = ctx.sb("f0_lgT", [128, T], F32)
            gmax = ctx.sb("f0_gmax", [128, T // 32], F32)
            plg = [ctx.ps("f0_plg", [128, 512], F32) for _ in range(2)]
            ptr = [ctx.ps("f0_ptr", [128, 128], BF16) for _ in range(2)]
            b_l, b_m, b_v = ctx.cnt(sl), ctx.cnt(sm), ctx.cnt(sv)
            with nc.Block() as blk:
                @blk.gpsimd
                def _(g):
                    g.dma_start(out=ctrl[:, :, :],
                                in_=dram["w_ctrl"].transpose([1, 0, 2])) \
                        .then_inc(sl, 16)
                    g.dma_start(out=tieB[:, :], in_=dram["c_tie"][:, :]).then_inc(sl, 16)
                ctx.inc(sl, 32)

                @blk.tensor
                def _(t):
                    t.wait_ge(sl, b_l + 32)
                    for ib in range(4):
                        if ib >= 2:
                            t.wait_ge(sv, b_v + ib - 1)
                        for ch in range(NCH):
                            last = t.matmul(plg[ib % 2][:, :], ctrl[:, ch, :],
                                            xn[:, ch, ib * 512:(ib + 1) * 512],
                                            start=(ch == 0), stop=(ch == NCH - 1))
                        last.then_inc(sm, 1)
                ctx.inc(sm, 4)

                @blk.vector
                def _(v):
                    for ib in range(4):
                        v.wait_ge(sm, b_m + ib + 1)
                        v.tensor_add(lgT[:, ib * 512:(ib + 1) * 512],
                                     plg[ib % 2][:, :], tieB[:, :]).then_inc(sv, 1)
                ctx.inc(sv, 4)

            with nc.Block() as blk:
                @blk.vector
                def _(v):
                    v.tensor_reduce(gmax[:, :].rearrange("p (g o) -> p g o", o=1),
                                    lgT[:, :].rearrange("p (g t) -> p g t", t=32),
                                    AX.X, ALU.max)
                    v.tensor_tensor(permT[:, :].rearrange("p (g t) -> p g t", t=32),
                                    lgT[:, :].rearrange("p (g t) -> p g t", t=32),
                                    gmax[:, :].rearrange("p (g o) -> p g o", o=1)
                                    .broadcast_to([128, T // 32, 32]),
                                    ALU.is_ge)

            b_m, b_v = ctx.cnt(sm), ctx.cnt(sv)
            with nc.Block() as blk:
                @blk.tensor
                def _(t):
                    for it in range(NT):
                        if it >= 2:
                            t.wait_ge(sv, b_v + it - 1)
                        t.transpose(ptr[it % 2][:, :], permT[:, it * 128:(it + 1) * 128],
                                    ident[:, :]).then_inc(sm, 1)
                ctx.inc(sm, NT)

                @blk.vector
                def _(v):
                    for it in range(NT):
                        v.wait_ge(sm, b_m + it + 1)
                        v.tensor_copy(permN[:, it, :], ptr[it % 2][:, :]).then_inc(sv, 1)
                ctx.inc(sv, NT)

        dT = ctx.sb("f0_dT", [128, NCH, GB, 128], BF16)
        xnat = [ctx.sb("f0_xnat", [128, NCH, 128], BF16) for _ in range(4)]
        f1w = [ctx.sb("f0_f1w", [128, WAVE, NCH, 32], BF16) for _ in range(2)]
        f2w = [ctx.sb("f0_f2w", [128, 4, 1024], BF16) for _ in range(2)]
        f1bias = ctx.sb("f0_f1bias", [32, 128], F32)
        inner = ctx.sb("f0_inner", [128, 4, GB], BF16)
        stage = [ctx.sb("f0_stage", [GB, 4, 1024], BF16) for _ in range(2)]
        intG = ctx.sb("f0_intG", [128, GB, 1024], BF16)
        xpc = [ctx.sb("f0_xpc", [128, NCH, 512], F32) for _ in range(2)]

        swl = ctx.sem("f0_wl")
        sxm = ctx.sem("f0_xm")
        sxe = ctx.sem("f0_xe")
        sdm = ctx.sem("f0_dm")
        sde = ctx.sem("f0_de")
        sf1 = ctx.sem("f0_f1")
        sfe = ctx.sem("f0_fe")
        sf2 = ctx.sem("f0_f2")
        sev = ctx.sem("f0_ev")
        srg = ctx.sem("f0_rg")
        sld = ctx.sem("f0_ld")
        ssc = ctx.sem("f0_sc")
        sre = ctx.sem("f0_re")
        sxp = ctx.sem("f0_xp")
        sst = ctx.sem("f0_st")

        with nc.Block() as blk:
            @blk.gpsimd
            def _(g):
                g.dma_start(out=f1bias[:, :], in_=dram["w_f1b"][:, :]).then_inc(swl, 16)
                g.wait_ge(swl, ctx.inc(swl, 16))

        for gb in range(NGB):
            g0 = gb * GB
            tile0 = g0 // 4

            with ctx.scoped():
                pxt = [ctx.ps("f0_pxt", [128, 128], BF16) for _ in range(2)]
                pd = [ctx.ps("f0_pd", [128, NCH, 128], F32) for _ in range(2)]
                b_xm, b_xe = ctx.cnt(sxm), ctx.cnt(sxe)
                b_dm, b_de = ctx.cnt(sdm), ctx.cnt(sde)
                with nc.Block() as blk:
                    @blk.tensor
                    def _(t):
                        for lt in range(4):
                            it = tile0 + lt
                            for ch in range(NCH):
                                k = lt * NCH + ch
                                if k >= 2:
                                    t.wait_ge(sxe, b_xe + k - 1)
                                t.transpose(pxt[k % 2][:, :],
                                            xn[:, ch, it * 128:(it + 1) * 128],
                                            ident[:, :]).then_inc(sxm, 1)
                    ctx.inc(sxm, 4 * NCH)

                    @blk.vector
                    def _(v):
                        for lt in range(4):
                            for ch in range(NCH):
                                k = lt * NCH + ch
                                v.wait_ge(sxm, b_xm + k + 1)
                                v.tensor_copy(xnat[lt][:, ch, :],
                                              pxt[k % 2][:, :]).then_inc(sxe, 1)
                    ctx.inc(sxe, 4 * NCH)

                with nc.Block() as blk:
                    @blk.tensor
                    def _(t):
                        for lg in range(GB):
                            lt = lg // 4
                            rowb = (lg % 4) * 32
                            it = tile0 + lt
                            if lg >= 2:
                                t.wait_ge(sde, b_de + lg - 1)
                            for ch in range(NCH):
                                last = t.matmul(
                                    pd[lg % 2][:, ch, :],
                                    xnat[lt][rowb:rowb + 32, ch, :],
                                    permN[rowb:rowb + 32, it, :],
                                    start=(ch % 4 == 0), stop=(ch % 4 == 3),
                                    tile_position=(rowb, 0))
                            last.then_inc(sdm, 1)
                    ctx.inc(sdm, GB)

                    @blk.vector
                    def _(v):
                        for lg in range(GB):
                            v.wait_ge(sdm, b_dm + lg + 1)
                            v.tensor_copy(dT[:, :, lg, :],
                                          pd[lg % 2][:, :, :]).then_inc(sde, 1)
                    ctx.inc(sde, GB)

            with ctx.scoped():
                pf = [ctx.ps("f0_pf", [32, GB], F32) for _ in range(4)]
                pi = [ctx.ps("f0_pi", [GB, 1024], F32) for _ in range(2)]
                b_wl, b_f1 = ctx.cnt(swl), ctx.cnt(sf1)
                b_fe, b_f2 = ctx.cnt(sfe), ctx.cnt(sf2)
                b_ev, b_rg = ctx.cnt(sev), ctx.cnt(srg)
                b_ld = ctx.cnt(sld)

                def strip(es):
                    return (es % WAVE) // 4

                with nc.Block() as blk:
                    @blk.gpsimd
                    def _(g):
                        for w in range(NWV):
                            if w >= 2:
                                g.wait_ge(sf2, b_f2 + WAVE * (w - 1))
                            e0 = w * WAVE
                            g.dma_start(out=f1w[w % 2][:, :, :, :],
                                        in_=dram["w_f1"][:, e0:e0 + WAVE, :, :]) \
                                .then_inc(swl, 16)
                            for j in range(4):
                                g.dma_start(
                                    out=f2w[w % 2][32 * j:32 * j + 32, :, :],
                                    in_=dram["w_f2"][:, e0 + 4 * j:e0 + 4 * j + 4, :]) \
                                    .then_inc(swl, 16)
                    ctx.inc(swl, 5 * 16 * NWV)

                    @blk.tensor
                    def _(t):
                        def f2_mms(es):
                            w = es // WAVE
                            st = strip(es)
                            el = es % 4
                            t.wait_ge(sfe, b_fe + es + 1)
                            if es >= 2:
                                t.wait_ge(sev, b_ev + es - 1)
                            for nb in range(2):
                                last = t.matmul(
                                    pi[es % 2][:, nb * 512:(nb + 1) * 512],
                                    inner[32 * st:32 * st + 32, es % 4, :],
                                    f2w[w % 2][32 * st:32 * st + 32, el,
                                               nb * 512:(nb + 1) * 512],
                                    start=True, stop=True,
                                    tile_position=(32 * st, 0))
                            last.then_inc(sf2, 1)

                        for es in range(128):
                            w = es // WAVE
                            t.wait_ge(swl, b_wl + 5 * 16 * (w + 1))
                            if es >= 4:
                                t.wait_ge(sfe, b_fe + es - 3)
                            for ch in range(NCH):
                                last = t.matmul(pf[es % 4][:, :],
                                                f1w[w % 2][:, es % WAVE, ch, :],
                                                dT[:, ch, :, es],
                                                start=(ch == 0), stop=(ch == NCH - 1))
                            last.then_inc(sf1, 1)
                            if es >= 2:
                                f2_mms(es - 2)
                        f2_mms(126)
                        f2_mms(127)
                    ctx.inc(sf1, 128)
                    ctx.inc(sf2, 128)

                    @blk.vector
                    def _(v):
                        def evac(es):
                            q = es // 4
                            if q >= 2:
                                v.wait_ge(srg, b_rg + 16 * (q - 1))
                            v.wait_ge(sf2, b_f2 + es + 1)
                            v.tensor_copy(stage[q % 2][:, es % 4, :],
                                          pi[es % 2][:, :]).then_inc(sev, 1)

                        for es in range(128):
                            st = strip(es)
                            v.wait_ge(sf1, b_f1 + es + 1)
                            if es >= 16:
                                v.wait_ge(sf2, b_f2 + es - 15)
                            v.tensor_scalar(inner[32 * st:32 * st + 32, es % 4, :],
                                            pf[es % 4][:, :],
                                            f1bias[:, es:es + 1], 0.0,
                                            ALU.add, ALU.max).then_inc(sfe, 1)
                            if es >= 2:
                                evac(es - 2)
                        evac(126)
                        evac(127)
                    ctx.inc(sfe, 128)
                    ctx.inc(sev, 128)

                    @blk.sync
                    def _(sy):
                        for q in range(32):
                            sy.wait_ge(sev, b_ev + 4 * (q + 1))
                            sy.dma_start(
                                out=int_d[4 * q:4 * q + 4, :, :].transpose([1, 0, 2]),
                                in_=stage[q % 2][:, :, :],
                            ).then_inc(srg, 16)
                        sy.wait_ge(srg, b_rg + 32 * 16)
                        sy.dma_start(out=intG[:, :, :], in_=int_d[:, :, :]) \
                            .then_inc(sld, 16)
                    ctx.inc(srg, 32 * 16)
                    ctx.inc(sld, 16)

            with ctx.scoped():
                psc = [ctx.ps("f0_psc", [128, 128], F32) for _ in range(2)]
                b_sc, b_re = ctx.cnt(ssc), ctx.cnt(sre)
                b_ld2, b_xp, b_st = ctx.cnt(sld), ctx.cnt(sxp), ctx.cnt(sst)
                t0 = g0 * 32
                with nc.Block() as blk:
                    @blk.gpsimd
                    def _(g):
                        if gb >= 2:
                            g.wait_ge(sst, b_st - 16)
                        g.dma_start(out=xpc[gb % 2][:, :, :],
                                    in_=park_d.rearrange("(c p) t -> p c t", p=128)
                                    [:, :, t0:t0 + 512]).then_inc(sxp, 16)
                    ctx.inc(sxp, 16)

                    @blk.tensor
                    def _(t):
                        t.wait_ge(sld, b_ld2)
                        for lt in range(4):
                            for ch in range(NCH):
                                idx = lt * NCH + ch
                                if idx >= 2:
                                    t.wait_ge(sre, b_re + idx - 1)
                                for lg4 in range(4):
                                    lg = lt * 4 + lg4
                                    last = t.matmul(
                                        psc[idx % 2][:, lg4 * 32:(lg4 + 1) * 32],
                                        intG[:, lg, ch * 128:(ch + 1) * 128],
                                        permT[:, (g0 + lg) * 32:(g0 + lg + 1) * 32],
                                        start=(lg4 == 0), stop=(lg4 == 3))
                                last.then_inc(ssc, 1)
                    ctx.inc(ssc, 4 * NCH)

                    @blk.vector
                    def _(v):
                        v.wait_ge(sxp, b_xp + 16)
                        for lt in range(4):
                            for ch in range(NCH):
                                idx = lt * NCH + ch
                                v.wait_ge(ssc, b_sc + idx + 1)
                                v.scalar_tensor_tensor(
                                    xpc[gb % 2][:, ch, lt * 128:(lt + 1) * 128],
                                    psc[idx % 2][:, :], 0.5,
                                    xpc[gb % 2][:, ch, lt * 128:(lt + 1) * 128],
                                    ALU.mult, ALU.add).then_inc(sre, 1)
                    ctx.inc(sre, 4 * NCH)

                    @blk.sync
                    def _(sy):
                        sy.wait_ge(sre, b_re + 4 * NCH)
                        sy.dma_start(
                            out=xout_d.rearrange("(c p) t -> p c t", p=128)
                            [:, :, t0:t0 + 512],
                            in_=xpc[gb % 2][:, :, :]).then_inc(sst, 16)
                        if gb == NGB - 1:
                            sy.wait_ge(sst, b_st + 16)
                    ctx.inc(sst, 16)


def prep_ffn0_host(controller, f1, bias, f2):
    w_ctrl = np.ascontiguousarray(np.asarray(controller, np.float32).reshape(8, 128, 128))
    w_f1 = np.ascontiguousarray(
        np.transpose(np.asarray(f1, np.float32).reshape(8, 128, 128, 32), (1, 2, 0, 3)))
    w_f1b = np.ascontiguousarray(np.asarray(bias, np.float32).reshape(128, 32).T)
    w_f2 = np.ascontiguousarray(
        np.transpose(np.asarray(f2, np.float32).reshape(128, 32, 1024), (1, 0, 2)))
    tie = np.linspace(0.0, 1e-6, 32, dtype=np.float32)
    c_tie = np.ascontiguousarray(np.broadcast_to(np.tile(tie, 16)[None, :], (128, 512)))
    return dict(w_ctrl=w_ctrl, w_f1=w_f1, w_f1b=w_f1b, w_f2=w_f2, c_tie=c_tie)


def const_host():
    i64 = np.eye(64, dtype=BF)
    return dict(c_ident=np.eye(128, dtype=BF),
                c_ident2=np.ascontiguousarray(np.concatenate([i64, i64], axis=0)))


FFN0_DRAM_SPECS = [
    ("w_ctrl", [8, 128, 128], F32),
    ("w_f1", [128, 128, 8, 32], F32),
    ("w_f1b", [32, 128], F32),
    ("w_f2", [32, 128, 1024], F32),
    ("c_tie", [128, 512], F32),
]


def build_ffn0_launch():
    nc = bass.Bass("TRN2", target_bir_lowering=False, detect_race_conditions=False)
    dram = {}
    dram["x_in"] = nc.dram_tensor("x_in", [D, T], F32, kind="ExternalInput").ap()
    dram["x_out"] = nc.dram_tensor("x_out", [D, T], F32, kind="ExternalOutput").ap()
    dram["c_ident"] = nc.dram_tensor("c_ident", [128, 128], BF16, kind="ExternalInput").ap()
    dram["c_ident2"] = nc.dram_tensor("c_ident2", [128, 64], BF16, kind="ExternalInput").ap()
    for nm, shape, dt in FFN0_DRAM_SPECS:
        dram[nm] = nc.dram_tensor(nm, shape, dt, kind="ExternalInput").ap()
    park = nc.dram_tensor("park", [D, T], F32).ap()
    intd = nc.dram_tensor("intd", [128, 16, 1024], BF16).ap()
    ctx = Ctx(nc, dram)
    with ctx.root:
        emit_const_setup(ctx)
        emit_ffn0(ctx, dram["x_in"], dram["x_out"], park, intd)
    return nc




def emit_att0a(ctx, xin_d, qT_d, kT_d, vhm_d):
    nc = ctx.nc
    dram = ctx.dram
    with ctx.scoped():
        xT = ctx.sb("a0_xT", [128, NCH, T], F32)
        xn = ctx.sb("a0_xn", [128, NCH, T], BF16)
        sx = ctx.sem("a0_sx")
        with nc.Block() as blk:
            @blk.sync
            def _(sy):
                sy.dma_start(out=xT[:, :, :],
                             in_=xin_d.rearrange("(c p) t -> p c t", p=128)) \
                    .then_inc(sx, 16)
                sy.wait_ge(sx, ctx.inc(sx, 16))
        emit_ln(ctx, xT, xn, T)

    with ctx.scoped():
        wqk = [ctx.sb("a0_wqk", [128, NCH, 128], BF16) for _ in range(2)]
        ctab = ctx.sb("a0_cos", [128, T], F32)
        stab = ctx.sb("a0_sin", [128, T], F32)
        qa = ctx.sb("a0_qa", [128, T], F32)
        qs = ctx.sb("a0_qs", [128, T], F32)
        qrot = ctx.sb("a0_qrot", [128, T], BF16)
        swl = ctx.sem("a0_wl")
        smm = ctx.sem("a0_mm")
        sev = ctx.sem("a0_ev")
        srp = ctx.sem("a0_rp")
        sst = ctx.sem("a0_st")
        b_wl, b_mm, b_ev = ctx.cnt(swl), ctx.cnt(smm), ctx.cnt(sev)
        b_rp, b_st = ctx.cnt(srp), ctx.cnt(sst)
        with ctx.scoped():
            pq = [ctx.ps("a0_pq", [128, 512], F32) for _ in range(4)]
            with nc.Block() as blk:
                @blk.gpsimd
                def _(g):
                    g.dma_start(out=ctab[:, :], in_=dram["c_cos"][:, :]).then_inc(swl, 16)
                    g.dma_start(out=stab[:, :], in_=dram["c_sin"][:, :]).then_inc(swl, 16)
                    for m in range(16):
                        if m >= 2:
                            g.wait_ge(smm, b_mm + 4 * (m - 1))
                        g.dma_start(out=wqk[m % 2][:, :, :],
                                    in_=dram["w_qk"][m, :, :, :].transpose([1, 0, 2])).then_inc(swl, 16)
                ctx.inc(swl, 18 * 16)

                @blk.tensor
                def _(t):
                    for m in range(16):
                        t.wait_ge(swl, b_wl + 32 + 16 * (m + 1))
                        if m >= 1:
                            t.wait_ge(srp, b_rp + m)
                        for tb in range(4):
                            for ch in range(NCH):
                                last = t.matmul(pq[tb % 4][:, :], wqk[m % 2][:, ch, :],
                                                xn[:, ch, tb * 512:(tb + 1) * 512],
                                                start=(ch == 0), stop=(ch == NCH - 1))
                            last.then_inc(smm, 1)
                ctx.inc(smm, 64)

                @blk.vector
                def _(v):
                    for m in range(16):
                        for tb in range(4):
                            v.wait_ge(smm, b_mm + 4 * m + tb + 1)
                            v.tensor_copy(qa[:, tb * 512:(tb + 1) * 512],
                                          pq[tb % 4][:, :])
                        if m >= 1:
                            v.wait_ge(sst, b_st + 16 * m)
                        for j in range(4):
                            src = (j // 2) * 64 + (1 - j % 2) * 32
                            dst = (j // 2) * 64 + (j % 2) * 32
                            v.tensor_copy(qs[dst:dst + 32, :], qa[src:src + 32, :])
                        v.tensor_mul(qs[:, :], qs[:, :], stab[:, :])
                        v.tensor_mul(qa[:, :], qa[:, :], ctab[:, :])
                        v.tensor_add(qrot[:, :], qa[:, :], qs[:, :]).then_inc(srp, 1)
                ctx.inc(srp, 16)

                @blk.sync
                def _(sy):
                    for m in range(16):
                        sy.wait_ge(srp, b_rp + m + 1)
                        dst = qT_d if m < 8 else kT_d
                        mm = m % 8
                        sy.dma_start(out=dst[mm * 128:(mm + 1) * 128, :],
                                     in_=qrot[:, :]).then_inc(sst, 16)
                        sy.wait_ge(sst, b_st + 16 * (m + 1))
                ctx.inc(sst, 16 * 16)

        with ctx.scoped():
            wv = ctx.sb("a0_wv", [128, NCH, 1024], BF16)
            vtile = [ctx.sb("a0_vt", [128, 16, 65], BF16) for _ in range(2)]
            pv = [ctx.ps("a0_pv", [128, 512], F32) for _ in range(4)]
            b_wl2, b_mm2, b_ev2 = ctx.cnt(swl), ctx.cnt(smm), ctx.cnt(sev)
            b_st2 = ctx.cnt(sst)
            with nc.Block() as blk:
                @blk.gpsimd
                def _(g):
                    g.dma_start(out=wv[:, :, :], in_=dram["w_v"].transpose([1, 0, 2])) \
                        .then_inc(swl, 16)
                ctx.inc(swl, 16)

                @blk.scalar
                def _(s):
                    pass

                @blk.tensor
                def _(t):
                    t.wait_ge(swl, b_wl2 + 16)
                    for it in range(NT):
                        if it >= 2:
                            t.wait_ge(sev, b_ev2 + it - 1)
                        for nb in range(2):
                            for ch in range(NCH):
                                last = t.matmul(pv[2 * (it % 2) + nb][:, :],
                                                xn[:, ch, it * 128:(it + 1) * 128],
                                                wv[:, ch, nb * 512:(nb + 1) * 512],
                                                start=(ch == 0), stop=(ch == NCH - 1))
                            last.then_inc(smm, 1)
                ctx.inc(smm, NT * 2)

                @blk.vector
                def _(v):
                    v.memset(vtile[0][:, :, 64:65], 1.0)
                    v.memset(vtile[1][:, :, 64:65], 1.0)
                    for it in range(NT):
                        if it >= 2:
                            v.wait_ge(sst, b_st2 + 16 * (it - 1))
                        for nb in range(2):
                            v.wait_ge(smm, b_mm2 + 2 * it + nb + 1)
                            last = v.tensor_copy(
                                vtile[it % 2][:, nb * 8:(nb + 1) * 8, 0:64],
                                pv[2 * (it % 2) + nb][:, :]
                                .rearrange("p (h e) -> p h e", e=64))
                        last.then_inc(sev, 1)
                ctx.inc(sev, NT)

                @blk.sync
                def _(sy):
                    for it in range(NT):
                        sy.wait_ge(sev, b_ev2 + it + 1)
                        sy.dma_start(
                            out=vhm_d[:, :, it, :].transpose([1, 0, 2]),
                            in_=vtile[it % 2][:, :, :],
                        ).then_inc(sst, 16)
                    sy.wait_ge(sst, b_st2 + 16 * NT)
                ctx.inc(sst, 16 * NT)


def ident_of(ctx):
    return ctx.const["ident"][:, :]


def prep_att0a_host(w_qkv, s0):
    w = np.asarray(w_qkv, np.float32)
    perm = np.zeros(1024, np.int64)
    for h in range(16):
        for j in range(64):
            perm[h * 64 + j] = h * 64 + (2 * j if j < 32 else 2 * (j - 32) + 1)
    wq = w[:, 0:1024][:, perm]
    wk = w[:, 1024:2048][:, perm]
    wv = w[:, 2048:3072]
    def chunks(a):
        return np.ascontiguousarray(
            a.reshape(8, 128, 8, 128).transpose(2, 0, 1, 3))
    w_qk = np.concatenate([chunks(wq), chunks(wk)], axis=0)
    w_v = np.ascontiguousarray(wv.reshape(8, 128, 1024))
    i = np.arange(32, dtype=np.float64)
    theta = (10000.0 ** (-(2 * i) / 64.0)) * 3.0
    pos = np.arange(s0, s0 + T, dtype=np.float64)
    ang = pos[:, None] * theta[None, :]
    cos = np.cos(ang).astype(np.float32).T
    sin = np.sin(ang).astype(np.float32).T
    blk64_c = np.concatenate([cos, cos], axis=0)
    blk64_s = np.concatenate([-sin, sin], axis=0)
    c_cos = np.ascontiguousarray(np.tile(blk64_c, (2, 1)))
    c_sin = np.ascontiguousarray(np.tile(blk64_s, (2, 1)))
    return dict(w_qk=w_qk, w_v=w_v, c_cos=c_cos, c_sin=c_sin)


ATT0A_DRAM_SPECS = [
    ("w_qk", [16, 8, 128, 128], F32),
    ("w_v", [8, 128, 1024], F32),
    ("c_cos", [128, T], F32),
    ("c_sin", [128, T], F32),
]


def build_att0a_launch():
    nc = bass.Bass("TRN2", target_bir_lowering=False, detect_race_conditions=False)
    dram = {}
    dram["x_in"] = nc.dram_tensor("x_in", [D, T], F32, kind="ExternalInput").ap()
    dram["c_ident"] = nc.dram_tensor("c_ident", [128, 128], BF16, kind="ExternalInput").ap()
    dram["c_ident2"] = nc.dram_tensor("c_ident2", [128, 64], BF16, kind="ExternalInput").ap()
    for nm, shape, dt in ATT0A_DRAM_SPECS:
        dram[nm] = nc.dram_tensor(nm, shape, dt, kind="ExternalInput").ap()
    qT = nc.dram_tensor("qT", [D, T], BF16, kind="ExternalOutput").ap()
    kT = nc.dram_tensor("kT", [D, T], BF16, kind="ExternalOutput").ap()
    vhm = nc.dram_tensor("vhm", [16, 128, NT, 65], BF16, kind="ExternalOutput").ap()
    ctx = Ctx(nc, dram)
    with ctx.root:
        emit_const_setup(ctx)
        emit_att0a(ctx, dram["x_in"], qT, kT, vhm)
    return nc


def emit_att0b(ctx, xin_d, qT_d, kT_d, vhm_d, xout_d):
    nc = ctx.nc
    dram = ctx.dram
    NKC = S // 1024
    onesrow = ctx.const["ones_row"]
    with ctx.scoped():
        attn_T = ctx.sb("a0b_at", [128, NCH, T], BF16)
        inner_scope = ctx.scoped()
        inner_scope.__enter__()
        qTc = ctx.sb("a0b_q", [128, NCH, T], BF16)
        out_sb = ctx.sb("a0b_os", [128, 16, 512], F32)
        kTc = [ctx.sb("a0b_k", [128, NCH, 1024], BF16) for _ in range(2)]
        vga = [ctx.sb("a0b_v", [128, 16, 8, 65], BF16) for _ in range(2)]
        Pt = [ctx.sb("a0b_pt", [128, 1024], BF16) for _ in range(2)]
        rcp = ctx.sb("a0b_rcp", [1, 512], F32)
        sx = ctx.sem("a0b_sx")
        skv = ctx.sem("a0b_kv")
        sS = ctx.sem("a0b_S")
        sE = ctx.sem("a0b_E")
        sP = ctx.sem("a0b_P")
        sO = ctx.sem("a0b_O")
        sN = ctx.sem("a0b_N")
        sB = ctx.sem("a0b_B")
        sM = ctx.sem("a0b_M")

        with nc.Block() as blk:
            @blk.sync
            def _(sy):
                sy.dma_start(out=qTc[:, :, :],
                             in_=qT_d.rearrange("(hp p) q -> p hp q", p=128)) \
                    .then_inc(sx, 16)
                sy.wait_ge(sx, ctx.inc(sx, 16))

        for qb in range(4):
            q0 = qb * 512
            with ctx.scoped():
                pS = [ctx.ps("a0b_pS", [128, 1024], F32) for _ in range(2)]
                pO = [ctx.ps("a0b_pO", [128, 512], F32) for _ in range(2)]
                b_kv, b_S, b_E = ctx.cnt(skv), ctx.cnt(sS), ctx.cnt(sE)
                b_P, b_O = ctx.cnt(sP), ctx.cnt(sO)
                with nc.Block() as blk:
                    @blk.gpsimd
                    def _(g):
                        for kc in range(NKC):
                            if kc >= 2:
                                g.wait_ge(sP, b_P + (kc - 1) * 64)
                            r, lp = kc // 2, (kc % 2) * 1024
                            g.dma_start(out=kTc[kc % 2][:, :, :],
                                        in_=kT_d[r, :, lp:lp + 1024]
                                        .rearrange("(hp p) k -> p hp k", p=128)) \
                                .then_inc(skv, 16)
                            g.dma_start(out=vga[kc % 2][:, :, :, :],
                                        in_=vhm_d[r, :, :, (kc % 2) * 8:
                                                  (kc % 2) * 8 + 8, :]
                                        .transpose([1, 0, 2, 3])) \
                                .then_inc(skv, 16)
                    ctx.inc(skv, 32 * NKC)

                    @blk.tensor
                    def _(t):
                        for kc in range(NKC):
                            t.wait_ge(skv, b_kv + 32 * (kc + 1))
                            for h in range(16):
                                hb = (h % 2) * 64
                                hp = h // 2
                                idx = kc * 16 + h
                                for u in range(4):
                                    j = 4 * idx + u
                                    if j >= 2:
                                        t.wait_ge(sE, b_E + j - 1)
                                    for kt in range(2):
                                        ktile = 2 * u + kt
                                        last = t.matmul(
                                            pS[j % 2][:, kt * 512:(kt + 1) * 512],
                                            kTc[kc % 2][hb:hb + 64, hp,
                                                        ktile * 128:(ktile + 1) * 128],
                                            qTc[hb:hb + 64, hp, q0:q0 + 512],
                                            start=True, stop=True)
                                    last.then_inc(sS, 1)
                                    if u >= 1:
                                        jp = j - 1
                                        up = u - 1
                                        t.wait_ge(sE, b_E + jp + 1)
                                        if idx >= 2 and up == 0:
                                            t.wait_ge(sO, b_O + idx - 1)
                                        for kt in range(2):
                                            ktile = 2 * up + kt
                                            last = t.matmul(
                                                pO[h % 2][0:65, :],
                                                vga[kc % 2][:, h, ktile, :],
                                                Pt[jp % 2][:, kt * 512:(kt + 1) * 512],
                                                start=(up == 0 and kt == 0),
                                                stop=False)
                                        last.then_inc(sP, 1)
                                j = 4 * idx + 3
                                t.wait_ge(sE, b_E + j + 1)
                                for kt in range(2):
                                    ktile = 6 + kt
                                    last = t.matmul(
                                        pO[h % 2][0:65, :],
                                        vga[kc % 2][:, h, ktile, :],
                                        Pt[j % 2][:, kt * 512:(kt + 1) * 512],
                                        start=False, stop=(kt == 1))
                                last.then_inc(sP, 1)
                    ctx.inc(sS, NKC * 64)
                    ctx.inc(sP, NKC * 64)

                    @blk.scalar
                    def _(s):
                        for kc in range(NKC):
                            for h in range(16):
                                idx = kc * 16 + h
                                for u in range(4):
                                    j = 4 * idx + u
                                    s.wait_ge(sS, b_S + j + 1)
                                    if j >= 2:
                                        s.wait_ge(sP, b_P + j - 1)
                                    s.activation(Pt[j % 2][:, :], pS[j % 2][:, :],
                                                 AF.Exp, scale=np.float32(0.125)) \
                                        .then_inc(sE, 1)
                    ctx.inc(sE, NKC * 64)

                    @blk.vector
                    def _(v):
                        for kc in range(NKC):
                            for h in range(16):
                                idx = kc * 16 + h
                                v.wait_ge(sP, b_P + 4 * idx + 4)
                                if kc == 0:
                                    v.tensor_copy(out_sb[0:65, h, :],
                                                  pO[h % 2][0:65, :]).then_inc(sO, 1)
                                else:
                                    v.tensor_add(out_sb[0:65, h, :],
                                                 out_sb[0:65, h, :],
                                                 pO[h % 2][0:65, :]).then_inc(sO, 1)
                    ctx.inc(sO, NKC * 16)

            with ctx.scoped():
                pB = [ctx.ps("a0b_pB", [64, 512], F32) for _ in range(2)]
                b_N, b_B, b_M = ctx.cnt(sN), ctx.cnt(sB), ctx.cnt(sM)
                with nc.Block() as blk:
                    @blk.vector
                    def _(v):
                        for h in range(16):
                            v.reciprocal(rcp[:, :], out_sb[64:65, h, :]).then_inc(sN, 1)
                            v.wait_ge(sB, b_B + h + 1)
                            v.tensor_mul(
                                attn_T[(h % 2) * 64:(h % 2) * 64 + 64, h // 2,
                                       q0:q0 + 512],
                                out_sb[0:64, h, :], pB[h % 2][:, :]).then_inc(sM, 1)
                    ctx.inc(sN, 16)
                    ctx.inc(sM, 16)

                    @blk.tensor
                    def _(t):
                        for h in range(16):
                            t.wait_ge(sN, b_N + h + 1)
                            if h >= 2:
                                t.wait_ge(sM, b_M + h - 1)
                            t.matmul(pB[h % 2][:, :], onesrow[0:1, 0:64],
                                     rcp[:, :], start=True, stop=True).then_inc(sB, 1)
                    ctx.inc(sB, 16)

        inner_scope.__exit__(None, None, None)

        with ctx.scoped():
            wp = ctx.sb("a0b_wp", [128, NCH, 1024], BF16)
            bcol = ctx.sb("a0b_bc", [128, NCH], F32)
            xb1 = [ctx.sb("a0b_x1", [128, NCH, 512], F32) for _ in range(2)]
            po = [ctx.ps("a0b_po", [128, 512], F32) for _ in range(2)]
            swp = ctx.sem("a0b_wp")
            sxl = ctx.sem("a0b_xl")
            som = ctx.sem("a0b_om")
            soe = ctx.sem("a0b_oe")
            sos = ctx.sem("a0b_ost")
            b_wp, b_xl, b_om = ctx.cnt(swp), ctx.cnt(sxl), ctx.cnt(som)
            b_oe, b_os = ctx.cnt(soe), ctx.cnt(sos)
            with nc.Block() as blk:
                @blk.gpsimd
                def _(g):
                    g.dma_start(out=wp[:, :, :],
                                in_=dram["w_proj"].transpose([1, 0, 2])).then_inc(swp, 16)
                    g.dma_start(out=bcol[:, :],
                                in_=dram["b_proj"][:, :]).then_inc(swp, 16)
                    for tb in range(4):
                        if tb >= 2:
                            g.wait_ge(sos, b_os + 16 * (tb - 1))
                        g.dma_start(out=xb1[tb % 2][:, :, :],
                                    in_=xin_d.rearrange("(c p) t -> p c t", p=128)
                                    [:, :, tb * 512:(tb + 1) * 512]).then_inc(sxl, 16)
                ctx.inc(swp, 32)
                ctx.inc(sxl, 64)

                @blk.tensor
                def _(t):
                    t.wait_ge(swp, b_wp + 32)
                    for tb in range(4):
                        for m in range(NCH):
                            idx = tb * NCH + m
                            if idx >= 2:
                                t.wait_ge(soe, b_oe + idx - 1)
                            for ch in range(NCH):
                                last = t.matmul(po[idx % 2][:, :],
                                                wp[:, ch, m * 128:(m + 1) * 128],
                                                attn_T[:, ch, tb * 512:(tb + 1) * 512],
                                                start=(ch == 0), stop=(ch == NCH - 1))
                            last.then_inc(som, 1)
                ctx.inc(som, 32)

                @blk.vector
                def _(v):
                    for tb in range(4):
                        v.wait_ge(sxl, b_xl + 16 * (tb + 1))
                        for m in range(NCH):
                            idx = tb * NCH + m
                            v.wait_ge(som, b_om + idx + 1)
                            v.scalar_tensor_tensor(
                                xb1[tb % 2][:, m, :], po[idx % 2][:, :],
                                bcol[:, m:m + 1], xb1[tb % 2][:, m, :],
                                ALU.add, ALU.add).then_inc(soe, 1)
                ctx.inc(soe, 32)

                @blk.sync
                def _(sy):
                    for tb in range(4):
                        sy.wait_ge(soe, b_oe + NCH * (tb + 1))
                        sy.dma_start(
                            out=xout_d.rearrange("(c p) t -> p c t", p=128)
                            [:, :, tb * 512:(tb + 1) * 512],
                            in_=xb1[tb % 2][:, :, :]).then_inc(sos, 16)
                    sy.wait_ge(sos, b_os + 64)
                ctx.inc(sos, 64)


def prep_att0b_host(w_proj, b_proj):
    wp = np.ascontiguousarray(np.asarray(w_proj, np.float32).reshape(8, 128, 1024))
    bc = np.ascontiguousarray(np.asarray(b_proj, np.float32).reshape(8, 128).T)
    return dict(w_proj=wp, b_proj=bc)


ATT0B_DRAM_SPECS = [
    ("w_proj", [8, 128, 1024], F32),
    ("b_proj", [128, 8], F32),
]


def build_att0b_launch():
    nc = bass.Bass("TRN2", target_bir_lowering=False, detect_race_conditions=False)
    dram = {}
    dram["x_in"] = nc.dram_tensor("x_in", [D, T], F32, kind="ExternalInput").ap()
    dram["qT"] = nc.dram_tensor("qT", [D, T], BF16, kind="ExternalInput").ap()
    dram["kT"] = nc.dram_tensor("kT", [4, D, T], BF16, kind="ExternalInput").ap()
    dram["vhm"] = nc.dram_tensor("vhm", [4, 16, 128, NT, 65], BF16,
                                 kind="ExternalInput").ap()
    dram["c_ident"] = nc.dram_tensor("c_ident", [128, 128], BF16, kind="ExternalInput").ap()
    dram["c_ident2"] = nc.dram_tensor("c_ident2", [128, 64], BF16, kind="ExternalInput").ap()
    for nm, shape, dt in ATT0B_DRAM_SPECS:
        dram[nm] = nc.dram_tensor(nm, shape, dt, kind="ExternalInput").ap()
    xout = nc.dram_tensor("x_out", [D, T], F32, kind="ExternalOutput").ap()
    ctx = Ctx(nc, dram)
    with ctx.root:
        emit_const_setup(ctx)
        emit_att0b(ctx, dram["x_in"], dram["qT"], dram["kT"], dram["vhm"], xout)
    return nc


def emit_ffn1_spec(ctx, etype, i, hbuf1, hbuf2, TB, s0b, l0g, l0b, s1ab, s1bb,
                   l1g, l1b, s2b, phase):
    nc = ctx.nc
    dram = ctx.dram
    onesbf = ctx.const["ones_bf"]
    onesrow = ctx.const["ones_row"]
    onescol = ctx.const["ones_col"]
    NB = TB // 512

    def dense(dst, src_buf, nk, w_dram_idx, w_name, nm_out, bias_cols, post,
              ln_g=None, ln_b=None):
        with ctx.scoped():
            ww = [ctx.sb("f1_sw", [128, nk, 128], BF16) for _ in range(2)]
            pp = [ctx.ps("f1_sp", [128, 512], F32) for _ in range(2)]
            swl = ctx.sem("f1_wl")
            smm = ctx.sem("f1_mm")
            sev = ctx.sem("f1_ev")
            b_wl, b_mm, b_ev = ctx.cnt(swl), ctx.cnt(smm), ctx.cnt(sev)
            with nc.Block() as blk:
                @blk.gpsimd
                def _(g):
                    for m in range(nm_out):
                        if m >= 2:
                            g.wait_ge(smm, b_mm + NB * (m - 1))
                        g.dma_start(out=ww[m % 2][:, :, :],
                                    in_=dram[w_name][w_dram_idx, m, :, :, :]
                                    .transpose([1, 0, 2])).then_inc(swl, 16)
                ctx.inc(swl, nm_out * 16)

                @blk.tensor
                def _(t):
                    for m in range(nm_out):
                        t.wait_ge(swl, b_wl + 16 * (m + 1))
                        for nb in range(NB):
                            k = NB * m + nb
                            if k >= 2:
                                t.wait_ge(sev, b_ev + k - 1)
                            for ch in range(nk):
                                last = t.matmul(
                                    pp[k % 2][:, :], ww[m % 2][:, ch, :],
                                    src_buf[:, ch, nb * 512:(nb + 1) * 512],
                                    start=(ch == 0), stop=(ch == nk - 1))
                            last.then_inc(smm, 1)
                ctx.inc(smm, nm_out * NB)

                @blk.scalar
                def _(s):
                    for m in range(nm_out):
                        for nb in range(NB):
                            k = NB * m + nb
                            s.wait_ge(smm, b_mm + k + 1)
                            s.activation(
                                dst[:, m, nb * 512:(nb + 1) * 512],
                                pp[k % 2][:, :],
                                AF.Gelu if post == "gelu" else AF.Identity,
                                bias=bias_cols[:, i, m:m + 1],
                                scale=onescol[:, :]).then_inc(sev, 1)
                ctx.inc(sev, nm_out * NB)

    def lnorm(buf, nm, g_cols, b_cols, gelu_after):
        F = 128 * nm
        with ctx.scoped():
            sq = [ctx.sb("f1_lsq", [128, 512], BF16) for _ in range(2)]
            stat_s = [ctx.ps("f1_lss", [1, 512], F32) for _ in range(NB)]
            stat_q = [ctx.ps("f1_lsq2", [1, 512], F32) for _ in range(NB)]
            rsum = ctx.sb("f1_lrs", [1, TB], F32)
            rsq = ctx.sb("f1_lrq", [1, TB], F32)
            rrow = ctx.sb("f1_lr", [1, TB], F32)
            prow = ctx.sb("f1_lp", [1, TB], F32)
            rB = ctx.sb("f1_lrB", [128, TB], F32)
            pB = ctx.sb("f1_lpB", [128, TB], F32)
            tmp = ctx.sb("f1_ltmp", [128, 512], F32)
            bmat = [ctx.ps("f1_lbm", [128, 512], F32) for _ in range(2)]
            ssq = ctx.sem("f1_lq")
            smt = ctx.sem("f1_lmt")
            sse = ctx.sem("f1_lse")
            b_q, b_mt, b_se = ctx.cnt(ssq), ctx.cnt(smt), ctx.cnt(sse)
            with nc.Block() as blk:
                @blk.scalar
                def _(s):
                    for m in range(nm):
                        for nb in range(NB):
                            k = NB * m + nb
                            if k >= 2:
                                s.wait_ge(smt, b_mt + k - 1)
                            s.activation(sq[k % 2][:, :],
                                         buf[:, m, nb * 512:(nb + 1) * 512],
                                         AF.Square).then_inc(ssq, 1)
                ctx.inc(ssq, nm * NB)

                @blk.tensor
                def _(t):
                    for m in range(nm):
                        for nb in range(NB):
                            k = NB * m + nb
                            t.wait_ge(ssq, b_q + k + 1)
                            t.matmul(stat_s[nb][:, :], onesbf[:, :],
                                     buf[:, m, nb * 512:(nb + 1) * 512],
                                     start=(m == 0), stop=(m == nm - 1))
                            last = t.matmul(stat_q[nb][:, :], onesbf[:, :],
                                            sq[k % 2][:, :],
                                            start=(m == 0), stop=(m == nm - 1))
                            last.then_inc(smt, 1)
                ctx.inc(smt, nm * NB)

                @blk.vector
                def _(v):
                    v.wait_ge(smt, b_mt + nm * NB)
                    for nb in range(NB):
                        v.tensor_copy(rsum[:, nb * 512:(nb + 1) * 512], stat_s[nb][:, :])
                        v.tensor_copy(rsq[:, nb * 512:(nb + 1) * 512], stat_q[nb][:, :])
                    v.nop().then_inc(sse, 1)
            ctx.inc(sse, 1)

            s1 = ctx.sem("f1_ls1")
            b_s1 = ctx.cnt(s1)
            with nc.Block() as blk:
                @blk.scalar
                def _(s):
                    s.activation(prow[:, :], rsum[:, :], AF.Square,
                                 bias=np.float32(0.0),
                                 scale=np.float32(1.0 / F)).then_inc(s1, 1)

                @blk.vector
                def _(v):
                    v.wait_ge(s1, ctx.inc(s1))
                    v.scalar_tensor_tensor(rrow[:, :], rsq[:, :], 1.0 / F, prow[:, :],
                                           ALU.mult, ALU.subtract).then_inc(s1, 1)

                @blk.scalar
                def _(s):
                    s.wait_ge(s1, ctx.inc(s1))
                    s.activation(rrow[:, :], rrow[:, :], AF.Sqrt,
                                 bias=np.float32(EPS_LN),
                                 scale=np.float32(1.0)).then_inc(s1, 1)

                @blk.vector
                def _(v):
                    sv = SerialEng(ctx, v, ctx.sem("f1_lsv"))
                    v.wait_ge(s1, ctx.inc(s1))
                    sv.reciprocal(rrow[:, :], rrow[:, :])
                    sv.tensor_scalar_mul(prow[:, :], rsum[:, :], 1.0 / F)
                    sv.tensor_mul(prow[:, :], prow[:, :], rrow[:, :])
                    sv.tensor_scalar_mul(prow[:, :], prow[:, :], -1.0)

            b_mt2, b_se2 = ctx.cnt(smt), ctx.cnt(sse)
            with nc.Block() as blk:
                @blk.tensor
                def _(t):
                    for j, row in enumerate((rrow, prow)):
                        for nb in range(NB):
                            k = j * NB + nb
                            if k >= 2:
                                t.wait_ge(sse, b_se2 + k - 1)
                            t.matmul(bmat[k % 2][:, :], onesrow[:, :],
                                     row[:, nb * 512:(nb + 1) * 512],
                                     start=True, stop=True).then_inc(smt, 1)
                ctx.inc(smt, 2 * NB)

                @blk.vector
                def _(v):
                    for j, dstb in enumerate((rB, pB)):
                        for nb in range(NB):
                            k = j * NB + nb
                            v.wait_ge(smt, b_mt2 + k + 1)
                            v.tensor_copy(dstb[:, nb * 512:(nb + 1) * 512],
                                          bmat[k % 2][:, :]).then_inc(sse, 1)
                ctx.inc(sse, 2 * NB)

            if getattr(ctx, "_lnorm_dbg", False):
                sdbg = ctx.sem("f1_dbg")
                with nc.Block() as blk:
                    @blk.sync
                    def _(sy):
                        sy.dma_start(out=ctx.dbg["rows"][0:1, :], in_=rsum[:, :]).then_inc(sdbg, 16)
                        sy.dma_start(out=ctx.dbg["rows"][1:2, :], in_=rsq[:, :]).then_inc(sdbg, 16)
                        sy.dma_start(out=ctx.dbg["rows"][2:3, :], in_=rrow[:, :]).then_inc(sdbg, 16)
                        sy.dma_start(out=ctx.dbg["rows"][3:4, :], in_=prow[:, :]).then_inc(sdbg, 16)
                        sy.wait_ge(sdbg, ctx.inc(sdbg, 64))
            sap = ctx.sem("f1_lap")
            b_ap = ctx.cnt(sap)
            with nc.Block() as blk:
                @blk.vector
                def _(v):
                    for m in range(nm):
                        for nb in range(NB):
                            sl = slice(nb * 512, (nb + 1) * 512)
                            v.tensor_mul(tmp[:, :], buf[:, m, sl], rB[:, sl])
                            last = v.tensor_add(tmp[:, :], tmp[:, :], pB[:, sl])
                            if gelu_after:
                                last.then_inc(sap, 1)
                            else:
                                v.tensor_scalar(buf[:, m, sl], tmp[:, :],
                                                g_cols[:, i, m:m + 1],
                                                b_cols[:, i, m:m + 1],
                                                ALU.mult, ALU.add)
                            if gelu_after:
                                v.wait_ge(sap, b_ap + 2 * (NB * m + nb) + 2)
                ctx.inc(sap, nm * NB * (1 if gelu_after else 0))

                if gelu_after:
                    @blk.scalar
                    def _(s):
                        for m in range(nm):
                            for nb in range(NB):
                                k = NB * m + nb
                                sl = slice(nb * 512, (nb + 1) * 512)
                                s.wait_ge(sap, b_ap + 2 * k + 1)
                                s.activation(buf[:, m, sl], tmp[:, :], AF.Gelu,
                                             bias=b_cols[:, i, m:m + 1],
                                             scale=g_cols[:, i, m:m + 1]) \
                                    .then_inc(sap, 1)
                    ctx.inc(sap, nm * NB)

    if phase == "dense":
        if etype == "s0":
            dense(hbuf2, hbuf1, 32, i, "f1_s0_W", 32, s0b, "copy")
        elif etype == "s2":
            dense(hbuf2, hbuf1, 32, i, "f1_s2_W", 32, s2b, "gelu")
        else:
            with ctx.scoped():
                t1buf = ctx.sb("f1_t1", [128, 16, TB], BF16)
                dense(t1buf, hbuf1, 32, i, "f1_s1a_W", 16, s1ab, "gelu")
                dense(hbuf1, t1buf, 16, i, "f1_s1b_W", 32, s1bb, "copy")
    else:
        if etype == "s0":
            lnorm(hbuf2, 32, l0g, l0b, gelu_after=True)
        elif etype == "s1":
            lnorm(hbuf2, 32, l1g, l1b, gelu_after=False)


def emit_ffn1_down(ctx, e, tb, t0, TB, hdown, w_sb, y_nat, first):
    nc = ctx.nc
    dram = ctx.dram
    onesrow = ctx.const["ones_row"]
    NB = TB // 512
    NTT = TB // 128
    with ctx.scoped():
        dww = ctx.sb("f1_dww", [128, 32, 512], BF16)
        dnb = ctx.sb("f1_dnb", [1, 512], BF16)
        pd = [ctx.ps("f1_pd", [128, 512], F32) for _ in range(2)]
        swl = ctx.sem("f1_dwl")
        smm = ctx.sem("f1_dmm")
        sev = ctx.sem("f1_dev")
        onesrow_bf = ctx.const["ones_row_bf"]
        for nb2 in range(2):
            b_wl, b_mm, b_ev = ctx.cnt(swl), ctx.cnt(smm), ctx.cnt(sev)
            with nc.Block() as blk:
                @blk.gpsimd
                def _(g):
                    g.dma_start(out=dnb[:, :],
                                in_=dram["f1_down_b"][:, e,
                                                      nb2 * 512:(nb2 + 1) * 512]) \
                        .then_inc(swl, 16)
                    for j in range(4):
                        g.dma_start(out=dww[:, 8 * j:8 * j + 8, :],
                                    in_=dram["f1_down_W"][e, 8 * j:8 * j + 8, :,
                                                          nb2 * 512:(nb2 + 1) * 512]
                                    .transpose([1, 0, 2])).then_inc(swl, 16)
                ctx.inc(swl, 80)

                @blk.tensor
                def _(t):
                    for tile in range(NTT):
                        if tile >= 2:
                            t.wait_ge(sev, b_ev + tile - 1)
                        for kch in range(32):
                            if tile == 0 and kch % 8 == 0:
                                t.wait_ge(swl, b_wl + 16 * (kch // 8 + 2))
                            t.matmul(pd[tile % 2][:, :],
                                     hdown[:, kch, tile * 128:(tile + 1) * 128],
                                     dww[:, kch, :],
                                     start=(kch == 0), stop=False)
                        if tile == 0:
                            t.wait_ge(swl, b_wl + 16)
                        last = t.matmul(pd[tile % 2][:, :], onesrow_bf[:, :],
                                        dnb[:, :],
                                        start=False, stop=True)
                        last.then_inc(smm, 1)
                ctx.inc(smm, NTT)

                @blk.vector
                def _(v):
                    for tile in range(NTT):
                        v.wait_ge(smm, b_mm + tile + 1)
                        gtile = (t0 // 128) + tile
                        wcol = w_sb[:, gtile, e:e + 1]
                        dst = y_nat[:, tile, nb2 * 512:(nb2 + 1) * 512]
                        if first:
                            last = v.tensor_scalar_mul(dst, pd[tile % 2][:, :], wcol)
                        else:
                            last = v.scalar_tensor_tensor(
                                dst, pd[tile % 2][:, :], wcol, dst,
                                ALU.mult, ALU.add)
                        last.then_inc(sev, 1)
                ctx.inc(sev, NTT)


def emit_ffn1_final(ctx, tb, t0, TB, y_nat, xin_d, zpark_d, xout_d):
    nc = ctx.nc
    ident = ctx.const["ident"][:, :]
    NTT = TB // 128
    with ctx.scoped():
        xb = ctx.sb("f1_xb", [128, NCH, 128], F32)
        zb = ctx.sb("f1_zb", [128, NCH, 128], BF16)
        ob = ctx.sb("f1_ob", [128, NCH, 128], F32)
        pt = [ctx.ps("f1_pt", [128, 128], BF16) for _ in range(2)]
        sxl = ctx.sem("f1_fx")
        stt_ = ctx.sem("f1_ft")
        sev = ctx.sem("f1_fe")
        sst = ctx.sem("f1_fs")
        for tile in range(NTT):
            tt = t0 + tile * 128
            b_xl, b_tt, b_ev, b_st = (ctx.cnt(sxl), ctx.cnt(stt_),
                                      ctx.cnt(sev), ctx.cnt(sst))
            with nc.Block() as blk:
                @blk.gpsimd
                def _(g):
                    g.dma_start(out=xb[:, :, :],
                                in_=xin_d.rearrange("(c p) t -> p c t", p=128)
                                [:, :, tt:tt + 128]).then_inc(sxl, 16)
                    g.dma_start(out=zb[:, :, :],
                                in_=zpark_d[:, :, tt:tt + 128]).then_inc(sxl, 16)
                ctx.inc(sxl, 32)

                @blk.tensor
                def _(t):
                    for ch in range(NCH):
                        if ch >= 2:
                            t.wait_ge(sev, b_ev + ch - 1)
                        t.transpose(pt[ch % 2][:, :],
                                    y_nat[:, tile, ch * 128:(ch + 1) * 128],
                                    ident).then_inc(stt_, 1)
                ctx.inc(stt_, NCH)

                @blk.vector
                def _(v):
                    sv = SerialEng(ctx, v, ctx.sem("f1_fsv"))
                    v.wait_ge(sxl, b_xl + 32)
                    for ch in range(NCH):
                        v.wait_ge(stt_, b_tt + ch + 1)
                        sv.scalar_tensor_tensor(xb[:, ch, :], pt[ch % 2][:, :], 0.5,
                                                xb[:, ch, :], ALU.mult, ALU.add)
                        sv.scalar_tensor_tensor(xb[:, ch, :], zb[:, ch, :], 0.5,
                                                xb[:, ch, :], ALU.mult, ALU.add)
                        sv.milestone(sev, 1)
                ctx.inc(sev, NCH)

                @blk.sync
                def _(sy):
                    sy.wait_ge(sev, b_ev + NCH)
                    sy.dma_start(out=xout_d.rearrange("(c p) t -> p c t", p=128)
                                 [:, :, tt:tt + 128],
                                 in_=xb[:, :, :]).then_inc(sst, 16)
                    sy.wait_ge(sst, ctx.inc(sst, 16))


def emit_ffn1(ctx, xin_d, xout_d, zpark_d):
    nc = ctx.nc
    dram = ctx.dram
    onesrow = ctx.const["ones_row"]
    onescol = ctx.const["ones_col"]
    TB = 1024
    NTB = T // TB

    with ctx.scoped():
        w_sb = ctx.sb("f1_w", [128, NT, 8], F32)
        gcol = ctx.sb("f1_g", [128, NCH], F32)
        bcol = ctx.sb("f1_b", [128, NCH], F32)
        upb = ctx.sb("f1_upb", [128, 8, 32], F32)
        s0b = ctx.sb("f1_s0b", [128, 2, 32], F32)
        l0g = ctx.sb("f1_l0g", [128, 2, 32], F32)
        l0b = ctx.sb("f1_l0b", [128, 2, 32], F32)
        s1ab = ctx.sb("f1_s1ab", [128, 2, 16], F32)
        s1bb = ctx.sb("f1_s1bb", [128, 2, 32], F32)
        l1g = ctx.sb("f1_l1g", [128, 2, 32], F32)
        l1b = ctx.sb("f1_l1b", [128, 2, 32], F32)
        s2b = ctx.sb("f1_s2b", [128, 2, 32], F32)
        sld = ctx.sem("f1_ld")

        with nc.Block() as blk:
            @blk.gpsimd
            def _(g):
                n = 0
                for nm, dst in [("f1_ln_g", gcol[:, :]), ("f1_ln_b", bcol[:, :]),
                                ("f1_up_b", upb[:, :, :]),
                                ("f1_s0_b", s0b[:, :, :]), ("f1_ln0_g", l0g[:, :, :]),
                                ("f1_ln0_b", l0b[:, :, :]), ("f1_s1a_b", s1ab[:, :, :]),
                                ("f1_s1b_b", s1bb[:, :, :]), ("f1_ln1_g", l1g[:, :, :]),
                                ("f1_ln1_b", l1b[:, :, :]), ("f1_s2_b", s2b[:, :, :])]:
                    g.dma_start(out=dst, in_=dram[nm]).then_inc(sld, 16)
                    n += 16
                g.wait_ge(sld, ctx.inc(sld, n))

        vrow = ctx.sb("f1_vrow", [1, T], F32)
        with ctx.scoped():
            z = ctx.sb("f1_z", [128, NCH, T], BF16)
            sx = ctx.sem("f1_sx")
            with ctx.scoped():
                xT = ctx.sb("f1_xT", [128, NCH, T], F32)
                with nc.Block() as blk:
                    @blk.sync
                    def _(sy):
                        sy.dma_start(out=xT[:, :, :],
                                     in_=xin_d.rearrange("(c p) t -> p c t", p=128)) \
                            .then_inc(sx, 16)
                        sy.wait_ge(sx, ctx.inc(sx, 16))
                emit_ln(ctx, xT, z, T, var_row=vrow)
            with nc.Block() as blk:
                @blk.sync
                def _(sy):
                    sy.dma_start(out=zpark_d[:, :, :], in_=z[:, :, :]).then_inc(sx, 16)
                    sy.wait_ge(sx, ctx.inc(sx, 16))
            nx = ctx.stacks[-2].enter_context(
                nc.sbuf_tensor(f"f1_nx_{ctx.uid()}", [128, NCH, T], BF16,
                               side="right"))
            ctx.nx_dbg = nx

            srow = ctx.sb("f1_srow", [1, T], F32)
            sB = ctx.sb("f1_sB", [128, T], F32)
            tmp = ctx.sb("f1_tmp", [128, 512], F32)
            s1_ = ctx.sem("f1_s1")
            sm_ = ctx.sem("f1_sm")
            sv_ = ctx.sem("f1_sv")
            with ctx.scoped():
                bmat = [ctx.ps("f1_bm", [128, 512], F32) for _ in range(2)]
                b_s1, b_sv = ctx.cnt(s1_), ctx.cnt(sv_)
                with nc.Block() as blk:
                    @blk.scalar
                    def _(s):
                        s.activation(srow[:, :], vrow[:, :], AF.Sqrt,
                                     bias=np.float32(EPS_LN),
                                     scale=np.float32(1.0)).then_inc(s1_, 1)
                        s.wait_ge(sv_, b_sv + 1)
                        s.activation(srow[:, :], srow[:, :], AF.Sqrt,
                                     bias=np.float32(EPS_LN),
                                     scale=np.float32(1.0)).then_inc(s1_, 1)
                    ctx.inc(s1_, 2)

                    @blk.vector
                    def _(v):
                        sv = SerialEng(ctx, v, ctx.sem("f1_svv"))
                        v.wait_ge(s1_, b_s1 + 1)
                        sv.reciprocal(srow[:, :], srow[:, :])
                        sv.tensor_mul(srow[:, :], srow[:, :], srow[:, :])
                        sv.tensor_mul(srow[:, :], srow[:, :], vrow[:, :])
                        sv.milestone(sv_, 1)
                        v.wait_ge(s1_, b_s1 + 2)
                        sv.reciprocal(srow[:, :], srow[:, :])
                        sv.milestone(sv_, 1)
                    ctx.inc(sv_, 2)

                b_sm2, b_sv2 = ctx.cnt(sm_), ctx.cnt(sv_)
                with nc.Block() as blk:
                    @blk.tensor
                    def _(t):
                        for ib in range(4):
                            if ib >= 2:
                                t.wait_ge(sv_, b_sv2 + ib - 1)
                            t.matmul(bmat[ib % 2][:, :], onesrow[:, :],
                                     srow[:, ib * 512:(ib + 1) * 512],
                                     start=True, stop=True).then_inc(sm_, 1)
                    ctx.inc(sm_, 4)

                    @blk.vector
                    def _(v):
                        for ib in range(4):
                            v.wait_ge(sm_, b_sm2 + ib + 1)
                            v.tensor_copy(sB[:, ib * 512:(ib + 1) * 512],
                                          bmat[ib % 2][:, :]).then_inc(sv_, 1)
                    ctx.inc(sv_, 4)

                with nc.Block() as blk:
                    @blk.vector
                    def _(v):
                        for ch in range(NCH):
                            for ib in range(4):
                                sl = slice(ib * 512, (ib + 1) * 512)
                                v.tensor_mul(tmp[:, :], z[:, ch, sl], sB[:, sl])
                                v.tensor_scalar(nx[:, ch, sl], tmp[:, :],
                                                gcol[:, ch:ch + 1],
                                                bcol[:, ch:ch + 1],
                                                ALU.mult, ALU.add)

        with ctx.scoped():
            rw = ctx.sb("f1_rw", [128, NCH, 8], BF16)
            lg = ctx.sb("f1_lg", [128, 8], F32)
            e_ = ctx.sb("f1_e", [128, 8], F32)
            e2_ = ctx.sb("f1_e2", [128, 8], F32)
            msk = ctx.sb("f1_msk", [128, 8], F32)
            ncol = ctx.sb("f1_ncol", [128, 1], F32)
            mcol = ctx.sb("f1_mcol", [128, 1], F32)
            prt = [ctx.ps("f1_prt", [128, 8], F32) for _ in range(2)]
            srt = ctx.sem("f1_rt")
            sva = ctx.sem("f1_va")
            ssb = ctx.sem("f1_sb")
            sre_ = ctx.sem("f1_re")
            b_rt, b_va, b_sb, b_re = (ctx.cnt(srt), ctx.cnt(sva),
                                      ctx.cnt(ssb), ctx.cnt(sre_))
            with nc.Block() as blk:
                @blk.gpsimd
                def _(g):
                    g.dma_start(out=rw[:, :, :],
                                in_=dram["f1_router_w"].transpose([1, 0, 2])) \
                        .then_inc(sld, 16)
                ctx.inc(sld, 16)

                @blk.tensor
                def _(t):
                    t.wait_ge(sld, ctx.cnt(sld))
                    for it in range(NT):
                        if it >= 2:
                            t.wait_ge(sre_, b_re + it - 1)
                        for ch in range(NCH):
                            last = t.matmul(prt[it % 2][:, :],
                                            nx[:, ch, it * 128:(it + 1) * 128],
                                            rw[:, ch, :],
                                            start=(ch == 0), stop=(ch == NCH - 1))
                        last.then_inc(srt, 1)
                ctx.inc(srt, NT)

                @blk.vector
                def _(v):
                    sv = SerialEng(ctx, v, ctx.sem("f1_vv"))
                    for it in range(NT):
                        v.wait_ge(srt, b_rt + it + 1)
                        sv.tensor_copy(lg[:, :], prt[it % 2][:, :])
                        sv.tensor_reduce(ncol[:, :], lg[:, :], AX.X,
                                         ALU.max, negate=True)
                        sv.milestone(sva, 1)
                        v.wait_ge(ssb, b_sb + it + 1)
                        sv.tensor_reduce(mcol[:, :], e_[:, :], AX.X, ALU.max)
                        sv.tensor_single_scalar(e2_[:, :], e_[:, :], mcol[:, :],
                                                ALU.is_lt)
                        sv.tensor_mul(e2_[:, :], e2_[:, :], e_[:, :])
                        sv.tensor_reduce(mcol[:, :], e2_[:, :], AX.X, ALU.max)
                        sv.tensor_single_scalar(msk[:, :], e_[:, :], mcol[:, :],
                                                ALU.is_ge)
                        sv.tensor_mul(msk[:, :], msk[:, :], e_[:, :])
                        sv.tensor_reduce(mcol[:, :], msk[:, :], AX.X, ALU.add)
                        sv.reciprocal(mcol[:, :], mcol[:, :])
                        sv.tensor_scalar_mul(w_sb[:, it, :], msk[:, :],
                                             mcol[:, :])
                        sv.milestone(sre_, 1)
                ctx.inc(sva, NT)
                ctx.inc(sre_, NT)

                @blk.scalar
                def _(s):
                    for it in range(NT):
                        s.wait_ge(sva, b_va + it + 1)
                        s.activation(e_[:, :], lg[:, :], AF.Exp,
                                     bias=ncol[:, :], scale=onescol[:, :]) \
                            .then_inc(ssb, 1)
                ctx.inc(ssb, NT)

        if getattr(ctx, "dbg", None) is not None:
            sdbg = ctx.sem("f1_dbg")
            with nc.Block() as blk:
                @blk.sync
                def _(sy):
                    sy.dma_start(out=ctx.dbg["nx"], in_=nx[:, :, :]).then_inc(sdbg, 16)
                    sy.dma_start(out=ctx.dbg["w"], in_=w_sb[:, :, :]).then_inc(sdbg, 16)
                    sy.wait_ge(sdbg, ctx.inc(sdbg, 32))

        y_nat = ctx.sb("f1_y", [128, 8, 1024], BF16)
        for tb in range(NTB):
            t0 = tb * TB
            for e in range(8):
                etype = ("s0" if e in (0, 4) else "s1" if e in (1, 5)
                         else "s2" if e in (2, 6) else "id")
                i = 0 if e < 4 else 1
                spec_args = (s0b, l0g, l0b, s1ab, s1bb, l1g, l1b, s2b)
                dbg_h1 = (getattr(ctx, "dbg", None) is not None and tb == 0
                          and e == 0)
                if etype in ("id", "s1"):
                    with ctx.scoped():
                        hbuf1 = ctx.sb("f1_h1", [128, 32, TB], BF16)
                        emit_ffn1_up(ctx, e, t0, TB, nx, upb, hbuf1)
                        if etype == "s1":
                            emit_ffn1_spec(ctx, etype, i, hbuf1, None, TB,
                                           *spec_args, phase="dense")
                            emit_ffn1_spec(ctx, etype, i, None, hbuf1, TB,
                                           *spec_args, phase="post")
                        emit_ffn1_down(ctx, e, tb, t0, TB, hbuf1, w_sb,
                                       y_nat, first=(e == 0))
                else:
                    with ctx.scoped():
                        hbuf2 = ctx.sb("f1_h2", [128, 32, TB], BF16)
                        with ctx.scoped():
                            hbuf1 = ctx.sb("f1_h1", [128, 32, TB], BF16)
                            emit_ffn1_up(ctx, e, t0, TB, nx, upb, hbuf1)
                            if dbg_h1:
                                sdbg = ctx.sem("f1_dbg")
                                with nc.Block() as blk:
                                    @blk.sync
                                    def _(sy):
                                        sy.dma_start(out=ctx.dbg["h1e0"],
                                                     in_=hbuf1[:, :, :]) \
                                            .then_inc(sdbg, 16)
                                        sy.wait_ge(sdbg, ctx.inc(sdbg, 16))
                            emit_ffn1_spec(ctx, etype, i, hbuf1, hbuf2, TB,
                                           *spec_args, phase="dense")
                            if dbg_h1:
                                sdbg = ctx.sem("f1_dbg")
                                with nc.Block() as blk:
                                    @blk.sync
                                    def _(sy):
                                        sy.dma_start(out=ctx.dbg["pre0"],
                                                     in_=hbuf2[:, :, :]) \
                                            .then_inc(sdbg, 16)
                                        sy.wait_ge(sdbg, ctx.inc(sdbg, 16))
                        ctx._lnorm_dbg = dbg_h1
                        emit_ffn1_spec(ctx, etype, i, None, hbuf2, TB,
                                       *spec_args, phase="post")
                        ctx._lnorm_dbg = False
                        if dbg_h1:
                            sdbg = ctx.sem("f1_dbg")
                            with nc.Block() as blk:
                                @blk.sync
                                def _(sy):
                                    sy.dma_start(out=ctx.dbg["s0e0"],
                                                 in_=hbuf2[:, :, :]) \
                                        .then_inc(sdbg, 16)
                                    sy.wait_ge(sdbg, ctx.inc(sdbg, 16))
                        emit_ffn1_down(ctx, e, tb, t0, TB, hbuf2, w_sb,
                                       y_nat, first=(e == 0))
                        if dbg_h1:
                            sdbg = ctx.sem("f1_dbg")
                            with nc.Block() as blk:
                                @blk.sync
                                def _(sy):
                                    sy.dma_start(out=ctx.dbg["ye0"],
                                                 in_=y_nat[:, :, :]) \
                                        .then_inc(sdbg, 16)
                                    sy.wait_ge(sdbg, ctx.inc(sdbg, 16))
            if getattr(ctx, "dbg", None) is not None and tb == 0:
                sdbg = ctx.sem("f1_dbg")
                with nc.Block() as blk:
                    @blk.sync
                    def _(sy):
                        sy.dma_start(out=ctx.dbg["y0"], in_=y_nat[:, :, :]) \
                            .then_inc(sdbg, 16)
                        sy.wait_ge(sdbg, ctx.inc(sdbg, 16))
            emit_ffn1_final(ctx, tb, t0, TB, y_nat, xin_d, zpark_d, xout_d)


def emit_ffn1_up(ctx, e, t0, TB, nx, upb, hbuf1):
    nc = ctx.nc
    dram = ctx.dram
    onescol = ctx.const["ones_col"]
    with ctx.scoped():
        upw = [ctx.sb("f1_upw", [128, NCH, 128], BF16) for _ in range(4)]
        pu = [ctx.ps("f1_pu", [128, 512], F32) for _ in range(2)]
        swl = ctx.sem("f1_wl")
        smm = ctx.sem("f1_mm")
        sev = ctx.sem("f1_ev")
        b_wl, b_mm, b_ev = ctx.cnt(swl), ctx.cnt(smm), ctx.cnt(sev)
        with nc.Block() as blk:
            @blk.gpsimd
            def _(g):
                for m in range(32):
                    if m >= 4:
                        g.wait_ge(smm, b_mm + 2 * (m - 3))
                    g.dma_start(out=upw[m % 4][:, :, :],
                                in_=dram["f1_up_W"][e, m, :, :, :]
                                .transpose([1, 0, 2])).then_inc(swl, 16)
            ctx.inc(swl, 32 * 16)

            @blk.tensor
            def _(t):
                for m in range(32):
                    t.wait_ge(swl, b_wl + 16 * (m + 1))
                    for nb in range(2):
                        k = 2 * m + nb
                        if k >= 2:
                            t.wait_ge(sev, b_ev + k - 1)
                        for ch in range(NCH):
                            last = t.matmul(
                                pu[k % 2][:, :], upw[m % 4][:, ch, :],
                                nx[:, ch, t0 + nb * 512:t0 + (nb + 1) * 512],
                                start=(ch == 0), stop=(ch == NCH - 1))
                        last.then_inc(smm, 1)
            ctx.inc(smm, 64)

            @blk.scalar
            def _(s):
                for m in range(32):
                    for nb in range(2):
                        k = 2 * m + nb
                        s.wait_ge(smm, b_mm + k + 1)
                        s.activation(hbuf1[:, m, nb * 512:(nb + 1) * 512],
                                     pu[k % 2][:, :], AF.Gelu,
                                     bias=upb[:, e, m:m + 1],
                                     scale=onescol[:, :]).then_inc(sev, 1)
            ctx.inc(sev, 64)


def prep_ffn1_host(ins):
    def cols(a, inner):
        a = np.asarray(a, np.float32)
        k = a.shape[0]
        return np.ascontiguousarray(a.reshape(k, a.shape[1] // 128, 128)
                                    .transpose(2, 0, 1))
    d = {}
    d["f1_ln_g"] = np.ascontiguousarray(
        np.asarray(ins["L1_ffn_ln_g"], np.float32).reshape(8, 128).T)
    d["f1_ln_b"] = np.ascontiguousarray(
        np.asarray(ins["L1_ffn_ln_b"], np.float32).reshape(8, 128).T)
    d["f1_router_w"] = np.ascontiguousarray(
        np.asarray(ins["L1_ffn_router_w"], np.float32).reshape(8, 128, 8))
    d["f1_up_W"] = np.ascontiguousarray(
        np.asarray(ins["L1_ffn_up_W"], np.float32)
        .reshape(8, 8, 128, 32, 128).transpose(0, 3, 1, 2, 4))
    d["f1_up_b"] = cols(ins["L1_ffn_up_b"], 128)
    d["f1_down_W"] = np.ascontiguousarray(
        np.asarray(ins["L1_ffn_down_W"], np.float32).reshape(8, 32, 128, 1024))
    d["f1_down_b"] = np.ascontiguousarray(
        np.asarray(ins["L1_ffn_down_b"], np.float32).reshape(1, 8, 1024))
    d["f1_s0_W"] = np.ascontiguousarray(
        np.asarray(ins["L1_ffn_spec0_W"], np.float32)
        .reshape(2, 32, 128, 32, 128).transpose(0, 3, 1, 2, 4))
    d["f1_s0_b"] = cols(ins["L1_ffn_spec0_b"], 128)
    d["f1_ln0_g"] = cols(ins["L1_ffn_ln0_g"], 128)
    d["f1_ln0_b"] = cols(ins["L1_ffn_ln0_b"], 128)
    d["f1_s1a_W"] = np.ascontiguousarray(
        np.asarray(ins["L1_ffn_spec1a_W"], np.float32)
        .reshape(2, 32, 128, 16, 128).transpose(0, 3, 1, 2, 4))
    d["f1_s1a_b"] = cols(ins["L1_ffn_spec1a_b"], 128)
    d["f1_s1b_W"] = np.ascontiguousarray(
        np.asarray(ins["L1_ffn_spec1b_W"], np.float32)
        .reshape(2, 16, 128, 32, 128).transpose(0, 3, 1, 2, 4))
    d["f1_s1b_b"] = cols(ins["L1_ffn_spec1b_b"], 128)
    d["f1_ln1_g"] = cols(ins["L1_ffn_ln1_g"], 128)
    d["f1_ln1_b"] = cols(ins["L1_ffn_ln1_b"], 128)
    d["f1_s2_W"] = np.ascontiguousarray(
        np.asarray(ins["L1_ffn_spec2_W"], np.float32)
        .reshape(2, 32, 128, 32, 128).transpose(0, 3, 1, 2, 4))
    d["f1_s2_b"] = cols(ins["L1_ffn_spec2_b"], 128)
    return d


FFN1_DRAM_SPECS = [
    ("f1_ln_g", [128, 8], F32), ("f1_ln_b", [128, 8], F32),
    ("f1_router_w", [8, 128, 8], F32),
    ("f1_up_W", [8, 32, 8, 128, 128], F32), ("f1_up_b", [128, 8, 32], F32),
    ("f1_down_W", [8, 32, 128, 1024], F32), ("f1_down_b", [1, 8, 1024], F32),
    ("f1_s0_W", [2, 32, 32, 128, 128], F32), ("f1_s0_b", [128, 2, 32], F32),
    ("f1_ln0_g", [128, 2, 32], F32), ("f1_ln0_b", [128, 2, 32], F32),
    ("f1_s1a_W", [2, 16, 32, 128, 128], F32), ("f1_s1a_b", [128, 2, 16], F32),
    ("f1_s1b_W", [2, 32, 16, 128, 128], F32), ("f1_s1b_b", [128, 2, 32], F32),
    ("f1_ln1_g", [128, 2, 32], F32), ("f1_ln1_b", [128, 2, 32], F32),
    ("f1_s2_W", [2, 32, 32, 128, 128], F32), ("f1_s2_b", [128, 2, 32], F32),
]


def build_ffn1_launch(debug=False):
    nc = bass.Bass("TRN2", target_bir_lowering=False, detect_race_conditions=False)
    dram = {}
    dram["x_in"] = nc.dram_tensor("x_in", [D, T], F32, kind="ExternalInput").ap()
    dram["c_ident"] = nc.dram_tensor("c_ident", [128, 128], BF16, kind="ExternalInput").ap()
    dram["c_ident2"] = nc.dram_tensor("c_ident2", [128, 64], BF16, kind="ExternalInput").ap()
    for nm, shape, dt in FFN1_DRAM_SPECS:
        dram[nm] = nc.dram_tensor(nm, shape, dt, kind="ExternalInput").ap()
    xout = nc.dram_tensor("x_out", [D, T], F32, kind="ExternalOutput").ap()
    zpark = nc.dram_tensor("zpark", [128, NCH, T], BF16).ap()
    ctx = Ctx(nc, dram)
    if debug:
        ctx.dbg = {
            "nx": nc.dram_tensor("d_nx", [128, NCH, T], BF16, kind="ExternalOutput").ap(),
            "w": nc.dram_tensor("d_w", [128, NT, 8], F32, kind="ExternalOutput").ap(),
            "y0": nc.dram_tensor("d_y0", [128, 8, 1024], BF16, kind="ExternalOutput").ap(),
            "h1e0": nc.dram_tensor("d_h1e0", [128, 32, 1024], BF16, kind="ExternalOutput").ap(),
            "s0e0": nc.dram_tensor("d_s0e0", [128, 32, 1024], BF16, kind="ExternalOutput").ap(),
            "pre0": nc.dram_tensor("d_pre0", [128, 32, 1024], BF16, kind="ExternalOutput").ap(),
            "rows": nc.dram_tensor("d_rows", [4, 1024], F32, kind="ExternalOutput").ap(),
            "ye0": nc.dram_tensor("d_ye0", [128, 8, 1024], BF16, kind="ExternalOutput").ap(),
        }
    with ctx.root:
        emit_const_setup(ctx)
        emit_ffn1(ctx, dram["x_in"], xout, zpark)
    return nc


def emit_att1a(ctx, xin_d, sqT_d, numden_d, state_d, relp_d):
    nc = ctx.nc
    dram = ctx.dram
    ident = ctx.const["ident"][:, :]
    with ctx.scoped():
        with ctx.scoped():
            z = ctx.sb("a1_z", [128, NCH, T], BF16, side="right")
            with ctx.scoped():
                xT = ctx.sb("a1_xT", [128, NCH, T], F32)
                sx = ctx.sem("a1_sx")
                with nc.Block() as blk:
                    @blk.sync
                    def _(sy):
                        sy.dma_start(out=xT[:, :, :],
                                     in_=xin_d.rearrange("(c p) t -> p c t", p=128)) \
                            .then_inc(sx, 16)
                        sy.wait_ge(sx, ctx.inc(sx, 16))
                emit_ln(ctx, xT, z, T)
            sqT = ctx.stacks[-2].enter_context(
                nc.sbuf_tensor(f"a1_sq_{ctx.uid()}", [128, NCH, T], BF16))
            skT = ctx.stacks[-2].enter_context(
                nc.sbuf_tensor(f"a1_sk_{ctx.uid()}", [128, NCH, T], BF16))
            vau = ctx.stacks[-2].enter_context(
                nc.sbuf_tensor(f"a1_v_{ctx.uid()}", [128, NT, 16, 65], BF16))

            def proj_tr(wname, dst):
                with ctx.scoped():
                    ww = [ctx.sb("a1_w", [128, NCH, 128], BF16) for _ in range(2)]
                    pq = [ctx.ps("a1_pq", [128, 512], F32) for _ in range(2)]
                    tmp1 = ctx.sb("a1_t1", [128, 512], F32)
                    tmp2 = ctx.sb("a1_t2", [128, 512], F32)
                    swl = ctx.sem("a1_wl")
                    smm = ctx.sem("a1_mm")
                    sev = ctx.sem("a1_ev")
                    sac = ctx.sem("a1_ac")
                    b_wl, b_mm, b_ev, b_ac = (ctx.cnt(swl), ctx.cnt(smm),
                                              ctx.cnt(sev), ctx.cnt(sac))
                    with nc.Block() as blk:
                        @blk.gpsimd
                        def _(g):
                            for m in range(NCH):
                                if m >= 2:
                                    g.wait_ge(smm, b_mm + 4 * (m - 1))
                                g.dma_start(out=ww[m % 2][:, :, :],
                                            in_=dram[wname][m, :, :, :]) \
                                    .then_inc(swl, 16)
                        ctx.inc(swl, NCH * 16)

                        @blk.tensor
                        def _(t):
                            for m in range(NCH):
                                t.wait_ge(swl, b_wl + 16 * (m + 1))
                                for tb in range(4):
                                    k = 4 * m + tb
                                    if k >= 2:
                                        t.wait_ge(sev, b_ev + 2 * (k - 1))
                                    for ch in range(NCH):
                                        last = t.matmul(
                                            pq[k % 2][:, :], ww[m % 2][:, ch, :],
                                            z[:, ch, tb * 512:(tb + 1) * 512],
                                            start=(ch == 0), stop=(ch == NCH - 1))
                                    last.then_inc(smm, 1)
                        ctx.inc(smm, NCH * 4)

                        @blk.vector
                        def _(v):
                            for m in range(NCH):
                                for tb in range(4):
                                    k = 4 * m + tb
                                    v.wait_ge(smm, b_mm + k + 1)
                                    v.tensor_scalar_min(tmp1[:, :], pq[k % 2][:, :],
                                                        0.0)
                                    last = v.tensor_scalar_max(
                                        tmp2[:, :], pq[k % 2][:, :], 0.0)
                                    last.then_inc(sev, 1)
                                    v.wait_ge(sac, b_ac + k + 1)
                                    v.tensor_add(
                                        dst[:, m, tb * 512:(tb + 1) * 512],
                                        tmp1[:, :], tmp2[:, :]).then_inc(sev, 1)
                        ctx.inc(sev, NCH * 4 * 2)

                        @blk.scalar
                        def _(s):
                            for m in range(NCH):
                                for tb in range(4):
                                    k = 4 * m + tb
                                    s.wait_ge(sev, b_ev + 2 * k + 1)
                                    s.activation(tmp1[:, :], tmp1[:, :], AF.Exp) \
                                        .then_inc(sac, 1)
                        ctx.inc(sac, NCH * 4)

            proj_tr("w_qT", sqT)
            proj_tr("w_kT", skT)

            with ctx.scoped():
                wv = ctx.sb("a1_wv", [128, NCH, 1024], BF16)
                pv = [ctx.ps("a1_pv", [128, 512], F32) for _ in range(4)]
                swl = ctx.sem("a1_wl")
                smm = ctx.sem("a1_mm")
                sev = ctx.sem("a1_ev")
                b_wl, b_mm, b_ev = ctx.cnt(swl), ctx.cnt(smm), ctx.cnt(sev)
                with nc.Block() as blk:
                    @blk.gpsimd
                    def _(g):
                        g.dma_start(out=wv[:, :, :],
                                    in_=dram["w_vT"].transpose([1, 0, 2])) \
                            .then_inc(swl, 16)
                    ctx.inc(swl, 16)

                    @blk.tensor
                    def _(t):
                        t.wait_ge(swl, b_wl + 16)
                        for it in range(NT):
                            if it >= 2:
                                t.wait_ge(sev, b_ev + it - 1)
                            for nb in range(2):
                                for ch in range(NCH):
                                    last = t.matmul(
                                        pv[2 * (it % 2) + nb][:, :],
                                        z[:, ch, it * 128:(it + 1) * 128],
                                        wv[:, ch, nb * 512:(nb + 1) * 512],
                                        start=(ch == 0), stop=(ch == NCH - 1))
                                last.then_inc(smm, 1)
                    ctx.inc(smm, NT * 2)

                    @blk.vector
                    def _(v):
                        v.memset(vau[:, :, :, 64:65], 1.0)
                        for it in range(NT):
                            for nb in range(2):
                                v.wait_ge(smm, b_mm + 2 * it + nb + 1)
                                last = v.tensor_copy(
                                    vau[:, it, nb * 8:(nb + 1) * 8, 0:64],
                                    pv[2 * (it % 2) + nb][:, :]
                                    .rearrange("p (h e) -> p h e", e=64))
                            last.then_inc(sev, 1)
                    ctx.inc(sev, NT)

        sst = ctx.sem("a1_st")
        with nc.Block() as blk:
            @blk.sync
            def _(sy):
                sy.dma_start(out=sqT_d.rearrange("(hp p) t -> p hp t", p=128),
                             in_=sqT[:, :, :]).then_inc(sst, 16)
                sy.wait_ge(sst, ctx.inc(sst, 16))

        with ctx.scoped():
            mnorm = ctx.sb("a1_mn", [128, 4, 16], BF16)
            rcol = ctx.sb("a1_rc", [4, 1], F32)
            prel = ctx.ps("a1_prel", [4, 512], F32)
            swl = ctx.sem("a1_wl")
            smm = ctx.sem("a1_mm")
            sev = ctx.sem("a1_ev")
            b_wl, b_mm, b_ev = ctx.cnt(swl), ctx.cnt(smm), ctx.cnt(sev)
            b_st0 = ctx.cnt(sst)
            with nc.Block() as blk:
                @blk.gpsimd
                def _(g):
                    g.dma_start(out=mnorm[:, :, :], in_=dram["w_mnorm"]) \
                        .then_inc(swl, 16)
                ctx.inc(swl, 16)

                @blk.tensor
                def _(t):
                    t.wait_ge(swl, b_wl + 16)
                    for h in range(16):
                        if h >= 1:
                            t.wait_ge(sev, b_ev + h)
                        hb = (h % 2) * 64
                        for tb in range(4):
                            last = t.matmul(prel[:, :],
                                            mnorm[hb:hb + 64, :, h],
                                            sqT[hb:hb + 64, h // 2,
                                                tb * 512:(tb + 1) * 512],
                                            start=(tb == 0), stop=(tb == 3))
                        last.then_inc(smm, 1)
                ctx.inc(smm, 16)

                @blk.vector
                def _(v):
                    sv = SerialEng(ctx, v, ctx.sem("a1_sv"))
                    for h in range(16):
                        v.wait_ge(smm, b_mm + h + 1)
                        if h >= 1:
                            v.wait_ge(sst, b_st0 + 16 * h)
                        sv.tensor_reduce(rcol[:, :], prel[:, :], AX.X, ALU.add)
                        sv.milestone(sev, 1)
                ctx.inc(sev, 16)

                @blk.sync
                def _(sy):
                    for h in range(16):
                        sy.wait_ge(sev, b_ev + h + 1)
                        sy.dma_start(out=relp_d[h:h + 1, :],
                                     in_=rcol[:, :]).then_inc(sst, 16)
                    sy.wait_ge(sst, b_st0 + 16 * 16)
            ctx.inc(sst, 16 * 16)

        with ctx.scoped():
            trimask = ctx.sb("a1_tri", [128, 128], BF16)
            sknat = [ctx.sb("a1_skn", [128, 64], BF16) for _ in range(2)]
            At = [ctx.sb("a1_At", [128, 128], BF16) for _ in range(2)]
            state_bf = ctx.sb("a1_stb", [128, 65], BF16)
            state_f = ctx.sb("a1_stf", [64, 65], F32)
            ndsb = ctx.sb("a1_nd", [65, T], F32)
            swl = ctx.sem("a1_wl")
            with nc.Block() as blk:
                @blk.gpsimd
                def _(g):
                    g.dma_start(out=trimask[:, :], in_=dram["c_tri"][:, :]) \
                        .then_inc(swl, 16)
                    g.wait_ge(swl, ctx.inc(swl, 16))

            for h in range(16):
                hb = (h % 2) * 64
                hp = h // 2
                with ctx.scoped():
                    pT = [ctx.ps("a1_pT", [128, 64], BF16) for _ in range(2)]
                    pA = [ctx.ps("a1_pA", [128, 128], F32) for _ in range(2)]
                    pN = [ctx.ps("a1_pN", [65, 128], F32) for _ in range(2)]
                    pS = ctx.ps("a1_pS", [64, 65], F32)
                    stm = ctx.sem("a1_c_tm")
                    ste = ctx.sem("a1_c_te")
                    sam = ctx.sem("a1_c_am")
                    sae = ctx.sem("a1_c_ae")
                    snm = ctx.sem("a1_c_nm")
                    sne = ctx.sem("a1_c_ne")
                    ssu = ctx.sem("a1_c_su")
                    ssc = ctx.sem("a1_c_sc")
                    b = {s: ctx.cnt(s) for s in (stm, ste, sam, sae, snm, sne,
                                                 ssu, ssc)}
                    with nc.Block() as blk:
                        @blk.tensor
                        def _(t):
                            for c in range(NT):
                                if c >= 2:
                                    t.wait_ge(ste, b[ste] + c - 1)
                                t.transpose(pT[c % 2][:, :],
                                            skT[hb:hb + 64, hp,
                                                c * 128:(c + 1) * 128],
                                            ctx.const["ident2"][hb:hb + 64, :]) \
                                    .then_inc(stm, 1)
                                if c >= 2:
                                    t.wait_ge(sae, b[sae] + c - 1)
                                t.matmul(pA[c % 2][:, :],
                                         skT[hb:hb + 64, hp, c * 128:(c + 1) * 128],
                                         sqT[hb:hb + 64, hp, c * 128:(c + 1) * 128],
                                         start=True, stop=True).then_inc(sam, 1)
                                t.wait_ge(sae, b[sae] + c + 1)
                                if c >= 2:
                                    t.wait_ge(sne, b[sne] + c - 1)
                                if c == 0:
                                    t.matmul(pN[0][:, :], vau[:, 0, h, :],
                                             At[0][:, :], start=True,
                                             stop=True).then_inc(snm, 1)
                                else:
                                    t.wait_ge(ssc, b[ssc] + c)
                                    t.matmul(pN[c % 2][:, :], vau[:, c, h, :],
                                             At[c % 2][:, :], start=True, stop=False)
                                    t.matmul(pN[c % 2][0:65, :],
                                             state_bf[hb:hb + 64, :],
                                             sqT[hb:hb + 64, hp,
                                                 c * 128:(c + 1) * 128],
                                             start=False, stop=True).then_inc(snm, 1)
                                t.wait_ge(ste, b[ste] + c + 1)
                                t.matmul(pS[:, :], sknat[c % 2][:, :],
                                         vau[:, c, h, :],
                                         start=(c == 0), stop=(c == NT - 1)) \
                                    .then_inc(ssu, 1)
                        ctx.inc(stm, NT)
                        ctx.inc(sam, NT)
                        ctx.inc(snm, NT)
                        ctx.inc(ssu, NT)

                        @blk.vector
                        def _(v):
                            for c in range(NT):
                                v.wait_ge(stm, b[stm] + c + 1)
                                v.tensor_copy(sknat[c % 2][:, :],
                                              pT[c % 2][:, :]).then_inc(ste, 1)
                                v.wait_ge(sam, b[sam] + c + 1)
                                v.tensor_mul(At[c % 2][:, :], pA[c % 2][:, :],
                                             trimask[:, :]).then_inc(sae, 1)
                                v.wait_ge(ssu, b[ssu] + c + 1)
                                v.tensor_copy(state_bf[hb:hb + 64, :], pS[:, :]) \
                                    .then_inc(ssc, 1)
                                v.wait_ge(snm, b[snm] + c + 1)
                                v.tensor_copy(ndsb[:, c * 128:(c + 1) * 128],
                                              pN[c % 2][:, :]).then_inc(sne, 1)
                        ctx.inc(ste, NT)
                        ctx.inc(sae, NT)
                        ctx.inc(ssc, NT)
                        ctx.inc(sne, NT)

                    with nc.Block() as blk:
                        @blk.vector
                        def _(v):
                            v.tensor_copy(state_f[:, :], pS[:, :])

                        @blk.sync
                        def _(sy):
                            pass
                    with nc.Block() as blk:
                        @blk.sync
                        def _(sy):
                            sy.dma_start(out=numden_d[h, :, :], in_=ndsb[:, :]) \
                                .then_inc(sst, 16)
                            sy.dma_start(out=state_d[h, :, :], in_=state_f[:, :]) \
                                .then_inc(sst, 16)
                            sy.wait_ge(sst, ctx.inc(sst, 32))


def prep_att1a_host(ins, core):
    d = {}
    for nm, w in [("w_qT", ins["L1_attn_w_q"]), ("w_kT", ins["L1_attn_w_k"]),
                  ("w_vT", ins["L1_attn_w_v"])]:
        wt = np.ascontiguousarray(np.asarray(w, np.float32).T)
        if nm == "w_vT":
            d[nm] = np.ascontiguousarray(wt.reshape(8, 128, 1024))
        else:
            d[nm] = np.ascontiguousarray(
                wt.reshape(8, 128, 8, 128).transpose(2, 0, 1, 3))
    mn = np.asarray(ins["L1_attn_memory_norms"], np.float32).transpose(2, 0, 1)
    d["w_mnorm"] = np.ascontiguousarray(np.concatenate([mn, mn], axis=0))
    tri = np.tril(np.ones((128, 128), np.float32)).T
    d["c_tri"] = np.ascontiguousarray(tri.astype(BF))
    return d


ATT1A_DRAM_SPECS = [
    ("w_qT", [8, 8, 128, 128], F32),
    ("w_kT", [8, 8, 128, 128], F32),
    ("w_vT", [8, 128, 1024], F32),
    ("w_mnorm", [128, 4, 16], F32),
    ("c_tri", [128, 128], BF16),
]


def build_att1a_launch():
    nc = bass.Bass("TRN2", target_bir_lowering=False, detect_race_conditions=False)
    dram = {}
    dram["x_in"] = nc.dram_tensor("x_in", [D, T], F32, kind="ExternalInput").ap()
    dram["c_ident"] = nc.dram_tensor("c_ident", [128, 128], BF16, kind="ExternalInput").ap()
    dram["c_ident2"] = nc.dram_tensor("c_ident2", [128, 64], BF16, kind="ExternalInput").ap()
    for nm, shape, dt in ATT1A_DRAM_SPECS:
        dram[nm] = nc.dram_tensor(nm, shape, dt, kind="ExternalInput").ap()
    sqT = nc.dram_tensor("sqT", [D, T], BF16, kind="ExternalOutput").ap()
    numden = nc.dram_tensor("numden", [16, 65, T], F32, kind="ExternalOutput").ap()
    state = nc.dram_tensor("state", [16, 64, 65], F32, kind="ExternalOutput").ap()
    relp = nc.dram_tensor("relp", [16, 4], F32, kind="ExternalOutput").ap()
    ctx = Ctx(nc, dram)
    with ctx.root:
        emit_const_setup(ctx)
        emit_att1a(ctx, dram["x_in"], sqT, numden, state, relp)
    return nc


def emit_att1b(ctx, xin_d, sqT_d, numden_d, states_d, relp_d, xout_d):
    nc = ctx.nc
    dram = ctx.dram
    onesrow = ctx.const["ones_row"]
    EPSC = 1e-6
    with ctx.scoped():
        sqT = ctx.sb("b1_sq", [128, NCH, T], BF16)
        attn_T = ctx.sb("b1_at", [128, NCH, T], BF16)
        memaug = ctx.sb("b1_mem", [128, 4, 16, 65], BF16)
        Sgp = ctx.sb("b1_sgp", [128, 16, 65], BF16)
        gcol = ctx.sb("b1_g", [128, 16], F32)
        wcol = ctx.sb("b1_wc", [64, 16, 4], F32)
        sx = ctx.sem("b1_sx")
        sv_ = ctx.sem("b1_sv")
        sm_ = ctx.sem("b1_sm")

        with nc.Block() as blk:
            @blk.sync
            def _(sy):
                sy.dma_start(out=sqT[:, :, :],
                             in_=sqT_d.rearrange("(hp p) t -> p hp t", p=128)) \
                    .then_inc(sx, 16)
                sy.wait_ge(sx, ctx.inc(sx, 16))

            @blk.gpsimd
            def _(g):
                g.dma_start(out=memaug[:, :, :, :], in_=dram["w_mem"]).then_inc(sx, 16)
                g.wait_ge(sx, ctx.inc(sx, 16))

        with ctx.scoped():
            sall = ctx.sb("b1_sall", [64, 4, 16, 65], F32)
            pref = ctx.sb("b1_pref", [64, 4], F32)
            sgf = ctx.sb("b1_sgf", [64, 16, 65], F32)
            relT = ctx.sb("b1_rel", [16, 4, 4], F32)
            rsum = ctx.sb("b1_rs", [16, 4], F32)
            rcol = ctx.sb("b1_rc", [16, 1], F32)
            cS = ctx.sb("b1_cS", [16, 1], F32)
            wgt = ctx.sb("b1_wg", [16, 4], F32)
            wrow = ctx.sb("b1_wr", [1, 64], F32)
            grow = ctx.sb("b1_gr", [1, 16], F32)
            gsig = ctx.sb("b1_gs", [1, 16], F32)
            pb = ctx.ps("b1_pb", [64, 64], F32)
            with nc.Block() as blk:
                @blk.gpsimd
                def _(g):
                    g.dma_start(out=sall[:, :, :, :],
                                in_=states_d.transpose([2, 0, 1, 3])) \
                        .then_inc(sx, 16)
                    g.dma_start(out=pref[:, :], in_=dram["c_pref"]).then_inc(sx, 16)
                    g.dma_start(out=relT[:, :, :],
                                in_=relp_d.transpose([1, 0, 2])) \
                        .then_inc(sx, 16)
                    g.dma_start(out=grow[:, :], in_=dram["w_gate"]).then_inc(sx, 16)
                    g.wait_ge(sx, ctx.inc(sx, 64))

            b_v, b_m = ctx.cnt(sv_), ctx.cnt(sm_)
            with nc.Block() as blk:
                @blk.vector
                def _(v):
                    sv = SerialEng(ctx, v, ctx.sem("b1_ser"))
                    v.memset(cS[:, :], 1.0 / S)
                    sv.tensor_scalar_mul(sgf[:, :, :], sall[:, 0, :, :],
                                         pref[:, 0:1])
                    for r in range(1, 4):
                        sv.scalar_tensor_tensor(sgf[:, :, :], sall[:, r, :, :],
                                                pref[:, r:r + 1], sgf[:, :, :],
                                                ALU.mult, ALU.add)
                    sv.tensor_copy(Sgp[0:64, :, :], sgf[:, :, :])
                    sv.tensor_copy(Sgp[64:128, :, :], sgf[:, :, :])
                    sv.tensor_add(rsum[:, :], relT[:, 0, :], relT[:, 1, :])
                    sv.tensor_add(rsum[:, :], rsum[:, :], relT[:, 2, :])
                    sv.tensor_add(rsum[:, :], rsum[:, :], relT[:, 3, :])
                    sv.tensor_reduce(rcol[:, :], rsum[:, :], AX.X, ALU.max,
                                     negate=True)
                    sv.tensor_scalar_mul(rcol[:, :], rcol[:, :], 1.0 / S)
                    sv.milestone(sv_, 1)
                    v.wait_ge(sm_, b_m + 1)
                    sv.tensor_reduce(rcol[:, :], wgt[:, :], AX.X, ALU.add)
                    sv.reciprocal(rcol[:, :], rcol[:, :])
                    sv.tensor_scalar_mul(wgt[:, :], wgt[:, :], rcol[:, :])
                    sv.milestone(sv_, 1)
                ctx.inc(sv_, 2)

                @blk.scalar
                def _(s):
                    s.wait_ge(sv_, b_v + 1)
                    s.activation(wgt[:, :], rsum[:, :], AF.Exp,
                                 bias=rcol[:, 0:1], scale=cS[:, 0:1]) \
                        .then_inc(sm_, 1)
                    s.activation(gsig[:, :], grow[:, :], AF.Sigmoid) \
                        .then_inc(sm_, 1)
                ctx.inc(sm_, 2)

            with nc.Block() as blk:
                @blk.sync
                def _(sy):
                    sy.dma_start(out=wrow[:, :], in_=wgt[:, :]).then_inc(sx, 16)
                    sy.wait_ge(sx, ctx.inc(sx, 16))

            b_m2 = ctx.cnt(sm_)
            with nc.Block() as blk:
                @blk.tensor
                def _(t):
                    t.matmul(pb[:, :], onesrow[0:1, 0:64], wrow[:, :],
                             start=True, stop=True).then_inc(sm_, 1)
                ctx.inc(sm_, 1)

                @blk.vector
                def _(v):
                    v.wait_ge(sm_, b_m2 + 1)
                    v.tensor_copy(wcol[:, :, :],
                                  pb[:, :].rearrange("p (h a) -> p h a", a=4))

            b_m3 = ctx.cnt(sm_)
            with nc.Block() as blk:
                @blk.tensor
                def _(t):
                    t.matmul(pb[:, 0:16], onesrow[0:1, 0:64], gsig[:, :],
                             start=True, stop=True).then_inc(sm_, 1)
                ctx.inc(sm_, 1)

                @blk.vector
                def _(v):
                    v.wait_ge(sm_, b_m3 + 1)
                    v.tensor_copy(gcol[0:64, :], pb[:, 0:16])
                    v.tensor_copy(gcol[64:128, :], pb[:, 0:16])

        with ctx.scoped():
            ndsb = ctx.sb("b1_nd", [65, T], F32)
            ltmp = ctx.sb("b1_lt", [64, 512], F32)
            lcl = ctx.sb("b1_lc", [64, 512], F32)
            tmp64 = ctx.sb("b1_t64", [64, 512], F32)
            macc = ctx.sb("b1_ma", [64, 512], F32)
            rB2 = ctx.sb("b1_rB", [64, 512], F32)
            drow = ctx.sb("b1_dr", [1, 512], F32)
            pC = ctx.ps("b1_pC", [65, 512], F32)
            pM = [ctx.ps("b1_pM", [65, 512], F32) for _ in range(4)]
            pB2 = [ctx.ps("b1_pB2", [64, 512], F32) for _ in range(2)]
            snd = ctx.sem("b1_nd")
            smm = ctx.sem("b1_mm")
            sbb = ctx.sem("b1_bb")
            sev = ctx.sem("b1_ev")
            srw = ctx.sem("b1_rw")
            for h in range(16):
                hb = (h % 2) * 64
                hp = h // 2
                b_nd, b_mm, b_bb = ctx.cnt(snd), ctx.cnt(smm), ctx.cnt(sbb)
                b_ev, b_rw = ctx.cnt(sev), ctx.cnt(srw)
                with nc.Block() as blk:
                    @blk.sync
                    def _(sy):
                        sy.dma_start(out=ndsb[:, :], in_=numden_d[h, :, :]) \
                            .then_inc(snd, 16)
                    ctx.inc(snd, 16)

                    @blk.tensor
                    def _(t):
                        for tb in range(4):
                            sl = slice(tb * 512, (tb + 1) * 512)
                            if tb >= 1:
                                t.wait_ge(sev, b_ev + tb)
                            t.matmul(pC[:, :], Sgp[hb:hb + 64, h, :],
                                     sqT[hb:hb + 64, hp, sl],
                                     start=True, stop=True).then_inc(smm, 1)
                            for mi in range(4):
                                t.matmul(pM[mi][:, :],
                                         memaug[hb:hb + 64, mi, h, :],
                                         sqT[hb:hb + 64, hp, sl],
                                         start=True, stop=True).then_inc(smm, 1)
                            for j in range(5):
                                t.wait_ge(srw, b_rw + 5 * tb + j + 1)
                                t.matmul(pB2[j % 2][:, :],
                                         onesrow[0:1, 0:64], drow[:, :],
                                         start=True, stop=True).then_inc(sbb, 1)
                    ctx.inc(smm, 20)
                    ctx.inc(sbb, 20)

                    @blk.vector
                    def _(v):
                        sv = SerialEng(ctx, v, ctx.sem("b1_ser"))
                        v.wait_ge(snd, b_nd + 16)
                        for tb in range(4):
                            sl = slice(tb * 512, (tb + 1) * 512)
                            v.wait_ge(smm, b_mm + 5 * tb + 1)
                            sv.tensor_add(ltmp[:, :], ndsb[0:64, sl],
                                          pC[0:64, :])
                            sv.tensor_add(drow[:, :], ndsb[64:65, sl],
                                          pC[64:65, :])
                            sv.tensor_scalar_max(drow[:, :], drow[:, :], EPSC)
                            sv.reciprocal(drow[:, :], drow[:, :])
                            sv.milestone(srw, 1)
                            v.wait_ge(sbb, b_bb + 5 * tb + 1)
                            sv.tensor_mul(lcl[:, :], ltmp[:, :], pB2[0][:, :])
                            for mi in range(4):
                                v.wait_ge(smm, b_mm + 5 * tb + mi + 2)
                                sv.tensor_scalar_max(drow[:, :],
                                                     pM[mi][64:65, :], EPSC)
                                sv.reciprocal(drow[:, :], drow[:, :])
                                sv.tensor_scalar_mul(drow[:, :], drow[:, :],
                                                     wcol[0:1, h, mi:mi + 1])
                                sv.milestone(srw, 1)
                                v.wait_ge(sbb, b_bb + 5 * tb + mi + 2)
                                sv.tensor_copy(rB2[:, :], pB2[(mi + 1) % 2][:, :])
                                sv.tensor_mul(tmp64[:, :], pM[mi][0:64, :],
                                              rB2[:, :])
                                if mi == 0:
                                    sv.tensor_copy(macc[:, :], tmp64[:, :])
                                else:
                                    sv.tensor_add(macc[:, :], macc[:, :],
                                                  tmp64[:, :])
                            sv.tensor_sub(tmp64[:, :], macc[:, :], lcl[:, :])
                            sv.scalar_tensor_tensor(
                                attn_T[hb:hb + 64, hp, sl], tmp64[:, :],
                                gcol[0:64, h:h + 1], lcl[:, :],
                                ALU.mult, ALU.add)
                            sv.milestone(sev, 1)
                    ctx.inc(srw, 20)
                    ctx.inc(sev, 4)

        with ctx.scoped():
            wp = ctx.sb("b1_wp", [128, NCH, 1024], BF16)
            xb1 = [ctx.sb("b1_x1", [128, NCH, 512], F32) for _ in range(2)]
            po = [ctx.ps("b1_po", [128, 512], F32) for _ in range(2)]
            swp = ctx.sem("b1_wp2")
            sxl = ctx.sem("b1_xl")
            som = ctx.sem("b1_om")
            soe = ctx.sem("b1_oe")
            sos = ctx.sem("b1_os")
            b_wp, b_xl, b_om = ctx.cnt(swp), ctx.cnt(sxl), ctx.cnt(som)
            b_oe, b_os = ctx.cnt(soe), ctx.cnt(sos)
            with nc.Block() as blk:
                @blk.gpsimd
                def _(g):
                    g.dma_start(out=wp[:, :, :],
                                in_=dram["w_oT"].transpose([1, 0, 2])) \
                        .then_inc(swp, 16)
                    for tb in range(4):
                        if tb >= 2:
                            g.wait_ge(sos, b_os + 16 * (tb - 1))
                        g.dma_start(out=xb1[tb % 2][:, :, :],
                                    in_=xin_d.rearrange("(c p) t -> p c t", p=128)
                                    [:, :, tb * 512:(tb + 1) * 512]).then_inc(sxl, 16)
                ctx.inc(swp, 16)
                ctx.inc(sxl, 64)

                @blk.tensor
                def _(t):
                    t.wait_ge(swp, b_wp + 16)
                    for tb in range(4):
                        for m in range(NCH):
                            idx = tb * NCH + m
                            if idx >= 2:
                                t.wait_ge(soe, b_oe + idx - 1)
                            for ch in range(NCH):
                                last = t.matmul(po[idx % 2][:, :],
                                                wp[:, ch, m * 128:(m + 1) * 128],
                                                attn_T[:, ch, tb * 512:(tb + 1) * 512],
                                                start=(ch == 0), stop=(ch == NCH - 1))
                            last.then_inc(som, 1)
                ctx.inc(som, 32)

                @blk.vector
                def _(v):
                    for tb in range(4):
                        v.wait_ge(sxl, b_xl + 16 * (tb + 1))
                        for m in range(NCH):
                            idx = tb * NCH + m
                            v.wait_ge(som, b_om + idx + 1)
                            v.tensor_add(xb1[tb % 2][:, m, :], xb1[tb % 2][:, m, :],
                                         po[idx % 2][:, :]).then_inc(soe, 1)
                ctx.inc(soe, 32)

                @blk.sync
                def _(sy):
                    for tb in range(4):
                        sy.wait_ge(soe, b_oe + NCH * (tb + 1))
                        sy.dma_start(
                            out=xout_d.rearrange("(c p) t -> p c t", p=128)
                            [:, :, tb * 512:(tb + 1) * 512],
                            in_=xb1[tb % 2][:, :, :]).then_inc(sos, 16)
                    sy.wait_ge(sos, b_os + 64)
                ctx.inc(sos, 64)


def prep_att1b_host(ins, core):
    d = {}
    mem = np.asarray(ins["L1_attn_memories"], np.float32)
    mn = np.asarray(ins["L1_attn_memory_norms"], np.float32)
    aug = np.concatenate([mem, mn[..., None]], axis=-1)
    aug = np.transpose(aug, (2, 0, 1, 3))
    d["w_mem"] = np.ascontiguousarray(np.concatenate([aug, aug], axis=0))
    wo = np.ascontiguousarray(np.asarray(ins["L1_attn_w_o"], np.float32).T)
    d["w_oT"] = np.ascontiguousarray(wo.reshape(8, 128, 1024))
    d["w_gate"] = np.ascontiguousarray(
        np.asarray(ins["L1_attn_gate"], np.float32).reshape(1, 16))
    pref = np.zeros((64, 4), np.float32)
    pref[:, :core % 4] = 1.0
    d["c_pref"] = pref
    return d


ATT1B_DRAM_SPECS = [
    ("w_mem", [128, 4, 16, 65], F32),
    ("w_oT", [8, 128, 1024], F32),
    ("w_gate", [1, 16], F32),
    ("c_pref", [64, 4], F32),
]


def build_att1b_launch():
    nc = bass.Bass("TRN2", target_bir_lowering=False, detect_race_conditions=False)
    dram = {}
    dram["x_in"] = nc.dram_tensor("x_in", [D, T], F32, kind="ExternalInput").ap()
    dram["c_ident"] = nc.dram_tensor("c_ident", [128, 128], BF16, kind="ExternalInput").ap()
    dram["c_ident2"] = nc.dram_tensor("c_ident2", [128, 64], BF16, kind="ExternalInput").ap()
    dram["sqT"] = nc.dram_tensor("sqT", [D, T], BF16, kind="ExternalInput").ap()
    dram["numden"] = nc.dram_tensor("numden", [16, 65, T], F32, kind="ExternalInput").ap()
    dram["states"] = nc.dram_tensor("states", [4, 16, 64, 65], F32, kind="ExternalInput").ap()
    dram["relp"] = nc.dram_tensor("relp", [4, 16, 4], F32, kind="ExternalInput").ap()
    for nm, shape, dt in ATT1B_DRAM_SPECS:
        dram[nm] = nc.dram_tensor(nm, shape, dt, kind="ExternalInput").ap()
    xout = nc.dram_tensor("x_out", [D, T], F32, kind="ExternalOutput").ap()
    ctx = Ctx(nc, dram)
    with ctx.root:
        emit_const_setup(ctx)
        emit_att1b(ctx, dram["x_in"], dram["sqT"], dram["numden"], dram["states"],
                   dram["relp"], xout)
    return nc


_NC_CACHE = {}


def _get_nc(name, builder):
    if name not in _NC_CACHE:
        _NC_CACHE[name] = builder()
    return _NC_CACHE[name]


def kernel_unfused(**inputs):
    ins = {k: np.asarray(v) for k, v in inputs.items()}
    x = np.asarray(ins["L0_attn_x"], np.float32)
    consts = const_host()

    def xs_of(c, xarr):
        b, sl = c // 4, c % 4
        return np.ascontiguousarray(xarr[b, sl * 2048:(sl + 1) * 2048, :].T)

    xcur = [xs_of(c, x) for c in range(8)]

    def run(nc, maps):
        return run_bass_kernel_spmd(nc, maps, core_ids=list(range(8)))

    w_f0 = prep_ffn0_host(ins["L0_ffn_controller"], ins["L0_ffn_f1"],
                          ins["L0_ffn_bias"], ins["L0_ffn_f2"])
    nc = _get_nc("ffn0", build_ffn0_launch)
    res = run(nc, [{**{"x_in": xcur[c]}, **w_f0, **consts} for c in range(8)])
    xcur = [res.results[c]["x_out"] for c in range(8)]

    nc = _get_nc("att0a", build_att0a_launch)
    a0w = [prep_att0a_host(ins["L0_attn_w_qkv"], (c % 4) * 2048) for c in range(8)]
    res_a = run(nc, [{**{"x_in": xcur[c]}, **a0w[c], **consts} for c in range(8)])

    kT_full = {}
    vhm_full = {}
    for b in range(2):
        kT_full[b] = np.ascontiguousarray(np.stack(
            [res_a.results[4 * b + sl]["kT"] for sl in range(4)], axis=0))
        vhm_full[b] = np.ascontiguousarray(np.stack(
            [res_a.results[4 * b + sl]["vhm"] for sl in range(4)], axis=0))
    w_a0b = prep_att0b_host(ins["L0_attn_w_proj"], ins["L0_attn_b_proj"])
    nc = _get_nc("att0b", build_att0b_launch)
    res = run(nc, [{**{"x_in": xcur[c], "qT": res_a.results[c]["qT"],
                       "kT": kT_full[c // 4], "vhm": vhm_full[c // 4]},
                    **w_a0b, **consts} for c in range(8)])
    xcur = [res.results[c]["x_out"] for c in range(8)]

    nc = _get_nc("ffn0", build_ffn0_launch)
    res = run(nc, [{**{"x_in": xcur[c]}, **w_f0, **consts} for c in range(8)])
    xcur = [res.results[c]["x_out"] for c in range(8)]

    w_f1 = prep_ffn1_host(ins)
    nc = _get_nc("ffn1", build_ffn1_launch)
    res = run(nc, [{**{"x_in": xcur[c]}, **w_f1, **consts} for c in range(8)])
    xcur = [res.results[c]["x_out"] for c in range(8)]

    nc = _get_nc("att1a", build_att1a_launch)
    w_a1a = prep_att1a_host(ins, 0)
    res_a = run(nc, [{**{"x_in": xcur[c]}, **w_a1a, **consts} for c in range(8)])

    nc = _get_nc("att1b", build_att1b_launch)
    maps = []
    for c in range(8):
        b = c // 4
        grp = [4 * b + r for r in range(4)]
        states = np.ascontiguousarray(
            np.stack([res_a.results[g_]["state"] for g_ in grp], 0))
        relp = np.ascontiguousarray(
            np.stack([res_a.results[g_]["relp"] for g_ in grp], 0))
        m = {"x_in": xcur[c], "sqT": res_a.results[c]["sqT"],
             "numden": res_a.results[c]["numden"],
             "states": states, "relp": relp}
        m.update(prep_att1b_host(ins, c))
        m.update(consts)
        maps.append(m)
    res = run(nc, maps)
    xcur = [res.results[c]["x_out"] for c in range(8)]

    nc = _get_nc("ffn1", build_ffn1_launch)
    res = run(nc, [{**{"x_in": xcur[c]}, **w_f1, **consts} for c in range(8)])
    xcur = [res.results[c]["x_out"] for c in range(8)]

    out = np.empty((2, 8192, 1024), np.float32)
    for c in range(8):
        b, sl = c // 4, c % 4
        out[b, sl * 2048:(sl + 1) * 2048, :] = xcur[c].T
    return out


def build_fused():
    nc = bass.Bass("TRN2", target_bir_lowering=False, detect_race_conditions=False)
    dram = {}
    dram["x_in"] = nc.dram_tensor("x_in", [D, T], F32, kind="ExternalInput").ap()
    dram["c_ident"] = nc.dram_tensor("c_ident", [128, 128], BF16, kind="ExternalInput").ap()
    dram["c_ident2"] = nc.dram_tensor("c_ident2", [128, 64], BF16, kind="ExternalInput").ap()
    for nm, shape, dt in (FFN0_DRAM_SPECS + ATT0A_DRAM_SPECS + ATT0B_DRAM_SPECS +
                          FFN1_DRAM_SPECS + ATT1A_DRAM_SPECS + ATT1B_DRAM_SPECS):
        dram[nm] = nc.dram_tensor(nm, shape, dt, kind="ExternalInput").ap()
    x_out = nc.dram_tensor("x_out", [D, T], F32, kind="ExternalOutput").ap()

    xi = [nc.dram_tensor(f"x{i}", [D, T], F32).ap() for i in range(1, 6)]
    park = nc.dram_tensor("park", [D, T], F32).ap()
    intd = nc.dram_tensor("intd", [128, 16, 1024], BF16).ap()
    zpark = nc.dram_tensor("zpark", [128, NCH, T], BF16).ap()
    qT = nc.dram_tensor("qT_i", [D, T], BF16).ap()
    kT = nc.dram_tensor("kT_i", [D, T], BF16).ap()
    vhm = nc.dram_tensor("vhm_i", [16, 128, NT, 65], BF16).ap()
    kTg = nc.dram_tensor("kTg_i", [4 * D, T], BF16).ap()
    vhmg = nc.dram_tensor("vhmg_i", [4 * 16, 128, NT, 65], BF16).ap()
    sqT = nc.dram_tensor("sqT_i", [D, T], BF16).ap()
    numden = nc.dram_tensor("numden_i", [16, 65, T], F32).ap()
    state = nc.dram_tensor("state_i", [16, 64, 65], F32).ap()
    relp = nc.dram_tensor("relp_i", [16, 4], F32).ap()
    stateg = nc.dram_tensor("stateg_i", [4 * 16, 64, 65], F32).ap()
    relpg = nc.dram_tensor("relpg_i", [4 * 16, 4], F32).ap()

    RG = [[0, 1, 2, 3], [4, 5, 6, 7]]
    ctx = Ctx(nc, dram)
    with ctx.root:
        emit_const_setup(ctx)
        emit_ffn0(ctx, dram["x_in"], xi[0], park, intd)
        emit_att0a(ctx, xi[0], qT, kT, vhm)
        scc = ctx.sem("cc")
        with nc.Block() as blk:
            @blk.gpsimd
            def _(g):
                g.collective_compute("AllGather", ALU.bypass, replica_groups=RG,
                                     ins=[kT.opt()], outs=[kTg.opt()]).then_inc(scc)
                g.collective_compute("AllGather", ALU.bypass, replica_groups=RG,
                                     ins=[vhm.opt()], outs=[vhmg.opt()]).then_inc(scc)
                g.wait_ge(scc, ctx.inc(scc, 2))
        emit_att0b(ctx, xi[0],
                   qT, kTg.rearrange("(r d) t -> r d t", r=4),
                   vhmg.rearrange("(r h) p n e -> r h p n e", r=4), xi[1])
        emit_ffn0(ctx, xi[1], xi[2], park, intd)
        emit_ffn1(ctx, xi[2], xi[3], zpark)
        emit_att1a(ctx, xi[3], sqT, numden, state, relp)
        with nc.Block() as blk:
            @blk.gpsimd
            def _(g):
                g.collective_compute("AllGather", ALU.bypass, replica_groups=RG,
                                     ins=[state.opt()], outs=[stateg.opt()]) \
                    .then_inc(scc)
                g.collective_compute("AllGather", ALU.bypass, replica_groups=RG,
                                     ins=[relp.opt()], outs=[relpg.opt()]).then_inc(scc)
                g.wait_ge(scc, ctx.inc(scc, 2))
        emit_att1b(ctx, xi[3], sqT, numden,
                   stateg.rearrange("(r h) d e -> r h d e", r=4),
                   relpg.rearrange("(r h) a -> r h a", r=4), xi[4])
        emit_ffn1(ctx, xi[4], x_out, zpark)
    return nc


def kernel(**inputs):
    ins = {k: np.asarray(v) for k, v in inputs.items()}
    x = np.asarray(ins["L0_attn_x"], np.float32)
    consts = const_host()
    w_f0 = prep_ffn0_host(ins["L0_ffn_controller"], ins["L0_ffn_f1"],
                          ins["L0_ffn_bias"], ins["L0_ffn_f2"])
    w_a0b = prep_att0b_host(ins["L0_attn_w_proj"], ins["L0_attn_b_proj"])
    w_f1 = prep_ffn1_host(ins)
    w_a1a = prep_att1a_host(ins, 0)
    nc = _get_nc("fused", build_fused)
    in_maps = []
    for c in range(8):
        b, sl = c // 4, c % 4
        m = {"x_in": np.ascontiguousarray(x[b, sl * 2048:(sl + 1) * 2048, :].T)}
        m.update(consts)
        m.update(w_f0)
        m.update(prep_att0a_host(ins["L0_attn_w_qkv"], sl * 2048))
        m.update(w_a0b)
        m.update(w_f1)
        m.update(w_a1a)
        m.update(prep_att1b_host(ins, c))
        in_maps.append(m)
    res = run_bass_kernel_spmd(nc, in_maps, core_ids=list(range(8)))
    out = np.empty((2, 8192, 1024), np.float32)
    for c in range(8):
        b, sl = c // 4, c % 4
        out[b, sl * 2048:(sl + 1) * 2048, :] = res.results[c]["x_out"].T
    return out
